# version 06
# speedup vs baseline: 1.1321x; 1.0079x over previous
import numpy as np
import concourse.bass as bass
import concourse.mybir as mybir
from concourse.bass_utils import run_bass_kernel_spmd

F32 = mybir.dt.float32
BF16 = mybir.dt.bfloat16
I32 = mybir.dt.int32
AF = mybir.ActivationFunctionType
ALU = mybir.AluOpType
AX = mybir.AxisListType.X

B, L, D = 4, 4096, 1024
HALF = 2048
DIN, NH, HD, DCONV = 2048, 32, 64, 4
ZXB_PAD = 4480
LT, NLT = 512, 4
FI = 3072
INV_2PI = float(1.0 / (2.0 * np.pi))
TWO_PI = float(2.0 * np.pi)
MAGIC = 12582912.0
PAIRS = [[0, 1], [2, 3], [4, 5], [6, 7]]
ALLC = [list(range(8))]


class Plan:
    SEMKEYS = ("t", "v", "s", "g", "dy", "ds", "dg", "cc")

    def __init__(self):
        self.q = {k: [] for k in "tvsgy"}
        self.cnt = {k: 0 for k in self.SEMKEYS}
        self.step = dict(t=1, v=1, s=1, g=1, dy=16, ds=16, dg=16, cc=1)
        self.last_wait = {e: {k: 0 for k in self.SEMKEYS} for e in "tvsgy"}
        self.sems = {}
        self.slotmap = {}
        self.dsem_pool = []

    def emit(self, ek, fn):
        self.q[ek].append(fn)

    def mark(self, ek, key, fn):
        self.cnt[key] += self.step[key]
        sems, step = self.sems, self.step[key]
        self.q[ek].append(lambda e: fn(e).then_inc(sems[key], step))
        return (key, self.cnt[key])

    def wait(self, ek, tok):
        if tok is None:
            return
        key, v = tok
        if self.last_wait[ek][key] >= v:
            return
        self.last_wait[ek][key] = v
        sems = self.sems
        self.q[ek].append(lambda e: e.wait_ge(sems[key], v))

    def waits(self, ek, toks):
        for t in toks:
            if t is not None:
                self.wait(ek, t)

    def barrier(self):
        own = {}
        for ek in "tvsg":
            self.wait(ek, (ek, self.cnt[ek]))
            own[ek] = self.mark(ek, ek, lambda e: e.nop())
        toks = [(kk, v) for kk, v in self.cnt.items() if v > 0]
        for ek in "tvsgy":
            for tok in toks:
                if tok[0] != ek:
                    self.wait(ek, tok)


DMAQ = {"y": "dy", "s": "ds", "g": "dg"}


class K:
    def __init__(self, nc, p, dram, psum):
        self.nc, self.p, self.dr, self.psum = nc, p, dram, psum
        self.sb = {}
        self.bank_readers = [dict() for _ in range(8)]

    def bank_deps(self, b):
        return tuple(self.bank_readers[b].values())

    def bank_read(self, b, ek, tok):
        self.bank_readers[b][tok[0]] = tok

    def bank_clear(self, b):
        self.bank_readers[b] = {}

    def dma(self, ek, dst_ap, src_ap, deps=(), slot=None):
        p = self.p
        assert slot is not None
        if slot not in p.slotmap:
            idx = len(p.slotmap)
            sem = p.dsem_pool[idx]
            p.slotmap[slot] = sem
            p.sems[slot] = sem
            p.cnt[slot] = 0
            p.step[slot] = 16
            for e2 in "tvsgy":
                p.last_wait[e2][slot] = 0
        p.waits(ek, deps)
        if p.cnt[slot] > 0:
            p.wait(ek, (slot, p.cnt[slot]))
        return p.mark(ek, slot, lambda e: e.dma_start(out=dst_ap, in_=src_ap))

    def op(self, ek, fn, deps=()):
        self.p.waits(ek, deps)
        return self.p.mark(ek, ek, fn)

    def opq(self, ek, fn, deps=()):
        self.p.waits(ek, deps)
        self.p.emit(ek, fn)

    def coll(self, kind, op, groups, in_ap, out_ap, deps=()):
        self.p.waits("g", deps)
        return self.p.mark("g", "cc", lambda e: e.collective_compute(
            kind, op, replica_groups=groups, ins=[in_ap.opt()], outs=[out_ap.opt()]))

    def mm(self, b, out_ap, lhsT, rhs, start, stop, deps=(), skip=False):
        d = tuple(deps) + (self.bank_deps(b) if start else ())
        if start:
            self.bank_clear(b)
        return self.op("t", lambda e: e.matmul(out_ap, lhsT, rhs, start=start, stop=stop,
                                               skip_group_check=skip), deps=d)

    def mmq(self, b, out_ap, lhsT, rhs, start, stop, deps=(), skip=False):
        d = tuple(deps) + (self.bank_deps(b) if start else ())
        if start:
            self.bank_clear(b)
        self.opq("t", lambda e: e.matmul(out_ap, lhsT, rhs, start=start, stop=stop,
                                         skip_group_check=skip), deps=d)

    def rd(self, b, ek, fn, deps=()):
        t = self.op(ek, fn, deps=deps)
        self.bank_read(b, ek, t)
        return t

    def tr_tiles(self, tiles, bank_pair, deps=(), ek="v"):
        ident = self.sb["ident_f"]
        last = None
        for i, (src, dst, dep) in enumerate(tiles):
            b = bank_pair[i % 2]
            bk = self.psum[b]
            r, c = src.shape[-2], src.shape[-1]
            self.mm(b, None, None, None, True, True) if False else None
            d = tuple(deps) + (dep,) + self.bank_deps(b)
            self.bank_clear(b)
            tt = self.op("t", lambda e, s=src, bb=bk, c_=c, r_=r: e.transpose(
                bb[0:c_, 0:r_], s, ident[0:r_, 0:r_]), deps=d)
            last = self.rd(b, ek, lambda e, bb=bk, d2=dst, c_=c, r_=r: e.tensor_copy(
                d2, bb[0:c_, 0:r_]), deps=(tt,))
        return last

    def newton_rsqrt(self, row, m_row, tmp_row, dep, eps):
        t = self.op("s", lambda e: e.activation(out=row, in_=m_row, func=AF.Sqrt, bias=eps),
                    deps=(dep,))
        t = self.op("v", lambda e: e.reciprocal(row, row), deps=(t,))
        for _ in range(2):
            t1 = self.op("v", lambda e: e.tensor_tensor(tmp_row, row, row, ALU.mult),
                         deps=(t,))
            t1 = self.op("v", lambda e: e.tensor_tensor(tmp_row, tmp_row, m_row, ALU.mult),
                         deps=(t1,))
            t1 = self.op("v", lambda e: e.tensor_scalar(tmp_row, tmp_row, -0.5,
                                                        1.5 - 0.5 * eps, ALU.mult, ALU.add),
                         deps=(t1,))
            t = self.op("v", lambda e: e.tensor_tensor(row, row, tmp_row, ALU.mult),
                        deps=(t1,))
        return t


def build_program():
    nc = bass.Bass("TRN2", target_bir_lowering=False)
    p = Plan()
    dram = {}

    def din(name, shape, dtype=F32):
        dram[name] = nc.dram_tensor(name, list(shape), dtype, kind="ExternalInput").ap()

    def dout(name, shape, dtype=F32):
        dram[name] = nc.dram_tensor(name, list(shape), dtype, kind="ExternalOutput").ap()

    def dscr(name, shape, dtype=F32):
        dram[name] = nc.dram_tensor(name, list(shape), dtype).ap()

    din("u_own", (HALF, D))
    din("w_in_t", (35, 128, 1024))
    din("w_out_t", (16, 128, D))
    din("conv_w", (128, 72))
    din("conv_b", (128, 18))
    din("dt_bias", (NH, 1))
    din("a_log", (NH, 1))
    din("d_exp", (128, 16))
    din("normw_col", (128, 16))
    din("w1", (32, 128, 1024))
    din("w2", (2, 8, 128, 2048))
    din("b1_col", (128, 32))
    din("qp_col", (128, 32))
    din("qr_col", (128, 32))
    din("qi_col", (128, 32))
    din("b2_col", (128, 8))
    din("wq_t", (4, 128, 1024))
    din("wk_t", (4, 128, 1024))
    din("wv_t", (4, 128, 1024))
    din("wo_t", (8, 128, D))
    din("wg_nat", (24, 128, D))
    din("wu_nat", (24, 128, D))
    din("wd_nat", (8, 128, FI))
    din("ident_f", (128, 128))
    din("ident_b", (128, 128))
    din("tri_f", (128, 128))
    din("mask_b", (128, 128))
    din("ones_row_f", (1, 128))
    din("is_odd", (128, 1))
    din("ind_hp", (32, 16 * 128))

    dout("out_own", (HALF, D))

    dscr("zxb", (ZXB_PAD, HALF))
    dscr("ystage", (NLT, 128, 8192))
    dscr("x2own", (NLT, 128, 4096))
    dscr("attn_heads", (4, 8, 128, 512))
    dscr("upst", (24, 128, HALF))
    dscr("gatest", (24, 128, HALF))
    dscr("ost", (24, 128, HALF))
    dscr("wdt_scr", (8, 128, FI))
    dscr("dbg", (128, 16384))
    dscr("ag_s_in", (128, DIN))
    dscr("ag_s_out", (2, 128, DIN))
    dscr("ag_h_in", (128, 54))
    dscr("ag_h_out", (2, 128, 54))
    dscr("ag_x_out", (2, NLT, 128, 4096))
    dscr("ag_a_out", (2, 4, 8, 128, 512))
    dscr("ar_in", (1, 4))
    dscr("ar_out", (1, 4))

    NA, NB, NC, ND, NE = 16384, 12544, 8192, 8192, 4864
    import contextlib
    with contextlib.ExitStack() as stack:
        ent = stack.enter_context
        t_a = ent(nc.sbuf_tensor("AR_A", [128, NA], F32))
        t_b = ent(nc.sbuf_tensor("AR_B", [128, NB], F32))
        t_c = ent(nc.sbuf_tensor("AR_C", [128, NC], F32))
        t_d = ent(nc.sbuf_tensor("AR_D", [128, ND], F32))
        t_e = ent(nc.sbuf_tensor("AR_E", [128, NE], F32))
        psl = [ent(nc.psum_tensor(f"psb{i}", [128, 512], F32)) for i in range(8)]
        ps0, ps1, ps2, ps3, ps4, ps5, ps6, ps7 = psl
        sm_t = ent(nc.semaphore("sm_t"))
        sm_v = ent(nc.semaphore("sm_v"))
        sm_s = ent(nc.semaphore("sm_s"))
        sm_g = ent(nc.semaphore("sm_g"))
        sm_dy = ent(nc.semaphore("sm_dy"))
        sm_ds = ent(nc.semaphore("sm_ds"))
        sm_dg = ent(nc.semaphore("sm_dg"))
        sm_cc = ent(nc.semaphore("sm_cc"))
        dsem_pool = [ent(nc.semaphore(f"dm{i}")) for i in range(72)]
        block = ent(nc.Block())
        p.sems = dict(t=sm_t, v=sm_v, s=sm_s, g=sm_g, dy=sm_dy, ds=sm_ds, dg=sm_dg, cc=sm_cc)
        p.dsem_pool = dsem_pool
        k = K(nc, p, dram, [ps0, ps1, ps2, ps3, ps4, ps5, ps6, ps7])
        k.A, k.Bm, k.C, k.Dm, k.E = (t.ap() for t in (t_a, t_b, t_c, t_d, t_e))

        cvals = (0.0, 1.0, 1e-5)
        for i, cv in enumerate(cvals):
            capn = k.E[:, NE - 3 + i:NE - 2 + i]
            nc.const_aps.register(F32, cv, capn)
            k.op("v", lambda e, a=capn, v=cv: e.memset(a, v))
        p.barrier()
        setup_consts(k)
        mamba_phase(k)
        p.barrier()
        for b in range(8):
            k.bank_clear(b)
        l0ffn_phase(k)
        p.barrier()
        for b in range(8):
            k.bank_clear(b)
        attn_phase(k)
        p.barrier()
        for b in range(8):
            k.bank_clear(b)
        l1ffn_phase(k)
        p.barrier()

        @block.tensor
        def _(e):
            for fn in p.q["t"]:
                fn(e)

        @block.vector
        def _(e):
            for fn in p.q["v"]:
                fn(e)

        @block.scalar
        def _(e):
            for fn in p.q["s"]:
                fn(e)

        @block.gpsimd
        def _(e):
            for fn in p.q["g"]:
                fn(e)

        @block.sync
        def _(e):
            for fn in p.q["y"]:
                fn(e)

    return nc


def setup_consts(k):
    E, dr = k.E, k.dr
    off = 0
    toks = {}

    def ld(name, n, src=None, parts=None, dtype=F32):
        nonlocal off
        ap = E[:, off:off + n]
        if dtype == BF16:
            ap = ap.bitcast(BF16)[:, 0:2 * n]
        if parts is not None:
            ap = ap[0:parts]
        k.sb[name] = ap
        toks[name] = k.dma("y", ap, src if src is not None else dr[name], slot="cst")
        off += n

    ld("ident_f", 128)
    ld("tri_f", 128)
    for nm in ("ident_b", "mask_b"):
        stg32 = E[:, off:off + 128]
        tld = k.dma("y", stg32, dr[nm], slot="cst")
        off += 128
        bap = E[:, off:off + 64].bitcast(BF16)[:, 0:128]
        k.sb[nm] = bap
        toks[nm] = k.op("v", lambda e, a=bap, s=stg32: e.tensor_copy(a, s), deps=(tld,))
        off += 64
    ld("ones_row_f", 128, parts=1)
    ld("is_odd", 1)
    k.indoff = off
    ld("ind_hp", 2048, parts=32)
    ld("d_exp", 16)
    ld("normw_col", 16)
    ld("dt_bias", 1, parts=32)
    ld("a_log", 1, parts=32)
    k.sb["conv_w"] = E[:, off:off + 72].rearrange("p (c t) -> p c t", c=18)
    toks["conv_w"] = k.dma("y", E[:, off:off + 72], dr["conv_w"], slot="cst")
    off += 72
    k.sb["conv_b"] = E[:, off:off + 18]
    toks["conv_b"] = k.dma("y", E[:, off:off + 18], dr["conv_b"], slot="cst")
    off += 18
    ld("b1_col", 32)
    ld("qp_col", 32)
    ld("qr_col", 32)
    ld("qi_col", 32)
    ld("b2_col", 8)
    k.sb["ones_col"] = E[:, off:off + 1]
    t_ones = k.op("v", lambda e: e.memset(k.sb["ones_col"], 1.0))
    off += 1
    k.sb["A_neg"] = E[0:32, off:off + 1]
    off += 1
    t1 = k.op("s", lambda e: e.activation(out=k.sb["A_neg"], in_=k.sb["a_log"], func=AF.Exp),
              deps=(toks["a_log"],))
    t1 = k.op("v", lambda e: e.tensor_scalar(k.sb["A_neg"], k.sb["A_neg"], -1.0, None,
                                             ALU.mult), deps=(t1,))
    thp = E[:, off:off + 32]; off += 32
    alpha = E[:, off:off + 32]; off += 32
    w1_ = E[:, off:off + 32]; off += 32
    w2_ = E[:, off:off + 32]; off += 32
    w3f = E[:, off:off + 32]; off += 32
    w3_ = w3f.bitcast(I32)[:, 0:32]
    k.sb["thp_col"], k.sb["alpha_col"] = thp, alpha

    def rr_sin(dst, src_ap, phase, dep):
        t = k.op("v", lambda e: e.tensor_scalar(w1_, src_ap, INV_2PI, phase * INV_2PI,
                                                ALU.mult, ALU.add), deps=(dep,))
        t = k.op("v", lambda e: e.tensor_scalar(w3f, w1_, MAGIC, MAGIC, ALU.add,
                                                ALU.subtract), deps=(t,))
        t = k.op("v", lambda e: e.tensor_copy(w3_, w3f), deps=(t,))
        t = k.op("v", lambda e: e.scalar_tensor_tensor(w2_, w3_, -1.0, w1_, ALU.mult,
                                                       ALU.add), deps=(t,))
        t = k.op("v", lambda e: e.tensor_scalar(w2_, w2_, 0.4999999, -0.4999999,
                                                ALU.min, ALU.max), deps=(t,))
        return k.op("s", lambda e, d=dst: e.activation(out=d, in_=w2_, func=AF.Sin,
                                                       scale=TWO_PI), deps=(t,))

    t = rr_sin(thp, k.sb["qp_col"], 0.0, toks["qp_col"])
    t = k.op("v", lambda e: e.tensor_tensor(alpha, thp, k.sb["qi_col"], ALU.mult),
             deps=(t, toks["qi_col"]))
    t2 = rr_sin(thp, k.sb["qp_col"], float(np.pi / 2), t)
    t2 = k.op("v", lambda e: e.tensor_tensor(w2_, thp, k.sb["qr_col"], ALU.mult),
              deps=(t2, toks["qr_col"]))
    t2 = k.op("v", lambda e: e.tensor_tensor(alpha, alpha, w2_, ALU.add), deps=(t2,))
    t2 = k.op("v", lambda e: e.scalar_tensor_tensor(alpha, alpha, 0.1, k.sb["qi_col"],
                                                    ALU.mult, ALU.mult), deps=(t2,))
    t3 = k.op("v", lambda e: e.scalar_tensor_tensor(thp, k.sb["b1_col"], 0.1, k.sb["qp_col"],
                                                    ALU.mult, ALU.add),
              deps=(t2, toks["b1_col"]))
    t3 = k.op("v", lambda e: e.tensor_scalar(thp, thp, INV_2PI, None, ALU.mult), deps=(t3,))
    k.consts_tok = tuple(toks.values()) + (t_ones, t1, t3)
    k.eoff = off


def mamba_phase(k):
    p, dr, ps = k.p, k.dr, k.psum
    A, Bm, C, Dm, E = k.A, k.Bm, k.C, k.Dm, k.E
    ones_row = k.sb["ones_row_f"]
    ones_col = k.sb["ones_col"]
    consts = k.consts_tok

    uT = A.rearrange("p (c l) -> p c l", c=8)
    utok = [Bm[:, i * 1024:(i + 1) * 1024] for i in range(2)]
    ld = [None, None]
    ev_last = {}
    for i in range(16):
        ld[i % 2] = k.dma("y", utok[i % 2], dr["u_own"][i * 128:(i + 1) * 128, :],
                          deps=consts + (ev_last.get(i - 2),), slot=f"u{i % 2}")
        ev_last[i] = k.tr_tiles(
            [(utok[i % 2][:, j * 128:(j + 1) * 128], uT[:, j, i * 128:(i + 1) * 128], None)
             for j in range(8)], (0, 1), deps=(ld[i % 2],))
    u_done = ev_last[15]

    wsl = [Dm[:, i * 1024:(i + 1) * 1024].rearrange("p (c m) -> p c m", c=8) for i in range(2)]
    stg = [Bm[:, 2048 + i * 512:2048 + (i + 1) * 512] for i in range(4)]
    halo = Bm[:, 12294:12348]
    wtok = [None, None]
    wuse = [None, None]
    ev_tok = {}
    st_tok = {}
    si = 0
    for r in range(35):
        wtok[r % 2] = k.dma("y", Dm[:, (r % 2) * 1024:(r % 2) * 1024 + 1024],
                            dr["w_in_t"][r], deps=(u_done, wuse[r % 2]),
                            slot=f"win{r % 2}")
        for lb in range(4):
            b = 2 + (si % 2)
            bk = ps[b]
            for kc in range(7):
                k.mmq(b, bk[:], wsl[r % 2][:, kc, :], uT[:, kc, lb * 512:(lb + 1) * 512],
                      kc == 0, False, deps=(wtok[r % 2],))
            mt = k.mm(b, bk[:], wsl[r % 2][:, 7, :], uT[:, 7, lb * 512:(lb + 1) * 512],
                      False, True, deps=(wtok[r % 2],))
            if lb == 3:
                wuse[r % 2] = mt
            s_ = stg[si % 4]
            t = k.rd(b, "v", lambda e, bb=bk, d=s_: e.tensor_copy(d, bb[:]),
                     deps=(mt, st_tok.get(si - 4)))
            if lb == 3 and 16 <= r < 34:
                t = k.op("v", lambda e, s2=s_, c=r - 16: e.tensor_copy(
                    halo[:, c * 3:c * 3 + 3], s2[:, 509:512]), deps=(t,))
            ev_tok[si] = t
            st_tok[si] = k.dma("s", dr["zxb"][r * 128:(r + 1) * 128, lb * 512:(lb + 1) * 512],
                               s_, deps=(t,), slot=f"stg{si % 4}")
            si += 1
    zx_done = tuple(st_tok[si - 1 - i] for i in range(4))

    hst = k.dma("g", dr["ag_h_in"], halo, deps=(ev_tok[si - 1],), slot="halo")
    halo_cc = k.coll("AllGather", ALU.bypass, PAIRS, dr["ag_h_in"], dr["ag_h_out"],
                     deps=(hst,))
    halo_rx = Bm[:, 12352:12406]
    t = k.dma("g", halo_rx, dr["ag_h_out"][0], deps=(halo_cc,), slot="halo")
    hmask = k.op("v", lambda e: e.tensor_scalar(halo_rx, halo_rx, k.sb["is_odd"], None,
                                                ALU.mult), deps=(t,))

    C_res = Dm[:, 0:2048]
    Acum_res = Dm[0:32, 2048:4096]
    S = Dm[:, 4096:6144]
    dtw = [Dm[0:32, 6144 + i * 512:6144 + (i + 1) * 512] for i in range(2)]
    CA_store = Dm[0:32, 7168:7184]
    CA_run = Dm[0:32, 7184:7185]
    lam_sb = Dm[0:1, 7186:7218]
    LamRep = Dm[:, 7220:7252]
    Btok = Dm[:, 7252:7764].rearrange("p (c n) -> p c n", c=4)
    convin = [Bm[:, i * 515:(i + 1) * 515] for i in range(2)]
    xconv = Bm[:, 1030:9222].rearrange("p (c l) -> p c l", c=16)
    acrow = [Bm[0:1, 9222 + i * 1024:9222 + (i + 1) * 1024] for i in range(2)]
    Bconv = Bm[:, 11270:11782]
    Cconv = Bm[:, 11782:12294]
    eo = k.eoff
    toks_ = E[:, eo:eo + 384].rearrange("p (c h) -> p c h", c=12)
    dt_tok = lambda cc: toks_[:, cc, :]
    decay_tok = lambda cc: toks_[:, 4 + cc, :]
    acum_tok = lambda cc: toks_[:, 8 + cc, :]
    Gm_t = E[:, eo + 384:eo + 512]
    L_t = E[:, eo + 512:eo + 640]
    GL_t = E[:, eo + 640:eo + 768]
    expA = E[0:32, eo + 768:eo + 896]
    eaF = E[:, eo + 896:eo + 1024]
    gptmp = E[:, eo + 1024:eo + 1536]
    L_t2 = Dm[:, 7892:8020]
    GL_t2 = Dm[:, 8020:8148]
    eaF_b = Bm[:, 12406:12534]
    xtok = C.rearrange("p (c q) -> p c q", c=4)

    s_zero = k.op("v", lambda e: e.memset(S, 0.0))
    CA_z = k.op("v", lambda e: e.memset(CA_run, 0.0))
    ones32 = Dm[0:32, 7764:7892]
    t_o32 = k.op("v", lambda e: e.memset(ones32, 1.0))
    prev = {}
    ydma = {}
    supd = {}
    prev["car"] = CA_z

    for lt in range(NLT):
        conv_done = []
        for ch in range(18):
            ci = convin[ch % 2]
            row0 = 2048 + ch * 128
            if lt == 0:
                t = k.dma("y", ci[:, 3:515], dr["zxb"][row0:row0 + 128, 0:512],
                          deps=zx_done + (prev.get(("cvslot", ch % 2)),),
                          slot=f"cv{ch % 2}")
                t = k.op("v", lambda e, c=ci, h=ch: e.tensor_copy(
                    c[:, 0:3], halo_rx[:, h * 3:h * 3 + 3]), deps=(hmask, t))
            else:
                t = k.dma("y", ci, dr["zxb"][row0:row0 + 128, lt * 512 - 3:(lt + 1) * 512],
                          deps=zx_done + (prev.get(("cvslot", ch % 2)),),
                          slot=f"cv{ch % 2}")
            dst = xconv[:, ch, :] if ch < 16 else (Bconv if ch == 16 else Cconv)
            dep_dst = ydma.get((lt - 1, 3)) if ch < 16 else prev.get(("bcfree", ch))
            if ch % 3 == 2:
                t0 = k.op("g", lambda e, c=ci, d=dst, h=ch: e.tensor_scalar(
                    d, c[:, 0:512], k.sb["conv_w"][:, h, 0:1], None, ALU.mult),
                    deps=(t, dep_dst))
                for tp in range(1, 4):
                    t0 = k.op("g", lambda e, c=ci, h=ch, tp_=tp: e.tensor_scalar(
                        gptmp, c[:, tp_:tp_ + 512], k.sb["conv_w"][:, h, tp_:tp_ + 1],
                        None, ALU.mult), deps=(t0,))
                    t0 = k.op("g", lambda e, d=dst: e.tensor_tensor(d, d, gptmp, ALU.add),
                              deps=(t0,))
            else:
                t0 = k.op("v", lambda e, c=ci, d=dst, h=ch: e.tensor_scalar(
                    d, c[:, 0:512], k.sb["conv_w"][:, h, 0:1], None, ALU.mult),
                    deps=(t, dep_dst))
                for tp in range(1, 4):
                    t0 = k.op("v", lambda e, c=ci, d=dst, h=ch, tp_=tp:
                              e.scalar_tensor_tensor(d, c[:, tp_:tp_ + 512],
                                                     k.sb["conv_w"][:, h, tp_:tp_ + 1], d,
                                                     ALU.mult, ALU.add), deps=(t0,))
            prev[("cvslot", ch % 2)] = t0
            t1 = k.op("s", lambda e, d=dst, h=ch: e.activation(
                out=d, in_=d, func=AF.Silu, bias=k.sb["conv_b"][:, h:h + 1], scale=1.0),
                deps=(t0,))
            conv_done.append(t1)
        ccp = k.op("v", lambda e, l=lt: e.tensor_copy(C_res[:, l * 512:(l + 1) * 512], Cconv),
                   deps=(conv_done[17],))
        prev[("bcfree", 17)] = ccp

        t = k.dma("y", dtw[0], dr["zxb"][4352:4384, lt * 512:(lt + 1) * 512],
                  deps=zx_done + (prev.get("dtwfree"),), slot="dt")
        t = k.op("v", lambda e: e.tensor_scalar(dtw[0], dtw[0], k.sb["dt_bias"], None,
                                                ALU.add), deps=(t,))
        t = k.op("s", lambda e: e.activation(out=dtw[1], in_=dtw[0], func=AF.Exp),
                 deps=(t, prev.get("dtw1free")))
        tdt = k.op("s", lambda e: e.activation(out=dtw[0], in_=dtw[1], func=AF.Ln, bias=1.0),
                   deps=(t,))
        dtt = k.tr_tiles([(dtw[0][:, cc * 128:(cc + 1) * 128], dt_tok(cc), tdt)
                          for cc in range(4)], (0, 1))
        adt = k.op("v", lambda e: e.tensor_scalar(dtw[0], dtw[0], k.sb["A_neg"], None,
                                                  ALU.mult), deps=(dtt,))
        sc_toks = []
        for cc in range(4):
            z = lt * 4 + cc
            ac = Acum_res[:, lt * 512 + cc * 128:lt * 512 + (cc + 1) * 128]
            sc = k.op("v", lambda e, a=ac, c=cc: e.tensor_tensor_scan(
                a, ones32, dtw[0][:, c * 128:(c + 1) * 128], 0.0, ALU.mult, ALU.add),
                deps=(adt, t_o32, prev.get(("acfree", z - 16))))
            t2 = k.op("v", lambda e, zz=z: e.tensor_copy(CA_store[:, zz:zz + 1], CA_run),
                      deps=(sc, prev.get("car")))
            t2 = k.op("v", lambda e, a=ac: e.tensor_tensor(CA_run, CA_run, a[:, 127:128],
                                                           ALU.add), deps=(t2,))
            prev["car"] = t2
            dd = k.op("v", lambda e, a=ac, c=cc: e.scalar_tensor_tensor(
                dtw[1][:, c * 128:(c + 1) * 128], a, -1.0,
                a[:, 127:128].to_broadcast((32, 128)), ALU.mult, ALU.add), deps=(sc,))
            dd = k.op("s", lambda e, c=cc: e.activation(
                out=dtw[1][:, c * 128:(c + 1) * 128], in_=dtw[1][:, c * 128:(c + 1) * 128],
                func=AF.Exp), deps=(dd,))
            sc_toks.append((sc, dd))
        att = k.tr_tiles([(Acum_res[:, lt * 512 + cc * 128:lt * 512 + (cc + 1) * 128],
                           acum_tok(cc), sc_toks[cc][0]) for cc in range(4)], (0, 1))
        dct = k.tr_tiles([(dtw[1][:, cc * 128:(cc + 1) * 128], decay_tok(cc),
                           sc_toks[cc][1]) for cc in range(4)], (0, 1))
        prev["dtwfree"] = att
        prev["dtw1free"] = dct

        xt_last = None
        for ch in range(16):
            xt_last = k.tr_tiles(
                [(xconv[:, ch, cc * 128:(cc + 1) * 128],
                  xtok[:, cc, ch * 128:(ch + 1) * 128], conv_done[ch]) for cc in range(4)],
                (0, 1), deps=(prev.get(("xtokfree", lt - 1)),))
        xs_tok = {}
        for cc in range(4):
            xs_tok[cc] = k.op("v", lambda e, c=cc: e.tensor_tensor(
                xtok[:, c].rearrange("p (h q) -> p h q", h=32),
                xtok[:, c].rearrange("p (h q) -> p h q", h=32),
                dt_tok(c).unsqueeze(2).to_broadcast((128, 32, 64)), ALU.mult),
                deps=(xt_last, dtt))
        btt = k.tr_tiles([(Bconv[:, cc * 128:(cc + 1) * 128], Btok[:, cc], conv_done[16])
                          for cc in range(4)], (0, 1))
        prev[("bcfree", 16)] = btt

        for cc in range(4):
            z = lt * 4 + cc
            ac = Acum_res[:, lt * 512 + cc * 128:lt * 512 + (cc + 1) * 128]
            t = k.op("t", lambda e, a=ac: e.transpose(ps[7][0:1, 256:288], a[:, 127:128],
                                                      k.sb["ident_f"][0:32, 0:32]),
                     deps=(sc_toks[cc][0],) + k.bank_deps(7))
            k.bank_clear(7)
            t = k.rd(7, "v", lambda e: e.tensor_copy(lam_sb, ps[7][0:1, 256:288]),
                     deps=(t, prev.get("lamfree")))
            t = k.op("s", lambda e: e.activation(out=lam_sb, in_=lam_sb, func=AF.Exp),
                     deps=(t,))
            t = k.op("t", lambda e: e.matmul(ps[7][:, 288:320], ones_row, lam_sb,
                                             start=True, stop=True, skip_group_check=True),
                     deps=(t,) + k.bank_deps(7))
            k.bank_clear(7)
            lamrep = k.rd(7, "v", lambda e: e.tensor_copy(LamRep, ps[7][:, 288:320]),
                          deps=(t, prev.get("lamrepfree")))
            prev["lamfree"] = lamrep
            g = k.op("t", lambda e, c=cc, l=lt: e.matmul(
                ps[7][:, 0:128], Bconv[:, c * 128:(c + 1) * 128],
                C_res[:, l * 512 + c * 128:l * 512 + (c + 1) * 128], start=True, stop=True,
                skip_group_check=True),
                deps=(conv_done[16], ccp) + k.bank_deps(7))
            k.bank_clear(7)
            gev = k.rd(7, "v", lambda e: e.tensor_tensor(Gm_t, ps[7][:, 0:128],
                                                         k.sb["tri_f"], ALU.mult),
                       deps=(g, prev.get("gmfree")))
            eat = k.op("s", lambda e, a=ac: e.activation(out=expA, in_=a, func=AF.Exp),
                       deps=(sc_toks[cc][0], prev.get("expAfree")))
            ac_t = {}
            ev_j = {}
            gl_last = None
            for j in range(16):
                if j % 4 == 0:
                    hg2 = j // 4
                    ac_t[hg2] = k.dma(
                        "g", acrow[hg2 % 2],
                        Acum_res[hg2 * 8:hg2 * 8 + 8,
                                 lt * 512 + cc * 128:lt * 512 + (cc + 1) * 128],
                        deps=(sc_toks[cc][0], prev.get(("acrowuse", hg2 % 2))),
                        slot=f"acr{hg2 % 2}")
                b = 2 + (j % 4)
                bank = ps[b]
                sl = slice((j // 4) * 128, (j // 4) * 128 + 128)
                yo = k.mm(b, bank[:, sl], S[:, j * 128:(j + 1) * 128],
                          C_res[:, lt * 512 + cc * 128:lt * 512 + (cc + 1) * 128],
                          j < 4, False,
                          deps=(ccp, s_zero, supd.get((z - 1, "done")),), skip=True)
                pj = 0
                ea_ps = ps[7]
                ea_bk = 7
                ea_reg = slice(128, 256)
                ea_buf = eaF
                ef = k.op("t", lambda e, jj=j, r=ea_reg, p_=ea_ps: e.matmul(
                    p_[:, r], k.sb["ind_hp"][:, jj * 128:(jj + 1) * 128], expA,
                    start=True, stop=True, skip_group_check=True),
                    deps=(eat, prev.get(("eaRfree", pj))) +
                    (k.bank_deps(7) if j == 0 else ()))
                efe = k.rd(ea_bk, "v", lambda e, r=ea_reg, eb=ea_buf, p_=ea_ps:
                           e.tensor_copy(eb, p_[:, r]),
                           deps=(ef, prev.get(("eaFfree", pj))))
                prev[("eaRfree", pj)] = efe
                fm = k.rd(b, "v", lambda e, bb=bank, s_=sl, eb=ea_buf: e.tensor_tensor(
                    bb[:, s_], bb[:, s_], eb, ALU.mult), deps=(yo, efe))
                prev[("eaFfree", pj)] = fm
                for hh in range(2):
                    h = 2 * j + hh
                    hg, hl = divmod(h, 8)
                    ph = h % 2
                    Lb = L_t if ph == 0 else L_t2
                    GLb = GL_t if ph == 0 else GL_t2
                    t6 = slice(0, 128) if ph == 0 else slice(256, 384)
                    tb = k.op("t", lambda e, a=acrow[hg % 2], hl_=hl, r=t6: e.matmul(
                        ps[6][:, r], ones_row, a[:, hl_ * 128:(hl_ + 1) * 128],
                        start=True, stop=True, skip_group_check=True),
                        deps=(ac_t[hg], prev.get(("t6free", ph))) +
                        (k.bank_deps(6) if j == 0 and hh == 0 else ()))
                    prev[("acrowuse", hg % 2)] = tb
                    lv = k.rd(6, "s", lambda e, c=cc, h_=h, r=t6, L_=Lb: e.activation(
                        out=L_, in_=ps[6][:, r], func=AF.Relu, scale=-1.0,
                        bias=acum_tok(c)[:, h_:h_ + 1]),
                        deps=(tb, att, prev.get(("ltfree", ph))))
                    prev[("t6free", ph)] = lv
                    lv = k.op("s", lambda e, L_=Lb: e.activation(out=L_, in_=L_,
                                                                 func=AF.Exp,
                                                                 scale=-1.0), deps=(lv,))
                    gl = k.op("v", lambda e, L_=Lb, G_=GLb: e.tensor_tensor(
                        G_, Gm_t, L_, ALU.mult),
                        deps=(lv, gev, prev.get(("glfree", ph))))
                    prev[("ltfree", ph)] = gl
                    yd = k.op("t", lambda e, bb=bank, s_=sl, c=cc, h_=h, pr=hh,
                              lst=(hh == 1), G_=GLb: e.matmul(
                                  bb[pr * 64:pr * 64 + 64, s_],
                                  xtok[:, c, h_ * 64:(h_ + 1) * 64], G_,
                                  start=False, stop=lst, skip_group_check=True),
                              deps=(fm, gl, xs_tok[cc]))
                    prev[("glfree", ph)] = yd
                    gl_last = yd
                ev_j[j] = k.rd(b, "v", lambda e, bb=bank, s_=sl, jj=j, c=cc:
                               e.scalar_tensor_tensor(
                                   xconv[:, jj, c * 128:(c + 1) * 128],
                                   xconv[:, jj, c * 128:(c + 1) * 128],
                                   k.sb["d_exp"][:, jj:jj + 1], bb[:, s_], ALU.mult,
                                   ALU.add), deps=(gl_last,))
            prev["gmfree"] = ev_j[15]
            prev["expAfree"] = ev_j[15]
            for j in range(16):
                prev[("evj", cc, j)] = ev_j[j]
            if cc == 3:
                ydma[(lt, 3)] = k.dma("s", dr["ystage"][lt], Bm[:, 1030:9222], slot="yst",
                                      deps=tuple(prev[("evj", cc2, j)]
                                                 for cc2 in range(4) for j in range(16)))
            xw = k.op("v", lambda e, c=cc: e.tensor_tensor(
                xtok[:, c].rearrange("p (h q) -> p h q", h=32),
                xtok[:, c].rearrange("p (h q) -> p h q", h=32),
                decay_tok(c).unsqueeze(2).to_broadcast((128, 32, 64)), ALU.mult),
                deps=(dct, gl_last))
            tsc = k.op("v", lambda e: e.tensor_tensor(
                S.rearrange("p (h q) -> p h q", h=32),
                S.rearrange("p (h q) -> p h q", h=32),
                LamRep[:, 0:32].unsqueeze(2).to_broadcast((128, 32, 64)), ALU.mult),
                deps=(lamrep, ev_j[15]))
            prev["lamrepfree"] = tsc
            sadd = None
            for sb2 in range(4):
                b = sb2 % 2
                st = k.mm(b, ps[b][:], Btok[:, cc], xtok[:, cc, sb2 * 512:(sb2 + 1) * 512],
                          True, True, deps=(btt, xw))
                sadd = k.rd(b, "v", lambda e, bb=ps[b], s2=sb2: e.tensor_tensor(
                    S[:, s2 * 512:(s2 + 1) * 512], S[:, s2 * 512:(s2 + 1) * 512], bb[:],
                    ALU.add), deps=(st, tsc))
                supd[(z, sb2)] = sadd
            supd[(z, "done")] = sadd
            prev[("acfree", z)] = sadd
        prev[("xtokfree", lt)] = supd[(lt * 4 + 3, "done")]

    t = k.dma("g", dr["ag_s_in"], S, deps=(supd[(15, "done")],), slot="sst")
    scc = k.coll("AllGather", ALU.bypass, PAIRS, dr["ag_s_in"], dr["ag_s_out"], deps=(t,))
    t = k.dma("g", S, dr["ag_s_out"][0], deps=(scc,), slot="sst")
    sprev_tok = k.op("v", lambda e: e.tensor_scalar(S, S, k.sb["is_odd"], None, ALU.mult),
                     deps=(t,))

    Yt = Bm[:, 0:8192].rearrange("p (c l) -> p c l", c=16)
    zring = Bm[:, 8192:8704]
    sqring = Bm[:, 8704:9216]
    wout = [Bm[:, 9216 + i * 1024:9216 + (i + 1) * 1024] for i in range(2)]
    tmp_row = Bm[0:1, 11264:11776]
    m_row = Dm[0:1, 6144:6656]
    q_row = Dm[0:1, 6656:7168]
    r_row = Dm[0:1, 7252:7764]
    x1 = uT
    k.ln_rows = (m_row, q_row, r_row, tmp_row)

    for lt in range(NLT):
        yld0 = k.dma("y", Bm[:, 0:8192], dr["ystage"][lt], slot="yld",
                     deps=(ydma[(lt, 3)], ydma[(3, 3)], supd[(15, "done")], sprev_tok) +
                     tuple(prev.get(("yfree", lt - 1, jj)) for jj in range(16)))
        yld = {jj: yld0 for jj in range(16)}
        for cc in range(4):
            z = lt * 4 + cc
            ac = Acum_res[:, lt * 512 + cc * 128:lt * 512 + (cc + 1) * 128]
            eag = k.op("s", lambda e, a=ac, zz=z: e.activation(
                out=expA, in_=a, func=AF.Exp, bias=CA_store[:, zz:zz + 1]),
                deps=(prev.get("expAfree"), sprev_tok))
            for j in range(16):
                b = 2 + (j % 4)
                bank = ps[b]
                sl = slice((j // 4) * 128, (j // 4) * 128 + 128)
                cm = k.mm(b, bank[:, sl], S[:, j * 128:(j + 1) * 128],
                          C_res[:, lt * 512 + cc * 128:lt * 512 + (cc + 1) * 128],
                          j < 4, j >= 12, deps=(sprev_tok,), skip=True)
                pj = 0
                ea_ps = ps[7]
                ea_bk = 7
                ea_reg = slice(128, 256)
                ea_buf = eaF
                ef = k.op("t", lambda e, jj=j, r=ea_reg, p_=ea_ps: e.matmul(
                    p_[:, r], k.sb["ind_hp"][:, jj * 128:(jj + 1) * 128], expA,
                    start=True, stop=True, skip_group_check=True),
                    deps=(eag, prev.get(("eaRfree", pj))) +
                    (k.bank_deps(7) if j == 0 else ()))
                efe = k.rd(ea_bk, "v", lambda e, r=ea_reg, eb=ea_buf, p_=ea_ps:
                           e.tensor_copy(eb, p_[:, r]),
                           deps=(ef, prev.get(("eaFfree", pj))))
                prev[("eaRfree", pj)] = efe
                fm = k.rd(b, "v", lambda e, bb=bank, s_=sl, eb=ea_buf: e.tensor_tensor(
                    bb[:, s_], bb[:, s_], eb, ALU.mult), deps=(cm, efe))
                prev[("eaFfree", pj)] = fm
                t = k.rd(b, "v", lambda e, bb=bank, s_=sl, jj=j, c=cc: e.tensor_tensor(
                    Yt[:, jj, c * 128:(c + 1) * 128], Yt[:, jj, c * 128:(c + 1) * 128],
                    bb[:, s_], ALU.add), deps=(fm, yld[j]))
                prev[("corr", z, j)] = t
            prev["expAfree"] = prev[("corr", z, 15)]
        corr_done = [prev[("corr", lt * 4 + 3, j)] for j in range(16)]
        gat = {}
        mmsq = None
        for jj in range(16):
            t = k.dma("y", zring, dr["zxb"][jj * 128:(jj + 1) * 128,
                                            lt * 512:(lt + 1) * 512],
                      deps=zx_done + (gat.get(jj - 1),), slot="zld")
            t = k.op("s", lambda e: e.activation(out=zring, in_=zring, func=AF.Silu),
                     deps=(t,))
            gat[jj] = k.op("v", lambda e, j_=jj: e.tensor_tensor(Yt[:, j_], Yt[:, j_],
                                                                 zring, ALU.mult),
                           deps=(t, corr_done[jj]))
            sq = k.op("s", lambda e, j_=jj: e.activation(out=sqring, in_=Yt[:, j_],
                                                         func=AF.Square),
                      deps=(gat[jj], mmsq))
            mmsq = k.mm(6, ps[6][0:1, 0:512], ones_col, sqring, jj == 0, jj == 15,
                        deps=(sq,), skip=True)
        t = k.rd(6, "v", lambda e: e.tensor_scalar(m_row, ps[6][0:1, 0:512], 1.0 / 2048.0,
                                                   None, ALU.mult),
                 deps=(mmsq, prev.get("mrowfree")))
        rst = k.newton_rsqrt(r_row, m_row, tmp_row, t, 1e-5)
        prev["mrowfree"] = rst
        rbc = k.mm(6, ps[6][:, 0:512], ones_row, r_row, True, True, deps=(rst,))
        app = {}
        for jj in range(16):
            app[jj] = k.rd(6, "v", lambda e, j_=jj: e.scalar_tensor_tensor(
                Yt[:, j_], Yt[:, j_], k.sb["normw_col"][:, j_:j_ + 1], ps[6][:, 0:512],
                ALU.mult, ALU.mult), deps=(rbc, gat[jj]))
        wo_tok = [None, None]
        wo_use = [None, None]
        opmm = None
        for ch in range(16):
            wo_tok[ch % 2] = k.dma("y", wout[ch % 2], dr["w_out_t"][ch],
                                   deps=(wo_use[ch % 2], supd[(15, "done")]),
                                   slot=f"wo{ch % 2}")
            for dc in range(7):
                k.mmq(dc, ps[dc][:], wout[ch % 2][:, dc * 128:(dc + 1) * 128], Yt[:, ch],
                      ch == 0, ch == 15,
                      deps=(wo_tok[ch % 2], app[ch]) +
                      ((app[15],) if ch == 0 and dc >= 6 else ()))
            wo_use[ch % 2] = k.mm(7, ps[7][:], wout[ch % 2][:, 7 * 128:8 * 128], Yt[:, ch],
                                  ch == 0, ch == 15,
                                  deps=(wo_tok[ch % 2], app[ch]) +
                                  ((app[15],) if ch == 0 else ()))
            if ch == 15:
                opmm = wo_use[ch % 2]
        for jj in range(16):
            prev[("yfree", lt, jj)] = opmm
        ln_feature_major(k, lambda dc, l=lt: x1[:, dc, l * 512:(l + 1) * 512],
                         lambda dc: ps[dc][:], opmm, lt, prev, "m3", gptmp)
    k.x1_ready = prev[("lnap", "m3", NLT - 1)]
    k.mprev = prev


def ln_feature_major(k, res_ap, psum_ap, mm_tok, lt, prev, tag, sqtile):
    ps = k.psum
    ones_row, ones_col = k.sb["ones_row_f"], k.sb["ones_col"]
    m_row, q_row, r_row, tmp_row = k.ln_rows
    radd = {}
    for dc in range(8):
        radd[dc] = k.rd(dc, "v", lambda e, d=dc: e.tensor_tensor(
            res_ap(d), res_ap(d), psum_ap(d), ALU.add), deps=(mm_tok,))
    smm = None
    qmm = None
    sqt = None
    for dc in range(8):
        smm = k.mm(6, ps[6][0:1, 0:512], ones_col, res_ap(dc), dc == 0, dc == 7,
                   deps=(radd[dc],), skip=True)
        sqt = k.op("s", lambda e, d=dc: e.activation(out=sqtile, in_=res_ap(d),
                                                     func=AF.Square), deps=(radd[dc], qmm))
        qmm = k.mm(7, ps[7][0:1, 0:512], ones_col, sqtile, dc == 0, dc == 7,
                   deps=(sqt,), skip=True)
    t = k.rd(6, "v", lambda e: e.tensor_scalar(m_row, ps[6][0:1, 0:512], 1.0 / 1024.0, None,
                                               ALU.mult), deps=(smm, prev.get("lnmfree")))
    t2 = k.rd(7, "v", lambda e: e.tensor_scalar(q_row, ps[7][0:1, 0:512], 1.0 / 1024.0, None,
                                                ALU.mult), deps=(qmm, prev.get("lnqfree")))
    t3 = k.op("v", lambda e: e.tensor_tensor(tmp_row, m_row, m_row, ALU.mult), deps=(t,))
    t3 = k.op("v", lambda e: e.tensor_tensor(q_row, q_row, tmp_row, ALU.subtract),
              deps=(t3, t2))
    rst = k.newton_rsqrt(r_row, q_row, tmp_row, t3, 1e-5)
    tb = k.mm(6, ps[6][:, 0:512], ones_row, m_row, True, True, deps=(rst,))
    tb2 = k.mm(7, ps[7][:, 0:512], ones_row, r_row, True, True, deps=(rst,))
    lnap = None
    for dc in range(8):
        t4 = k.rd(6, "v", lambda e, d=dc: e.tensor_tensor(
            res_ap(d), res_ap(d), ps[6][:, 0:512], ALU.subtract), deps=(tb, radd[dc]))
        lnap = k.rd(7, "v", lambda e, d=dc: e.tensor_tensor(
            res_ap(d), res_ap(d), ps[7][:, 0:512], ALU.mult), deps=(tb2, t4))
        prev[("lnapc", tag, lt, dc)] = lnap
    prev["lnmfree"] = lnap
    prev["lnqfree"] = lnap
    prev[("lnap", tag, lt)] = lnap
    return lnap


def l0ffn_phase(k):
    p, dr, ps = k.p, k.dr, k.psum
    A, Bm, C, Dm, E = k.A, k.Bm, k.C, k.Dm, k.E
    prev = dict(k.mprev) if hasattr(k, "mprev") else {}
    prev = {}
    x1 = A.rearrange("p (c l) -> p c l", c=8)
    w1sl = [Bm[:, i * 1024:(i + 1) * 1024].rearrange("p (c m) -> p c m", c=8)
            for i in range(2)]
    w2sl = [Bm[:, 2048 + i * 2048:2048 + (i + 1) * 2048].rearrange("p (c m) -> p c m", c=16)
            for i in range(2)]
    outacc = Bm[:, 6144:10240].rearrange("p (c l) -> p c l", c=8)
    vt = Bm[:, 10240:10752]
    st_ = Bm[:, 10752:11264]
    ktf = Bm[:, 11264:11776]
    kt = ktf.bitcast(I32)[:, 0:512]
    asx = Bm[:, 11776:12288]
    act = C.rearrange("p (c l) -> p c l", c=16)
    x2st = Dm[:, 0:4096].rearrange("p (c l) -> p c l", c=8)
    k.ln_rows = (Dm[0:1, 4096:4608], Dm[0:1, 4608:5120], Dm[0:1, 5120:5632],
                 Dm[0:1, 5632:6144])

    x2dma = {}
    for lt in range(NLT):
        for fh in range(2):
            actuse = prev.get(("actuse", fh))
            for ff in range(16):
                f = fh * 16 + ff
                w1t = k.dma("y", Bm[:, (ff % 2) * 1024:(ff % 2) * 1024 + 1024],
                            dr["w1"][f], deps=(k.x1_ready, prev.get(("w1use", ff % 2))),
                            slot=f"w1{ff % 2}")
                b = ff % 2
                for kc in range(7):
                    k.mmq(b, ps[b][:], w1sl[ff % 2][:, kc, :],
                          x1[:, kc, lt * 512:(lt + 1) * 512], kc == 0, False,
                          deps=(w1t,))
                hmm = k.mm(b, ps[b][:], w1sl[ff % 2][:, 7, :],
                           x1[:, 7, lt * 512:(lt + 1) * 512], False, True, deps=(w1t,))
                prev[("w1use", ff % 2)] = hmm
                t = k.rd(b, "v", lambda e, bb=ps[b], f_=f: e.tensor_scalar(
                    vt, bb[:], 0.1 * INV_2PI, k.sb["thp_col"][:, f_:f_ + 1], ALU.mult,
                    ALU.add), deps=(hmm, prev.get("vtfree")))
                t = k.op("v", lambda e: e.tensor_scalar(ktf, vt, MAGIC, MAGIC, ALU.add,
                                                        ALU.subtract),
                         deps=(t, prev.get("ktfree")))
                t = k.op("v", lambda e: e.tensor_copy(kt, ktf), deps=(t,))
                t = k.op("v", lambda e: e.scalar_tensor_tensor(st_, kt, -1.0, vt, ALU.mult,
                                                               ALU.add),
                         deps=(t, prev.get("stfree")))
                prev["ktfree"] = t
                t = k.op("s", lambda e: e.activation(out=st_, in_=st_, func=AF.Sin,
                                                     scale=TWO_PI), deps=(t,))
                t = k.op("v", lambda e, f_=f: e.tensor_scalar(
                    asx, st_, k.sb["alpha_col"][:, f_:f_ + 1], None, ALU.mult),
                    deps=(t, prev.get("asxfree")))
                t = k.rd(b, "v", lambda e, bb=ps[b]: e.scalar_tensor_tensor(
                    st_, bb[:], 1.0, asx, ALU.mult, ALU.add), deps=(t,))
                prev["vtfree"] = t
                prev["asxfree"] = t
                t = k.op("s", lambda e, f_=f, ff_=ff: e.activation(
                    out=act[:, ff_], in_=st_, func=AF.Relu,
                    bias=k.sb["b1_col"][:, f_:f_ + 1]), deps=(t, actuse))
                prev["stfree"] = t
                prev[("act", ff)] = t
            w2t = [None, None]
            w2use = [None, None]
            last_mm = None
            for dc in range(8):
                w2t[dc % 2] = k.dma("y", Bm[:, 2048 + (dc % 2) * 2048:
                                            2048 + (dc % 2) * 2048 + 2048],
                                    dr["w2"][fh][dc], deps=(w2use[dc % 2],),
                                    slot=f"w2{dc % 2}")
                b = 2 + (dc % 2)
                for ff in range(15):
                    k.mmq(b, ps[b][:], w2sl[dc % 2][:, ff, :], act[:, ff],
                          ff == 0, False, deps=(w2t[dc % 2], prev[("act", ff)]))
                last_mm = k.mm(b, ps[b][:], w2sl[dc % 2][:, 15, :], act[:, 15],
                               False, True, deps=(w2t[dc % 2], prev[("act", 15)]))
                w2use[dc % 2] = last_mm
                if fh == 0:
                    t = k.rd(b, "v", lambda e, bb=ps[b], d=dc: e.tensor_copy(
                        outacc[:, d], bb[:]), deps=(last_mm, prev.get(("oafree", dc))))
                else:
                    t = k.rd(b, "v", lambda e, bb=ps[b], d=dc: e.tensor_tensor(
                        outacc[:, d], outacc[:, d], bb[:], ALU.add),
                        deps=(last_mm, prev.get(("oafree", dc))))
                prev[("oacc", fh, dc)] = t
            prev[("actuse", fh)] = last_mm
        radd_dep = [prev[("oacc", 1, dc)] for dc in range(8)]
        for dc in range(8):
            t = k.op("v", lambda e, d=dc, l=lt: e.scalar_tensor_tensor(
                outacc[:, d], outacc[:, d], k.sb["b2_col"][:, d:d + 1],
                x1[:, d, l * 512:(l + 1) * 512], ALU.add, ALU.add), deps=(radd_dep[dc],))
            prev[("oafree", dc)] = t
            radd_dep[dc] = t
        lnin = ln_ffn(k, outacc, radd_dep, prev, lt)
        cps = []
        for dc in range(8):
            t = k.op("v", lambda e, d=dc, l=lt: e.tensor_copy(
                x1[:, d, l * 512:(l + 1) * 512], outacc[:, d]), deps=(lnin[dc],))
            cps.append(t)
            prev[("oafree", dc)] = t
        x2dma[lt] = k.dma("s", dr["x2own"][lt], Bm[:, 6144:10240], deps=tuple(cps), slot="x2st")
        for dc in range(8):
            prev[("oafree", dc)] = x2dma[lt]
    k.x2_ready = x2dma[NLT - 1]
    k.x2dma = x2dma


def ln_ffn(k, res, radd_dep, prev, lt, res_fn=None):
    if res_fn is None:
        res_fn = lambda dc: res[:, dc]
    ps = k.psum
    ones_row, ones_col = k.sb["ones_row_f"], k.sb["ones_col"]
    m_row, q_row, r_row, tmp_row = k.ln_rows
    sqtile = k.E[:, k.eoff + 1024:k.eoff + 1536]
    smm = None
    qmm = None
    for dc in range(8):
        smm = k.mm(6, ps[6][0:1, 0:512], ones_col, res_fn(dc), dc == 0, dc == 7,
                   deps=(radd_dep[dc],), skip=True)
        sqt = k.op("s", lambda e, d=dc: e.activation(out=sqtile, in_=res_fn(d),
                                                     func=AF.Square),
                   deps=(radd_dep[dc], qmm))
        qmm = k.mm(7, ps[7][0:1, 0:512], ones_col, sqtile, dc == 0, dc == 7,
                   deps=(sqt,), skip=True)
    t = k.rd(6, "v", lambda e: e.tensor_scalar(m_row, ps[6][0:1, 0:512], 1.0 / 1024.0, None,
                                               ALU.mult), deps=(smm, prev.get("lnmfree")))
    t2 = k.rd(7, "v", lambda e: e.tensor_scalar(q_row, ps[7][0:1, 0:512], 1.0 / 1024.0,
                                                None, ALU.mult),
              deps=(qmm, prev.get("lnqfree")))
    t3 = k.op("v", lambda e: e.tensor_tensor(tmp_row, m_row, m_row, ALU.mult), deps=(t,))
    t3 = k.op("v", lambda e: e.tensor_tensor(q_row, q_row, tmp_row, ALU.subtract),
              deps=(t3, t2))
    rst = k.newton_rsqrt(r_row, q_row, tmp_row, t3, 1e-5)
    tb = k.mm(6, ps[6][:, 0:512], ones_row, m_row, True, True, deps=(rst,))
    tb2 = k.mm(7, ps[7][:, 0:512], ones_row, r_row, True, True, deps=(rst,))
    outs = []
    for dc in range(8):
        t4 = k.rd(6, "v", lambda e, d=dc: e.tensor_tensor(
            res_fn(d), res_fn(d), ps[6][:, 0:512], ALU.subtract), deps=(tb, radd_dep[dc]))
        t5 = k.rd(7, "v", lambda e, d=dc: e.tensor_tensor(
            res_fn(d), res_fn(d), ps[7][:, 0:512], ALU.mult), deps=(tb2, t4))
        outs.append(t5)
    prev["lnmfree"] = outs[-1]
    prev["lnqfree"] = outs[-1]
    return outs


def attn_phase(k):
    p, dr, ps = k.p, k.dr, k.psum
    A, Bm, C, Dm, E = k.A, k.Bm, k.C, k.Dm, k.E
    prev = {}
    eo = k.eoff
    xcc = k.coll("AllGather", ALU.bypass, PAIRS, dr["x2own"], dr["ag_x_out"],
                 deps=(k.x2_ready,))
    xa = A.rearrange("p (b c l) -> p b c l", b=4, c=8)
    xc = C.rearrange("p (b c l) -> p b c l", b=2, c=8)
    xd = Dm.rearrange("p (b c l) -> p b c l", b=2, c=8)
    ldx = tuple(
        k.dma("y",
              (A[:, i * 4096:(i + 1) * 4096] if i < 4 else
               (C[:, (i - 4) * 4096:(i - 3) * 4096] if i < 6 else
                Dm[:, (i - 6) * 4096:(i - 5) * 4096])),
              dr["ag_x_out"][i // 4][i % 4], deps=(xcc,), slot=f"xf{i % 4}")
        for i in range(8))

    def x2ap(kc, blk):
        if blk < 4:
            return xa[:, blk, kc, :]
        if blk < 6:
            return xc[:, blk - 4, kc, :]
        return xd[:, blk - 6, kc, :]

    qt = Bm[:, 0:4096]
    ktile = Bm[:, 4096:8192]
    vtok = Bm[:, 8192:12352].rearrange("p (c m) -> p c m", c=32)
    io = k.indoff
    wslab = E[:, io:io + 1024].rearrange("p (c m) -> p c m", c=8)
    wslab_flat = E[:, io:io + 1024]
    Pt = [E[:, io + 1024:io + 1536], E[:, io + 1536:io + 2048]]
    den_c = E[:, eo:eo + 1]
    att_dma = {}

    vones = k.op("v", lambda e: e.memset(
        vtok[:, :, 64:65].rearrange("p c m -> p (c m)"), 1.0))
    vones2 = k.op("v", lambda e: e.memset(
        vtok[:, :, 129:130].rearrange("p c m -> p (c m)"), 1.0))

    for hp in range(4):
        vev = {}
        wld = k.dma("y", wslab_flat, dr["wv_t"][hp],
                    deps=ldx + (prev.get("wslab_use"),), slot="wsl")
        for blk in range(8):
            b = blk % 2
            for kc in range(7):
                k.mmq(b, ps[b][:], wslab[:, kc, :], x2ap(kc, blk), kc == 0, False,
                      deps=(wld,))
            mt = k.mm(b, ps[b][:], wslab[:, 7, :], x2ap(7, blk), False, True,
                      deps=(wld,))
            vev[blk] = k.rd(b, "v", lambda e, bb=ps[b], b2=blk: e.tensor_copy(
                qt[:, b2 * 512:(b2 + 1) * 512], bb[:]),
                deps=(mt, prev.get(("qfree", hp))))
            if blk == 7:
                prev["wslab_use"] = mt
        vtt = None
        for i in range(32):
            b = 2 + (i % 2)
            dep = (vev[i // 4],) + k.bank_deps(b) + (vones, vones2,
                                                     prev.get(("vtokfree", hp)))
            k.bank_clear(b)
            tt = k.op("t", lambda e, s=qt[:, i * 128:(i + 1) * 128], b3=ps[b]:
                      e.transpose(b3[:, 0:128], s, k.sb["ident_f"]), deps=dep)
            for hh in range(2):
                vtt = k.rd(b, "v", lambda e, b3=ps[b], i_=i, h_=hh: e.tensor_copy(
                    vtok[:, i_, h_ * 65:h_ * 65 + 64], b3[:, h_ * 64:h_ * 64 + 64]),
                    deps=(tt,))
        qev = {}
        kev = {}
        for (dst, wnm, evd) in ((qt, "wq_t", qev), (ktile, "wk_t", kev)):
            wld = k.dma("y", wslab_flat, dr[wnm][hp],
                        deps=((vtt,) if dst is qt else ()) + (prev["wslab_use"],),
                        slot="wsl")
            for blk in range(8):
                b = blk % 2
                for kc in range(7):
                    k.mmq(b, ps[b][:], wslab[:, kc, :], x2ap(kc, blk), kc == 0, False,
                          deps=(wld,) + ((vtt,) if dst is qt else
                                         (prev.get(("kfree", hp)),)))
                mt = k.mm(b, ps[b][:], wslab[:, 7, :], x2ap(7, blk), False, True,
                          deps=(wld,) + ((vtt,) if dst is qt else
                                         (prev.get(("kfree", hp)),)))
                evd[blk] = k.rd(b, "v", lambda e, bb=ps[b], b2=blk, d2=dst: e.tensor_copy(
                    d2[:, b2 * 512:(b2 + 1) * 512], bb[:]), deps=(mt,))
                if blk == 7:
                    prev["wslab_use"] = mt
        qk_done = tuple(qev.values()) + tuple(kev.values())

        for lb in range(8):
            nsc = lb * 4 + 4
            scb = [(0, 1), (2, 3)]
            pev = {}
            avlast = {0: None, 1: None}
            pmuse = {}
            for sc in range(nsc):
                for hh in range(2):
                    b = scb[sc % 2][hh]
                    diag = lb * 512 <= sc * 128 < (lb + 1) * 512
                    smm = k.mm(b, ps[b][:],
                               ktile[hh * 64:hh * 64 + 64, sc * 128:(sc + 1) * 128],
                               qt[hh * 64:hh * 64 + 64, lb * 512:(lb + 1) * 512],
                               True, True, deps=qk_done)
                    if diag:
                        off = sc * 128 - lb * 512
                        smm = k.op("t", lambda e, bb=ps[b], o=off: e.matmul(
                            bb[:, o:o + 128], k.sb["ident_b"], k.sb["mask_b"],
                            start=False, stop=True, skip_group_check=True), deps=())
                    pev[(sc, hh)] = k.rd(b, "s", lambda e, bb=ps[b], h_=hh: e.activation(
                        out=Pt[h_], in_=bb[:], func=AF.Exp, scale=0.125),
                        deps=(smm, pmuse.get((sc - 1, hh)),
                              (prev.get("last_att") if sc == 0 else None),
                              (prev.get(("ptuse", hh)) if sc == 0 else None)))
                for hh in range(2):
                    bav = 6 + hh
                    last = None
                    for j in range(4):
                        if lb * 4 + j < sc:
                            continue
                        last = k.mm(bav, ps[bav][:, j * 65:j * 65 + 65],
                                    Pt[hh][:, j * 128:(j + 1) * 128],
                                    vtok[:, sc, hh * 65:hh * 65 + 65],
                                    sc == 0 and j == 0, False,
                                    deps=(pev[(sc, hh)], vtt), skip=True)
                    pmuse[(sc, hh)] = last
                    avlast[hh] = last
            out_sl = Pt[0]
            outw = []
            for hh in range(2):
                bav = 6 + hh
                for j in range(4):
                    t = k.rd(bav, "v", lambda e, bb=ps[bav], j_=j: e.tensor_scalar(
                        den_c, bb[:, j_ * 65 + 64:j_ * 65 + 65], 1e-8, None, ALU.add),
                        deps=(avlast[hh], pev[(nsc - 1, 0)], pev[(nsc - 1, 1)],
                              prev.get("denfree")))
                    t = k.op("v", lambda e: e.reciprocal(den_c, den_c), deps=(t,))
                    t = k.rd(bav, "v", lambda e, bb=ps[bav], j_=j, h_=hh:
                             e.tensor_scalar(
                                 out_sl[:, j_ * 128 + h_ * 64:j_ * 128 + h_ * 64 + 64],
                                 bb[:, j_ * 65:j_ * 65 + 64], den_c, None, ALU.mult),
                             deps=(t,))
                    prev["denfree"] = t
                    outw.append(t)
            att_dma[(hp, lb)] = k.dma(
                "s", dr["attn_heads"][hp][lb], out_sl, deps=tuple(outw),
                slot=f"att{lb % 2}")
            prev[("ptuse", 0)] = avlast[0]
            prev[("ptuse", 1)] = avlast[1]
            prev["last_att"] = att_dma[(hp, lb)]
        prev[("qfree", hp)] = att_dma[(hp, 7)]
        prev[("kfree", hp)] = att_dma[(hp, 7)]
        prev[("vtokfree", hp)] = att_dma[(hp, 7)]

    attn_outproj(k, att_dma[(3, 7)])


def attn_outproj(k, attn_done):
    p, dr, ps = k.p, k.dr, k.psum
    A, Bm, C, Dm, E = k.A, k.Bm, k.C, k.Dm, k.E
    prev = {}
    acc = k.coll("AllGather", ALU.bypass, PAIRS, dr["attn_heads"], dr["ag_a_out"],
                 deps=(attn_done,))
    wo = Bm[:, 0:8192].rearrange("p (c m) -> p c m", c=8)
    attT = Bm[:, 8192:9216].rearrange("p (c m) -> p c m", c=8)
    cand = C.rearrange("p (a h l) -> p a h l", a=4, h=4)
    outacc = Dm[:, 0:4096].rearrange("p (c l) -> p c l", c=8)
    x2res = Dm[:, 4096:8192].rearrange("p (c l) -> p c l", c=8)
    x3 = A.rearrange("p (c l) -> p c l", c=8)
    io = k.indoff
    k.ln_rows = (E[0:1, io:io + 512], E[0:1, io + 512:io + 1024],
                 E[0:1, io + 1024:io + 1536], E[0:1, io + 1536:io + 2048])
    wotok = []
    for c in range(8):
        wotok.append(k.dma("y", Bm[:, c * 1024:(c + 1) * 1024], dr["wo_t"][c],
                           deps=(attn_done,), slot=f"wo{c % 2}"))
    woldt = (wotok[-1], wotok[-2])

    for lbo in range(4):
        lds = {}
        for a, (slot, lbc) in enumerate(((0, lbo), (1, lbo), (0, lbo + 4), (1, lbo + 4))):
            for hp in range(4):
                lds[(a, hp)] = k.dma("y", cand[:, a, hp, :], dr["ag_a_out"][slot][hp][lbc],
                                     deps=(acc, prev.get(("canduse", a, hp))),
                                     slot=f"cnd{a}")
        bl = {}
        for slot in range(2):
            for hp in range(4):
                a0, a1 = slot, slot + 2
                t = k.op("v", lambda e, a0_=a0, a1_=a1, h=hp: e.tensor_tensor(
                    cand[:, a1_, h, :], cand[:, a1_, h, :], cand[:, a0_, h, :],
                    ALU.subtract), deps=(lds[(a0, hp)], lds[(a1, hp)],
                                         prev.get(("canduse", slot, hp))))
                t = k.op("v", lambda e, a0_=a0, a1_=a1, h=hp: e.scalar_tensor_tensor(
                    cand[:, a1_, h, :], cand[:, a1_, h, :], k.sb["is_odd"],
                    cand[:, a0_, h, :], ALU.mult, ALU.add), deps=(t,))
                bl[slot * 4 + hp] = t
        for rr in range(4):
            r = lbo * 4 + rr
            tt = k.tr_tiles(
                [(cand[:, 2 + c // 4, c % 4, rr * 128:(rr + 1) * 128], attT[:, c, :],
                  bl[c]) for c in range(8)], (2, 3), deps=(prev.get("attTuse"),))
            opmm = None
            for kc in range(8):
                for dc in range(8):
                    b = 4 + dc // 4
                    if kc == 7 and dc == 7:
                        opmm = k.mm(b, ps[b][:, (dc % 4) * 128:(dc % 4) * 128 + 128],
                                    wo[:, kc, dc * 128:(dc + 1) * 128], attT[:, kc, :],
                                    False, True, deps=(tt,) + woldt, skip=True)
                    else:
                        k.mmq(b, ps[b][:, (dc % 4) * 128:(dc % 4) * 128 + 128],
                              wo[:, kc, dc * 128:(dc + 1) * 128], attT[:, kc, :],
                              kc == 0 and dc % 4 == 0, False,
                              deps=(tt,) + woldt, skip=True)
            prev["attTuse"] = opmm
            for dc in range(8):
                b = 4 + dc // 4
                prev[("opev", rr, dc)] = k.rd(
                    b, "v", lambda e, bb=ps[b], d=dc, r_=rr: e.tensor_copy(
                        outacc[:, d, r_ * 128:(r_ + 1) * 128],
                        bb[:, (d % 4) * 128:(d % 4) * 128 + 128]),
                    deps=(opmm, prev.get(("oafree", dc))))
        for a in range(4):
            for hp in range(4):
                prev[("canduse", a, hp)] = prev[("opev", 3, 7)]
        xr = k.dma("y", Dm[:, 4096:8192], dr["x2own"][lbo], deps=(prev.get("x2resuse"),), slot="x2r")
        radd = []
        for dc in range(8):
            t = k.op("v", lambda e, d=dc: e.tensor_tensor(
                outacc[:, d], outacc[:, d], x2res[:, d], ALU.add),
                deps=(xr,) + tuple(prev[("opev", rr, dc)] for rr in range(4)))
            radd.append(t)
        prev["x2resuse"] = radd[-1]
        lnt = ln_ffn(k, outacc, radd, prev, lbo)
        for dc in range(8):
            t = k.op("v", lambda e, d=dc, l=lbo: e.tensor_copy(
                x3[:, d, l * 512:(l + 1) * 512], outacc[:, d]), deps=(lnt[dc],))
            prev[("oafree", dc)] = t
    k.x3_ready = prev[("oafree", 7)]
    k.p.waits("s", [prev[("oafree", dc)] for dc in range(8)])
    k.x3dbg = k.dma("s", dr["dbg"], A, deps=(), slot="out")


def l1ffn_phase(k):
    p, dr, ps = k.p, k.dr, k.psum
    A, Bm, C, Dm, E = k.A, k.Bm, k.C, k.Dm, k.E
    prev = {}
    x3 = A.rearrange("p (c l) -> p c l", c=8)
    io = k.indoff
    mx = E[:, io:io + 1]
    mx2 = E[:, io + 1:io + 2]
    mx3 = E[:, io + 2:io + 3]
    srow = E[0:1, io + 4:io + 136]
    scol = {}
    rcol = {}
    for i, nm in enumerate(("s1", "s2", "s3", "s4", "s5")):
        scol[nm] = E[:, io + 140 + 2 * i:io + 141 + 2 * i]
        rcol[nm] = E[:, io + 141 + 2 * i:io + 142 + 2 * i]
    bmax = E[:, io + 152:io + 248]
    bscale = E[:, io + 248:io + 344]
    brecip = E[:, io + 344:io + 440]
    x3q = [C.rearrange("p (c l) -> p c l", c=4), Dm.rearrange("p (c l) -> p c l", c=4)]

    def x3qap(kc):
        return x3q[kc // 4][:, kc % 4, :]

    def global_max_scale(nm, slot, part_tok, qmax, deps):
        d = tuple(deps)
        t = k.op("t", lambda e: e.transpose(ps[6][0:1, 0:128], mx, k.sb["ident_f"]),
                 deps=d + k.bank_deps(6))
        k.bank_clear(6)
        t = k.rd(6, "v", lambda e: e.tensor_reduce(srow[:, 0:1], ps[6][0:1, 0:128], AX,
                                                   ALU.max), deps=(t,))
        t = k.dma("g", dr["ar_in"][:, slot:slot + 1], srow[:, 0:1], deps=(t,), slot="arg")
        t = k.coll("AllReduce", ALU.max, ALLC, dr["ar_in"][:, slot:slot + 1],
                   dr["ar_out"][:, slot:slot + 1], deps=(t,))
        t = k.dma("y", srow[:, 1:2], dr["ar_out"][:, slot:slot + 1], deps=(t,), slot="ary")
        t = k.op("t", lambda e: e.matmul(ps[6][:, 0:1], k.sb["ones_row_f"], srow[:, 1:2],
                                         start=True, stop=True), deps=(t,) + k.bank_deps(6))
        k.bank_clear(6)
        t = k.rd(6, "v", lambda e, n=nm, q=qmax: e.tensor_scalar(
            scol[n], ps[6][:, 0:1], 1.0 / q, 1e-12, ALU.mult, ALU.max), deps=(t,))
        t = k.op("v", lambda e, n=nm: e.reciprocal(rcol[n], scol[n]), deps=(t,))
        return t

    t = None
    for kc in range(8):
        t = k.op("v", lambda e, c=kc: e.tensor_reduce(
            mx2, x3[:, c, :], AX, ALU.max, apply_absolute_value=True),
            deps=(k.x3_ready, t))
        t = k.op("v", lambda e, c=kc: (e.tensor_copy(mx, mx2) if c == 0 else
                                       e.tensor_tensor(mx, mx, mx2, ALU.max)), deps=(t,))
    s1 = global_max_scale("s1", 0, mx, 32767.0, (t,))
    xqt = []
    xqh = C.bitcast(BF16).rearrange("p (c l) -> p c l", c=8)
    xql = Dm.bitcast(BF16).rearrange("p (c l) -> p c l", c=8)
    qt_ = Bm[:, 4096:6144]
    t = None
    for kc in range(8):
        t = k.op("v", lambda e, c=kc: e.tensor_scalar(
            qt_, x3[:, c, :], rcol["s1"], None, ALU.mult), deps=(s1, t))
        t = k.op("v", lambda e: e.tensor_scalar(
            qt_.bitcast(I32)[:, 0:2048], qt_, MAGIC, MAGIC, ALU.add, ALU.subtract),
            deps=(t,))
        t = k.op("v", lambda e: e.tensor_scalar(
            qt_, qt_.bitcast(I32)[:, 0:2048], scol["s1"], None, ALU.mult), deps=(t,))
        t = k.op("v", lambda e, c=kc: e.tensor_copy(xqh[:, c, :], qt_), deps=(t,))
        t = k.op("v", lambda e, c=kc: e.tensor_tensor(xql[:, c, :], qt_, xqh[:, c, :],
                                                      ALU.subtract), deps=(t,))
        xqt.append(t)
    xq_done = tuple(xqt)

    wnat = [Bm[:, i * 1024:(i + 1) * 1024] for i in range(2)]
    wi32 = Bm[:, 2048:3072].bitcast(I32)[:, 0:1024]
    wTh = Bm[:, 3072:3584].bitcast(BF16).rearrange("p (c m) -> p c m", c=8)
    wTl = Bm[:, 3584:4096].bitcast(BF16).rearrange("p (c m) -> p c m", c=8)
    stg = [Bm[:, 4096 + i * 512:4096 + (i + 1) * 512] for i in range(4)]

    def dequant_slab(src_dram, o, wslot, deps):
        t = k.dma("y", wnat[wslot], src_dram[o], deps=deps, slot=f"wn{wslot}")
        t = k.op("v", lambda e, w=wnat[wslot]: e.tensor_reduce(
            bmax[:, 0:32], w.rearrange("p (b q) -> p b q", q=32), AX, ALU.max,
            apply_absolute_value=True), deps=(t, prev.get("bmaxfree")))
        t = k.op("v", lambda e: e.tensor_scalar(bscale[:, 0:32], bmax[:, 0:32], 1.0 / 7.0,
                                                1e-12, ALU.mult, ALU.max), deps=(t,))
        t = k.op("v", lambda e: e.reciprocal(brecip[:, 0:32], bscale[:, 0:32]), deps=(t,))
        t = k.op("v", lambda e, w=wnat[wslot]: e.tensor_tensor(
            Bm[:, 2048:3072].rearrange("p (b q) -> p b q", q=32),
            w.rearrange("p (b q) -> p b q", q=32),
            brecip[:, 0:32].unsqueeze(2).to_broadcast((128, 32, 32)), ALU.mult),
            deps=(t, prev.get("wi32free")))
        t = k.op("v", lambda e: e.tensor_scalar(
            wi32, Bm[:, 2048:3072], MAGIC, MAGIC, ALU.add, ALU.subtract), deps=(t,))
        t = k.op("v", lambda e, w=wnat[wslot]: e.tensor_tensor(
            w.rearrange("p (b q) -> p b q", q=32),
            wi32.rearrange("p (b q) -> p b q", q=32),
            bscale[:, 0:32].unsqueeze(2).to_broadcast((128, 32, 32)), ALU.mult),
            deps=(t,))
        prev["bmaxfree"] = t
        prev["wi32free"] = t
        last = None
        for i in range(8):
            b = i % 2
            bk = ps[b]
            d = (t, prev.get("wTuse")) + k.bank_deps(b)
            k.bank_clear(b)
            tt = k.op("t", lambda e, w=wnat[wslot], i_=i, bb=bk: e.transpose(
                bb[0:128, 0:128], w[:, i_ * 128:(i_ + 1) * 128], k.sb["ident_f"]),
                deps=d)
            t1 = k.rd(b, "v", lambda e, bb=bk, i_=i: e.tensor_copy(
                wTh[:, i_, :], bb[0:128, 0:128]), deps=(tt,))
            last = k.rd(b, "v", lambda e, bb=bk, i_=i: e.tensor_tensor(
                wTl[:, i_, :], bb[0:128, 0:128], wTh[:, i_, :], ALU.subtract),
                deps=(t1,))
        return last

    mmax = {"up": None, "gate": None}
    stok = {}
    for o in range(24):
        for im, (nm, src, dst) in enumerate((("gate", dr["wg_nat"], dr["gatest"]),
                                             ("up", dr["wu_nat"], dr["upst"]))):
            wdq = dequant_slab(src, o, im, deps=(prev.get(("wnatuse", im)),))
            for lb in range(4):
                b = 2 + lb % 2
                sl_ = slice(lb * 512, (lb + 1) * 512)
                mt = None
                for kc in range(8):
                    for wTx, xx in ((wTh, xqh), (wTl, xqh), (wTh, xql)):
                        first = kc == 0 and xx is xqh and wTx is wTh
                        lastm = kc == 7 and wTx is wTh and xx is xql
                        if lastm:
                            mt = k.mm(b, ps[b][:], wTx[:, kc, :], xx[:, kc, sl_],
                                      False, True, deps=(wdq,) + xq_done)
                        else:
                            k.mmq(b, ps[b][:], wTx[:, kc, :], xx[:, kc, sl_],
                                  first, False, deps=(wdq,) + xq_done)
                if lb == 3:
                    prev[("wnatuse", im)] = mt
                    prev["wTuse"] = mt
                s_ = stg[lb]
                t = k.rd(b, "v", lambda e, bb=ps[b], s2=s_: e.tensor_copy(s2, bb[:]),
                         deps=(mt, stok.get(("last", lb))))
                t2 = k.op("v", lambda e, s2=s_: e.tensor_reduce(
                    mx2, s2, AX, ALU.max, apply_absolute_value=True), deps=(t,))
                t2 = k.op("v", lambda e, first=(o == 0 and lb == 0), m=mmax[nm], n=nm:
                          (e.tensor_copy(mx if n == "gate" else mx3, mx2) if first else
                           e.tensor_tensor(mx if n == "gate" else mx3,
                                           mx if n == "gate" else mx3, mx2, ALU.max)),
                          deps=(t2, mmax[nm]))
                mmax[nm] = t2
                stok[(nm, o, lb)] = k.dma("s", dst[o][:, lb * 512:(lb + 1) * 512], s_,
                                          deps=(t,), slot=f"gu{lb}")
                stok[("last", lb)] = stok[(nm, o, lb)]
    gu_done = tuple(stok[(nm, 23, lb)] for nm in ("gate", "up") for lb in range(4))
    s3 = global_max_scale("s3", 2, mx, 32767.0, (mmax["gate"],))
    t = k.op("v", lambda e: e.tensor_copy(mx, mx3), deps=(s3, mmax["up"]))
    s2 = global_max_scale("s2", 1, mx, 32767.0, (t,))

    wdn = C[:, 0:3072]
    wdf = C[:, 3072:6144]
    wdi = wdf.bitcast(I32)[:, 0:3072]
    wdtt = Dm[:, 0:3072]
    xq_war = (prev[("wnatuse", 0)], prev[("wnatuse", 1)])
    for dc in range(8):
        t = k.dma("y", wdn, dr["wd_nat"][dc],
                  deps=(prev.get("wdnatuse"),) + (xq_war if dc == 0 else ()), slot="wd")
        t = k.op("v", lambda e: e.tensor_reduce(
            bmax[:, 0:96], wdn.rearrange("p (b q) -> p b q", q=32), AX, ALU.max,
            apply_absolute_value=True), deps=(t, prev.get("bmaxfree")))
        t = k.op("v", lambda e: e.tensor_scalar(bscale[:, 0:96], bmax[:, 0:96], 1.0 / 7.0,
                                                1e-12, ALU.mult, ALU.max), deps=(t,))
        t = k.op("v", lambda e: e.reciprocal(brecip[:, 0:96], bscale[:, 0:96]), deps=(t,))
        t = k.op("v", lambda e: e.tensor_tensor(
            wdf.rearrange("p (b q) -> p b q", q=32),
            wdn.rearrange("p (b q) -> p b q", q=32),
            brecip[:, 0:96].unsqueeze(2).to_broadcast((128, 96, 32)), ALU.mult),
            deps=(t, prev.get("wdi32free")))
        t = k.op("v", lambda e: e.tensor_scalar(
            wdi, wdf, MAGIC, MAGIC, ALU.add, ALU.subtract), deps=(t,))
        t = k.op("v", lambda e: e.tensor_tensor(
            wdn.rearrange("p (b q) -> p b q", q=32),
            wdi.rearrange("p (b q) -> p b q", q=32),
            bscale[:, 0:96].unsqueeze(2).to_broadcast((128, 96, 32)), ALU.mult), deps=(t,))
        prev["bmaxfree"] = t
        prev["wdi32free"] = t
        wdh = wdtt[:, 0:1536].bitcast(BF16).rearrange("p (c m) -> p c m", c=24)
        wdl = wdtt[:, 1536:3072].bitcast(BF16).rearrange("p (c m) -> p c m", c=24)
        tt = None
        for i in range(24):
            b = i % 2
            bk = ps[b]
            d = (t, prev.get("wdTuse")) + k.bank_deps(b)
            k.bank_clear(b)
            t1 = k.op("t", lambda e, i_=i, bb=bk: e.transpose(
                bb[0:128, 0:128], wdn[:, i_ * 128:(i_ + 1) * 128], k.sb["ident_f"]),
                deps=d)
            t2 = k.rd(b, "v", lambda e, bb=bk, i_=i: e.tensor_copy(
                wdh[:, i_, :], bb[0:128, 0:128]), deps=(t1,))
            tt = k.rd(b, "v", lambda e, bb=bk, i_=i: e.tensor_tensor(
                wdl[:, i_, :], bb[0:128, 0:128], wdh[:, i_, :], ALU.subtract),
                deps=(t2,))
        prev["wdnatuse"] = tt
        prev["wdTuse"] = k.dma("s", dr["wdt_scr"][dc], wdtt, deps=(tt,), slot="wdt")
    wdT_done = prev["wdTuse"]

    g2st = dr["zxb"]
    g2stok = {}
    gmax = None
    gldb = [Bm[:, 10240:12288], Bm[:, 0:2048]]
    sigb = [Bm[:, 6144:8192], Bm[:, 2048:4096]]
    g2b = [Bm[:, 8192:10240], Bm[:, 4096:6144]]
    for o in range(24):
        pa = o % 2
        g, sg, g2_ = gldb[pa], sigb[pa], g2b[pa]
        t = k.dma("y", g, dr["gatest"][o], deps=(gu_done if o < 2 else ()) +
                  (s3, prev.get(("glduse", pa))), slot=f"gld{pa}")
        t = k.op("v", lambda e, w=g: e.tensor_scalar(w, w, rcol["s3"], None, ALU.mult),
                 deps=(t,))
        t = k.op("v", lambda e, w=g: e.tensor_scalar(
            w.bitcast(I32)[:, 0:2048], w, MAGIC, MAGIC, ALU.add, ALU.subtract),
            deps=(t,))
        t = k.op("v", lambda e, w=g: e.tensor_scalar(
            w, w.bitcast(I32)[:, 0:2048], scol["s3"], None, ALU.mult), deps=(t,))
        t = k.op("s", lambda e, w=g, s_=sg: e.activation(out=s_, in_=w, func=AF.Sigmoid),
                 deps=(t, prev.get(("sigfree", pa))))
        t = k.op("v", lambda e, s_=sg: e.tensor_scalar(s_, s_, 65536.0, None, ALU.mult),
                 deps=(t,))
        t = k.op("v", lambda e, s_=sg: e.tensor_scalar(
            s_.bitcast(I32)[:, 0:2048], s_, MAGIC, MAGIC, ALU.add, ALU.subtract),
            deps=(t,))
        t = k.op("v", lambda e, s_=sg: e.tensor_scalar(s_, s_.bitcast(I32)[:, 0:2048],
                                                       float(2.0 ** -16),
                                                       float(65535.0 * 2.0 ** -16),
                                                       ALU.mult, ALU.min), deps=(t,))
        t = k.op("v", lambda e, w=g, s_=sg, g_=g2_: e.tensor_tensor(g_, w, s_, ALU.mult),
                 deps=(t, prev.get(("g2free", pa))))
        prev[("sigfree", pa)] = t
        prev[("glduse", pa)] = t
        g2stok[o] = k.dma("s", g2st[o * 128:(o + 1) * 128, :], g2_, deps=(t,),
                          slot=f"g2s{pa}")
        t2 = k.op("v", lambda e, g_=g2_: e.tensor_reduce(
            mx2, g_, AX, ALU.max, apply_absolute_value=True), deps=(t,))
        gmax = k.op("v", lambda e, first=(o == 0): (
            e.tensor_copy(mx, mx2) if first else e.tensor_tensor(mx, mx, mx2, ALU.max)),
            deps=(t2, gmax))
        prev[("g2free", pa)] = g2stok[o]
    s4 = global_max_scale("s4", 3, mx, 32767.0, (gmax,))

    omax = None
    uldb = [Bm[:, 0:2048], Bm[:, 2048:4096]]
    gldc = [Bm[:, 10240:12288], Bm[:, 8192:10240]]
    opre = [Bm[:, 4096:6144], Bm[:, 6144:8192]]
    ot = {}
    for o in range(24):
        pa = o % 2
        g, u, g2_ = gldc[pa], uldb[pa], opre[pa]
        tg = k.dma("y", g, g2st[o * 128:(o + 1) * 128, :],
                   deps=(s4, g2stok[o], prev.get(("gldcuse", pa))) +
                   ((g2stok[22],) if o == 1 else ()), slot=f"gld{pa}")
        tu = k.dma("y", u, dr["upst"][o],
                   deps=(s4, prev.get(("ulduse", pa))), slot=f"uld{pa}")
        t = k.op("v", lambda e, w=g: e.tensor_scalar(w, w, rcol["s4"], None, ALU.mult),
                 deps=(tg,))
        t = k.op("v", lambda e, w=g: e.tensor_scalar(
            w.bitcast(I32)[:, 0:2048], w, MAGIC, MAGIC, ALU.add, ALU.subtract),
            deps=(t,))
        t = k.op("v", lambda e, w=g: e.tensor_scalar(w, w.bitcast(I32)[:, 0:2048],
                                                     scol["s4"], None, ALU.mult), deps=(t,))
        t2 = k.op("v", lambda e, w=u: e.tensor_scalar(w, w, rcol["s2"], None, ALU.mult),
                  deps=(tu,))
        t2 = k.op("v", lambda e, w=u: e.tensor_scalar(
            w.bitcast(I32)[:, 0:2048], w, MAGIC, MAGIC, ALU.add, ALU.subtract),
            deps=(t2,))
        t2 = k.op("v", lambda e, w=u: e.tensor_scalar(w, w.bitcast(I32)[:, 0:2048],
                                                      scol["s2"], None, ALU.mult),
                  deps=(t2,))
        t = k.op("v", lambda e, w=g, w2=u, g_=g2_: e.tensor_tensor(g_, w, w2, ALU.mult),
                 deps=(t, t2, prev.get(("opfree", pa))) +
                 ((g2stok[23],) if o == 0 else ()))
        prev[("gldcuse", pa)] = t
        prev[("ulduse", pa)] = t
        t2 = k.op("v", lambda e, g_=g2_: e.tensor_reduce(
            mx2, g_, AX, ALU.max, apply_absolute_value=True), deps=(t,))
        omax = k.op("v", lambda e, first=(o == 0): (
            e.tensor_copy(mx, mx2) if first else e.tensor_tensor(mx, mx, mx2, ALU.max)),
            deps=(t2, omax))
        ot[o] = k.dma("s", dr["ost"][o], g2_, deps=(t,), slot=f"ost{pa}")
        prev[("opfree", pa)] = ot[o]
    s5 = global_max_scale("s5", 3, mx, 32767.0, (omax,))

    oq = x3q
    oqv = C.rearrange("p (c l) -> p c l", c=16)
    oqv2 = Dm[:, 0:4096].rearrange("p (c l) -> p c l", c=8)
    k.ln_rows = (Dm[0:1, 4096:4608], Dm[0:1, 4608:5120], Dm[0:1, 5120:5632],
                 Dm[0:1, 5632:6144])
    wdt = [Bm[:, i * 3072:(i + 1) * 3072] for i in range(2)]
    ostage = Bm[:, 6144 + 512:6144 + 512 + 1024]

    def oqap(fi):
        return oqv[:, fi, :] if fi < 16 else oqv2[:, fi - 16, :]

    oqtmp = Bm[:, 7680:8192]
    out_dma = []
    for lb in range(4):
        oqd = []
        for fi in range(24):
            a = oqap(fi)
            ah = a[:, 0:256].bitcast(BF16)
            al = a[:, 256:512].bitcast(BF16)
            t = k.dma("y", a, dr["ost"][fi][:, lb * 512:(lb + 1) * 512],
                      deps=(s5, wdT_done, ot[fi], prev.get(("oquse", fi))),
                      slot=f"oq{fi % 4}")
            t = k.op("v", lambda e, a_=a: e.tensor_scalar(
                oqtmp, a_, rcol["s5"], None, ALU.mult),
                deps=(t, prev.get("oqtfree")) +
                ((ot[23],) if lb == 0 and fi == 0 else ()))
            t = k.op("v", lambda e: e.tensor_scalar(
                oqtmp.bitcast(I32)[:, 0:512], oqtmp, MAGIC, MAGIC, ALU.add,
                ALU.subtract), deps=(t,))
            t = k.op("v", lambda e: e.tensor_scalar(
                oqtmp, oqtmp.bitcast(I32)[:, 0:512], scol["s5"], None, ALU.mult),
                deps=(t,))
            t = k.op("v", lambda e, h=ah: e.tensor_copy(h, oqtmp), deps=(t,))
            t = k.op("v", lambda e, h=ah, l=al: e.tensor_tensor(
                l, oqtmp, h, ALU.subtract), deps=(t,))
            prev["oqtfree"] = t
            oqd.append(t)
        radd = []
        for dc in range(8):
            wldt = k.dma("y", wdt[dc % 2], dr["wdt_scr"][dc],
                         deps=(prev.get(("wdtuse", dc % 2)),) +
                         ((prev.get(("ulduse", 0)), prev.get(("ulduse", 1)),
                           prev.get(("opfree", 0))) if lb == 0 and dc < 2 else ()),
                         slot=f"wdl{dc % 2}")
            wdth = wdt[dc % 2][:, 0:1536].bitcast(BF16).rearrange(
                "p (c m) -> p c m", c=24)
            wdtl = wdt[dc % 2][:, 1536:3072].bitcast(BF16).rearrange(
                "p (c m) -> p c m", c=24)
            b = 2 + dc % 2
            mt = None
            for fi in range(24):
                a = oqap(fi)
                ah = a[:, 0:256].bitcast(BF16)
                al = a[:, 256:512].bitcast(BF16)
                for wx, xx in ((wdth, ah), (wdtl, ah), (wdth, al)):
                    first = fi == 0 and wx is wdth and xx is ah
                    lastm = fi == 23 and wx is wdth and xx is al
                    if lastm:
                        mt = k.mm(b, ps[b][:], wx[:, fi, :], xx, False, True,
                                  deps=(wldt, oqd[fi]))
                    else:
                        k.mmq(b, ps[b][:], wx[:, fi, :], xx, first, False,
                              deps=(wldt, oqd[fi]))
            prev[("wdtuse", dc % 2)] = mt
            t = k.rd(b, "v", lambda e, bb=ps[b], d=dc, l=lb: e.tensor_tensor(
                x3[:, d, l * 512:(l + 1) * 512], x3[:, d, l * 512:(l + 1) * 512], bb[:],
                ALU.add), deps=(mt,))
            radd.append(t)
        for fi in range(24):
            prev[("oquse", fi)] = radd[-1]
        lnt = ln_ffn(k, None, radd, prev, lb,
                     res_fn=lambda dc, l=lb: x3[:, dc, l * 512:(l + 1) * 512])
        for rr in range(4):
            r = lb * 4 + rr
            tt = k.tr_tiles(
                [(x3[:, i, r * 128:(r + 1) * 128], ostage[:, i * 128:(i + 1) * 128],
                  lnt[i]) for i in range(8)], (0, 1), deps=(prev.get("ostuse"),))
            od = k.dma("s", dr["out_own"][r * 128:(r + 1) * 128, :], ostage, deps=(tt,), slot="out")
            prev["ostuse"] = od
            out_dma.append(od)
    k.out_done = out_dma[-1]


_NC_CACHE = {}


def _get_nc():
    if "nc" not in _NC_CACHE:
        _NC_CACHE["nc"] = build_program()
    return _NC_CACHE["nc"]


def kernel(**inputs):
    import ml_dtypes
    ins = {kk: np.asarray(v) for kk, v in inputs.items()}
    nc = _get_nc()

    ident_f = np.eye(128, dtype=np.float32)
    ident_b = np.eye(128, dtype=np.float32)
    sl = np.arange(128)
    tri_f = (sl[:, None] <= sl[None, :]).astype(np.float32)
    mask_b = np.where(sl[:, None] > sl[None, :], np.float32(-1e30), np.float32(0.0))
    ones_row_f = np.ones((1, 128), np.float32)
    ind = np.zeros((32, 16, 128), np.float32)
    for h in range(32):
        for j in range(16):
            ind[h, j, :] = ((2 * j + (np.arange(128) // 64)) == h).astype(np.float32)
    ind = ind.reshape(32, 2048)

    w_in = ins["L0_attn_W_in"]
    w_in_pad = np.zeros((ZXB_PAD, D), np.float32)
    w_in_pad[:4384] = w_in
    w_in_t = np.ascontiguousarray(
        w_in_pad.T.reshape(8, 128, 35, 128).transpose(2, 1, 0, 3).reshape(35, 128, 1024))
    w_out_t = np.ascontiguousarray(ins["L0_attn_W_out"].T.reshape(16, 128, D))
    conv_w = np.zeros((18 * 128, DCONV), np.float32)
    conv_w[:2304] = ins["L0_attn_conv_w"][:, 0, :]
    conv_w = np.ascontiguousarray(
        conv_w.reshape(18, 128, DCONV).transpose(1, 0, 2).reshape(128, 72))
    conv_b = np.zeros((18 * 128,), np.float32)
    conv_b[:2304] = ins["L0_attn_conv_b"]
    conv_b = np.ascontiguousarray(conv_b.reshape(18, 128).T)
    d_exp = np.zeros((128, 16), np.float32)
    for j in range(16):
        d_exp[:, j] = ins["L0_attn_D_param"][2 * j + np.arange(128) // 64]
    normw_col = np.ascontiguousarray(ins["L0_attn_norm_w"].reshape(16, 128).T)

    w1 = np.ascontiguousarray(
        ins["L0_ffn_W1"].reshape(8, 128, 32, 128).transpose(2, 1, 0, 3).reshape(32, 128, 1024))
    w2 = np.ascontiguousarray(
        ins["L0_ffn_W2"].reshape(2, 16, 128, 8, 128).transpose(0, 3, 2, 1, 4)
        .reshape(2, 8, 128, 2048))
    b1_col = np.ascontiguousarray(ins["L0_ffn_b1"].reshape(32, 128).T)
    qp_col = np.ascontiguousarray(ins["L0_ffn_quantum_phases"].reshape(32, 128).T)
    qr_col = np.ascontiguousarray(ins["L0_ffn_quantum_weights_real"].reshape(32, 128).T)
    qi_col = np.ascontiguousarray(ins["L0_ffn_quantum_weights_imag"].reshape(32, 128).T)
    b2_col = np.ascontiguousarray(ins["L0_ffn_b2"].reshape(8, 128).T)

    wq_t_full = ins["L1_attn_wq"].T
    wk_t_full = ins["L1_attn_wk"].T
    wv_t_full = ins["L1_attn_wv"].T
    wo_t = np.ascontiguousarray(ins["L1_attn_wo"].T.reshape(8, 128, D))
    wg_nat = np.ascontiguousarray(ins["L1_ffn_w_gate"].reshape(24, 128, D))
    wu_nat = np.ascontiguousarray(ins["L1_ffn_w_up"].reshape(24, 128, D))
    wd_nat = np.ascontiguousarray(ins["L1_ffn_w_down"].reshape(8, 128, FI))
    u = ins["L0_attn_u"]

    def qkv_tile(w_t, half):
        s = w_t[:, half * 512:(half + 1) * 512]
        a = s.reshape(8, 128, 4, 128).transpose(2, 1, 0, 3)
        return np.ascontiguousarray(a.reshape(4, 128, 1024))

    in_maps = []
    for c in range(8):
        b, half = c // 2, c % 2
        m = dict(
            u_own=np.ascontiguousarray(u[b, half * HALF:(half + 1) * HALF, :]),
            w_in_t=w_in_t, w_out_t=w_out_t, conv_w=conv_w, conv_b=conv_b,
            dt_bias=np.ascontiguousarray(ins["L0_attn_dt_bias"].reshape(NH, 1)),
            a_log=np.ascontiguousarray(ins["L0_attn_A_log"].reshape(NH, 1)),
            d_exp=d_exp, normw_col=normw_col,
            w1=w1, w2=w2, b1_col=b1_col, qp_col=qp_col, qr_col=qr_col, qi_col=qi_col,
            b2_col=b2_col,
            wq_t=qkv_tile(wq_t_full, half), wk_t=qkv_tile(wk_t_full, half),
            wv_t=qkv_tile(wv_t_full, half),
            wo_t=wo_t, wg_nat=wg_nat, wu_nat=wu_nat, wd_nat=wd_nat,
            ident_f=ident_f, ident_b=ident_b, tri_f=tri_f, mask_b=mask_b,
            ones_row_f=ones_row_f,
            is_odd=np.full((128, 1), float(half), np.float32),
            ind_hp=ind,
        )
        in_maps.append(m)

    res = run_bass_kernel_spmd(nc, in_maps, core_ids=list(range(8)))
    out = np.zeros((B, L, D), np.float32)
    for c in range(8):
        b, half = c // 2, c % 2
        out[b, half * HALF:(half + 1) * HALF, :] = res.results[c]["out_own"]
    return out



# version 07
# speedup vs baseline: 1.1780x; 1.0405x over previous
import numpy as np
import concourse.bass as bass
import concourse.mybir as mybir
from concourse.bass_utils import run_bass_kernel_spmd

F32 = mybir.dt.float32
BF16 = mybir.dt.bfloat16
I32 = mybir.dt.int32
AF = mybir.ActivationFunctionType
ALU = mybir.AluOpType
AX = mybir.AxisListType.X

B, L, D = 4, 4096, 1024
HALF = 2048
DIN, NH, HD, DCONV = 2048, 32, 64, 4
ZXB_PAD = 4480
LT, NLT = 512, 4
FI = 3072
INV_2PI = float(1.0 / (2.0 * np.pi))
TWO_PI = float(2.0 * np.pi)
MAGIC = 12582912.0
PAIRS = [[0, 1], [2, 3], [4, 5], [6, 7]]
ALLC = [list(range(8))]


class Plan:
    SEMKEYS = ("t", "v", "s", "g", "dy", "ds", "dg", "cc")

    def __init__(self):
        self.q = {k: [] for k in "tvsgy"}
        self.cnt = {k: 0 for k in self.SEMKEYS}
        self.step = dict(t=1, v=1, s=1, g=1, dy=16, ds=16, dg=16, cc=1)
        self.last_wait = {e: {k: 0 for k in self.SEMKEYS} for e in "tvsgy"}
        self.sems = {}
        self.slotmap = {}
        self.dsem_pool = []

    def emit(self, ek, fn):
        self.q[ek].append(fn)

    def mark(self, ek, key, fn):
        self.cnt[key] += self.step[key]
        sems, step = self.sems, self.step[key]
        self.q[ek].append(lambda e: fn(e).then_inc(sems[key], step))
        return (key, self.cnt[key])

    def wait(self, ek, tok):
        if tok is None:
            return
        key, v = tok
        if self.last_wait[ek][key] >= v:
            return
        self.last_wait[ek][key] = v
        sems = self.sems
        self.q[ek].append(lambda e: e.wait_ge(sems[key], v))

    def waits(self, ek, toks):
        for t in toks:
            if t is not None:
                self.wait(ek, t)

    def barrier(self):
        own = {}
        for ek in "tvsg":
            self.wait(ek, (ek, self.cnt[ek]))
            own[ek] = self.mark(ek, ek, lambda e: e.nop())
        toks = [(kk, v) for kk, v in self.cnt.items() if v > 0]
        for ek in "tvsgy":
            for tok in toks:
                if tok[0] != ek:
                    self.wait(ek, tok)


DMAQ = {"y": "dy", "s": "ds", "g": "dg"}


class K:
    def __init__(self, nc, p, dram, psum):
        self.nc, self.p, self.dr, self.psum = nc, p, dram, psum
        self.sb = {}
        self.bank_readers = [dict() for _ in range(8)]

    def bank_deps(self, b):
        return tuple(self.bank_readers[b].values())

    def bank_read(self, b, ek, tok):
        self.bank_readers[b][tok[0]] = tok

    def bank_clear(self, b):
        self.bank_readers[b] = {}

    def dma(self, ek, dst_ap, src_ap, deps=(), slot=None):
        p = self.p
        assert slot is not None
        if slot not in p.slotmap:
            idx = len(p.slotmap)
            sem = p.dsem_pool[idx]
            p.slotmap[slot] = sem
            p.sems[slot] = sem
            p.cnt[slot] = 0
            p.step[slot] = 16
            for e2 in "tvsgy":
                p.last_wait[e2][slot] = 0
        p.waits(ek, deps)
        if p.cnt[slot] > 0:
            p.wait(ek, (slot, p.cnt[slot]))
        return p.mark(ek, slot, lambda e: e.dma_start(out=dst_ap, in_=src_ap))

    def op(self, ek, fn, deps=()):
        self.p.waits(ek, deps)
        return self.p.mark(ek, ek, fn)

    def opq(self, ek, fn, deps=()):
        self.p.waits(ek, deps)
        self.p.emit(ek, fn)

    def coll(self, kind, op, groups, in_ap, out_ap, deps=()):
        self.p.waits("g", deps)
        return self.p.mark("g", "cc", lambda e: e.collective_compute(
            kind, op, replica_groups=groups, ins=[in_ap.opt()], outs=[out_ap.opt()]))

    def mm(self, b, out_ap, lhsT, rhs, start, stop, deps=(), skip=False):
        d = tuple(deps) + (self.bank_deps(b) if start else ())
        if start:
            self.bank_clear(b)
        return self.op("t", lambda e: e.matmul(out_ap, lhsT, rhs, start=start, stop=stop,
                                               skip_group_check=skip), deps=d)

    def mmq(self, b, out_ap, lhsT, rhs, start, stop, deps=(), skip=False):
        d = tuple(deps) + (self.bank_deps(b) if start else ())
        if start:
            self.bank_clear(b)
        self.opq("t", lambda e: e.matmul(out_ap, lhsT, rhs, start=start, stop=stop,
                                         skip_group_check=skip), deps=d)

    def rd(self, b, ek, fn, deps=()):
        t = self.op(ek, fn, deps=deps)
        self.bank_read(b, ek, t)
        return t

    def tr_tiles(self, tiles, bank_pair, deps=(), ek="v"):
        ident = self.sb["ident_f"]
        last = None
        for i, (src, dst, dep) in enumerate(tiles):
            b = bank_pair[i % 2]
            bk = self.psum[b]
            r, c = src.shape[-2], src.shape[-1]
            self.mm(b, None, None, None, True, True) if False else None
            d = tuple(deps) + (dep,) + self.bank_deps(b)
            self.bank_clear(b)
            tt = self.op("t", lambda e, s=src, bb=bk, c_=c, r_=r: e.transpose(
                bb[0:c_, 0:r_], s, ident[0:r_, 0:r_]), deps=d)
            last = self.rd(b, ek, lambda e, bb=bk, d2=dst, c_=c, r_=r: e.tensor_copy(
                d2, bb[0:c_, 0:r_]), deps=(tt,))
        return last

    def newton_rsqrt(self, row, m_row, tmp_row, dep, eps):
        t = self.op("s", lambda e: e.activation(out=row, in_=m_row, func=AF.Sqrt, bias=eps),
                    deps=(dep,))
        t = self.op("v", lambda e: e.reciprocal(row, row), deps=(t,))
        for _ in range(2):
            t1 = self.op("v", lambda e: e.tensor_tensor(tmp_row, row, row, ALU.mult),
                         deps=(t,))
            t1 = self.op("v", lambda e: e.tensor_tensor(tmp_row, tmp_row, m_row, ALU.mult),
                         deps=(t1,))
            t1 = self.op("v", lambda e: e.tensor_scalar(tmp_row, tmp_row, -0.5,
                                                        1.5 - 0.5 * eps, ALU.mult, ALU.add),
                         deps=(t1,))
            t = self.op("v", lambda e: e.tensor_tensor(row, row, tmp_row, ALU.mult),
                        deps=(t1,))
        return t


def build_program():
    nc = bass.Bass("TRN2", target_bir_lowering=False)
    p = Plan()
    dram = {}

    def din(name, shape, dtype=F32):
        dram[name] = nc.dram_tensor(name, list(shape), dtype, kind="ExternalInput").ap()

    def dout(name, shape, dtype=F32):
        dram[name] = nc.dram_tensor(name, list(shape), dtype, kind="ExternalOutput").ap()

    def dscr(name, shape, dtype=F32):
        dram[name] = nc.dram_tensor(name, list(shape), dtype).ap()

    din("u_own", (HALF, D))
    din("w_in_t", (35, 128, 1024))
    din("w_out_t", (16, 128, D))
    din("conv_w", (128, 72))
    din("conv_b", (128, 18))
    din("dt_bias", (NH, 1))
    din("a_log", (NH, 1))
    din("d_exp", (128, 16))
    din("normw_col", (128, 16))
    din("w1", (32, 128, 1024))
    din("w2", (2, 8, 128, 2048))
    din("b1_col", (128, 32))
    din("qp_col", (128, 32))
    din("qr_col", (128, 32))
    din("qi_col", (128, 32))
    din("b2_col", (128, 8))
    din("wq_t", (4, 128, 1024))
    din("wk_t", (4, 128, 1024))
    din("wv_t", (4, 128, 1024))
    din("wo_t", (8, 128, D))
    din("wg_nat", (24, 128, D))
    din("wu_nat", (24, 128, D))
    din("wd_nat", (8, 128, FI))
    din("ident_f", (128, 128))
    din("ident_b", (128, 128))
    din("tri_f", (128, 128))
    din("mask_b", (128, 128))
    din("ones_row_f", (1, 128))
    din("is_odd", (128, 1))
    din("ind_hp", (32, 16 * 128))

    dout("out_own", (HALF, D))

    dscr("zxb", (ZXB_PAD, HALF))
    dscr("ystage", (NLT, 128, 8192))
    dscr("x2own", (NLT, 128, 4096))
    dscr("attn_heads", (4, 8, 128, 512))
    dscr("upst", (24, 128, HALF))
    dscr("gatest", (24, 128, HALF))
    dscr("ost", (24, 128, HALF))
    dscr("wdt_scr", (8, 128, FI))
    dscr("dbg", (128, 16384))
    dscr("ag_s_in", (128, DIN))
    dscr("ag_s_out", (2, 128, DIN))
    dscr("ag_h_in", (128, 54))
    dscr("ag_h_out", (2, 128, 54))
    dscr("ag_x_out", (2, NLT, 128, 4096))
    dscr("ag_a_out", (2, 4, 8, 128, 512))
    dscr("ar_in", (1, 4))
    dscr("ar_out", (1, 4))

    NA, NB, NC, ND, NE = 16384, 12544, 8192, 8192, 4864
    import contextlib
    with contextlib.ExitStack() as stack:
        ent = stack.enter_context
        t_a = ent(nc.sbuf_tensor("AR_A", [128, NA], F32))
        t_b = ent(nc.sbuf_tensor("AR_B", [128, NB], F32))
        t_c = ent(nc.sbuf_tensor("AR_C", [128, NC], F32))
        t_d = ent(nc.sbuf_tensor("AR_D", [128, ND], F32))
        t_e = ent(nc.sbuf_tensor("AR_E", [128, NE], F32))
        psl = [ent(nc.psum_tensor(f"psb{i}", [128, 512], F32)) for i in range(8)]
        ps0, ps1, ps2, ps3, ps4, ps5, ps6, ps7 = psl
        sm_t = ent(nc.semaphore("sm_t"))
        sm_v = ent(nc.semaphore("sm_v"))
        sm_s = ent(nc.semaphore("sm_s"))
        sm_g = ent(nc.semaphore("sm_g"))
        sm_dy = ent(nc.semaphore("sm_dy"))
        sm_ds = ent(nc.semaphore("sm_ds"))
        sm_dg = ent(nc.semaphore("sm_dg"))
        sm_cc = ent(nc.semaphore("sm_cc"))
        dsem_pool = [ent(nc.semaphore(f"dm{i}")) for i in range(72)]
        block = ent(nc.Block())
        p.sems = dict(t=sm_t, v=sm_v, s=sm_s, g=sm_g, dy=sm_dy, ds=sm_ds, dg=sm_dg, cc=sm_cc)
        p.dsem_pool = dsem_pool
        k = K(nc, p, dram, [ps0, ps1, ps2, ps3, ps4, ps5, ps6, ps7])
        k.A, k.Bm, k.C, k.Dm, k.E = (t.ap() for t in (t_a, t_b, t_c, t_d, t_e))

        cvals = (0.0, 1.0, 1e-5)
        for i, cv in enumerate(cvals):
            capn = k.E[:, NE - 3 + i:NE - 2 + i]
            nc.const_aps.register(F32, cv, capn)
            k.op("v", lambda e, a=capn, v=cv: e.memset(a, v))
        p.barrier()
        setup_consts(k)
        mamba_phase(k)
        p.barrier()
        for b in range(8):
            k.bank_clear(b)
        l0ffn_phase(k)
        p.barrier()
        for b in range(8):
            k.bank_clear(b)
        attn_phase(k)
        p.barrier()
        for b in range(8):
            k.bank_clear(b)
        l1ffn_phase(k)
        p.barrier()

        @block.tensor
        def _(e):
            for fn in p.q["t"]:
                fn(e)

        @block.vector
        def _(e):
            for fn in p.q["v"]:
                fn(e)

        @block.scalar
        def _(e):
            for fn in p.q["s"]:
                fn(e)

        @block.gpsimd
        def _(e):
            for fn in p.q["g"]:
                fn(e)

        @block.sync
        def _(e):
            for fn in p.q["y"]:
                fn(e)

    return nc


def setup_consts(k):
    E, dr = k.E, k.dr
    off = 0
    toks = {}

    def ld(name, n, src=None, parts=None, dtype=F32):
        nonlocal off
        ap = E[:, off:off + n]
        if dtype == BF16:
            ap = ap.bitcast(BF16)[:, 0:2 * n]
        if parts is not None:
            ap = ap[0:parts]
        k.sb[name] = ap
        toks[name] = k.dma("y", ap, src if src is not None else dr[name], slot="cst")
        off += n

    ld("ident_f", 128)
    ld("tri_f", 128)
    for nm in ("ident_b", "mask_b"):
        stg32 = E[:, off:off + 128]
        tld = k.dma("y", stg32, dr[nm], slot="cst")
        off += 128
        bap = E[:, off:off + 64].bitcast(BF16)[:, 0:128]
        k.sb[nm] = bap
        toks[nm] = k.op("v", lambda e, a=bap, s=stg32: e.tensor_copy(a, s), deps=(tld,))
        off += 64
    ld("ones_row_f", 128, parts=1)
    ld("is_odd", 1)
    k.indoff = off
    ld("ind_hp", 2048, parts=32)
    ld("d_exp", 16)
    ld("normw_col", 16)
    ld("dt_bias", 1, parts=32)
    ld("a_log", 1, parts=32)
    k.sb["conv_w"] = E[:, off:off + 72].rearrange("p (c t) -> p c t", c=18)
    toks["conv_w"] = k.dma("y", E[:, off:off + 72], dr["conv_w"], slot="cst")
    off += 72
    k.sb["conv_b"] = E[:, off:off + 18]
    toks["conv_b"] = k.dma("y", E[:, off:off + 18], dr["conv_b"], slot="cst")
    off += 18
    ld("b1_col", 32)
    ld("qp_col", 32)
    ld("qr_col", 32)
    ld("qi_col", 32)
    ld("b2_col", 8)
    k.sb["ones_col"] = E[:, off:off + 1]
    t_ones = k.op("v", lambda e: e.memset(k.sb["ones_col"], 1.0))
    off += 1
    k.sb["A_neg"] = E[0:32, off:off + 1]
    off += 1
    t1 = k.op("s", lambda e: e.activation(out=k.sb["A_neg"], in_=k.sb["a_log"], func=AF.Exp),
              deps=(toks["a_log"],))
    t1 = k.op("v", lambda e: e.tensor_scalar(k.sb["A_neg"], k.sb["A_neg"], -1.0, None,
                                             ALU.mult), deps=(t1,))
    thp = E[:, off:off + 32]; off += 32
    alpha = E[:, off:off + 32]; off += 32
    w1_ = E[:, off:off + 32]; off += 32
    w2_ = E[:, off:off + 32]; off += 32
    w3f = E[:, off:off + 32]; off += 32
    w3_ = w3f.bitcast(I32)[:, 0:32]
    k.sb["thp_col"], k.sb["alpha_col"] = thp, alpha

    def rr_sin(dst, src_ap, phase, dep):
        t = k.op("v", lambda e: e.tensor_scalar(w1_, src_ap, INV_2PI, phase * INV_2PI,
                                                ALU.mult, ALU.add), deps=(dep,))
        t = k.op("v", lambda e: e.tensor_scalar(w3f, w1_, MAGIC, MAGIC, ALU.add,
                                                ALU.subtract), deps=(t,))
        t = k.op("v", lambda e: e.tensor_copy(w3_, w3f), deps=(t,))
        t = k.op("v", lambda e: e.scalar_tensor_tensor(w2_, w3_, -1.0, w1_, ALU.mult,
                                                       ALU.add), deps=(t,))
        t = k.op("v", lambda e: e.tensor_scalar(w2_, w2_, 0.4999999, -0.4999999,
                                                ALU.min, ALU.max), deps=(t,))
        return k.op("s", lambda e, d=dst: e.activation(out=d, in_=w2_, func=AF.Sin,
                                                       scale=TWO_PI), deps=(t,))

    t = rr_sin(thp, k.sb["qp_col"], 0.0, toks["qp_col"])
    t = k.op("v", lambda e: e.tensor_tensor(alpha, thp, k.sb["qi_col"], ALU.mult),
             deps=(t, toks["qi_col"]))
    t2 = rr_sin(thp, k.sb["qp_col"], float(np.pi / 2), t)
    t2 = k.op("v", lambda e: e.tensor_tensor(w2_, thp, k.sb["qr_col"], ALU.mult),
              deps=(t2, toks["qr_col"]))
    t2 = k.op("v", lambda e: e.tensor_tensor(alpha, alpha, w2_, ALU.add), deps=(t2,))
    t2 = k.op("v", lambda e: e.scalar_tensor_tensor(alpha, alpha, 0.1, k.sb["qi_col"],
                                                    ALU.mult, ALU.mult), deps=(t2,))
    t3 = k.op("v", lambda e: e.scalar_tensor_tensor(thp, k.sb["b1_col"], 0.1, k.sb["qp_col"],
                                                    ALU.mult, ALU.add),
              deps=(t2, toks["b1_col"]))
    t3 = k.op("v", lambda e: e.tensor_scalar(thp, thp, INV_2PI, None, ALU.mult), deps=(t3,))
    k.consts_tok = tuple(toks.values()) + (t_ones, t1, t3)
    k.eoff = off


def mamba_phase(k):
    p, dr, ps = k.p, k.dr, k.psum
    A, Bm, C, Dm, E = k.A, k.Bm, k.C, k.Dm, k.E
    ones_row = k.sb["ones_row_f"]
    ones_col = k.sb["ones_col"]
    consts = k.consts_tok

    uT = A.rearrange("p (c l) -> p c l", c=8)
    utok = [Bm[:, i * 1024:(i + 1) * 1024] for i in range(2)]
    ld = [None, None]
    ev_last = {}
    for i in range(16):
        ld[i % 2] = k.dma("y", utok[i % 2], dr["u_own"][i * 128:(i + 1) * 128, :],
                          deps=consts + (ev_last.get(i - 2),), slot=f"u{i % 2}")
        ev_last[i] = k.tr_tiles(
            [(utok[i % 2][:, j * 128:(j + 1) * 128], uT[:, j, i * 128:(i + 1) * 128], None)
             for j in range(8)], (0, 1), deps=(ld[i % 2],))
    u_done = ev_last[15]

    wsl = [Dm[:, i * 1024:(i + 1) * 1024].rearrange("p (c m) -> p c m", c=8) for i in range(2)]
    stg = [Bm[:, 2048 + i * 512:2048 + (i + 1) * 512] for i in range(4)]
    halo = Bm[:, 12294:12348]
    wtok = [None, None]
    wuse = [None, None]
    ev_tok = {}
    st_tok = {}
    si = 0
    for r in range(35):
        wtok[r % 2] = k.dma("y", Dm[:, (r % 2) * 1024:(r % 2) * 1024 + 1024],
                            dr["w_in_t"][r], deps=(u_done, wuse[r % 2]),
                            slot=f"win{r % 2}")
        for lb in range(4):
            b = 2 + (si % 2)
            bk = ps[b]
            for kc in range(7):
                k.mmq(b, bk[:], wsl[r % 2][:, kc, :], uT[:, kc, lb * 512:(lb + 1) * 512],
                      kc == 0, False, deps=(wtok[r % 2],))
            mt = k.mm(b, bk[:], wsl[r % 2][:, 7, :], uT[:, 7, lb * 512:(lb + 1) * 512],
                      False, True, deps=(wtok[r % 2],))
            if lb == 3:
                wuse[r % 2] = mt
            s_ = stg[si % 4]
            t = k.rd(b, "v", lambda e, bb=bk, d=s_: e.tensor_copy(d, bb[:]),
                     deps=(mt, st_tok.get(si - 4)))
            if lb == 3 and 16 <= r < 34:
                t = k.op("v", lambda e, s2=s_, c=r - 16: e.tensor_copy(
                    halo[:, c * 3:c * 3 + 3], s2[:, 509:512]), deps=(t,))
            ev_tok[si] = t
            st_tok[si] = k.dma("s", dr["zxb"][r * 128:(r + 1) * 128, lb * 512:(lb + 1) * 512],
                               s_, deps=(t,), slot=f"stg{si % 4}")
            si += 1
    zx_done = tuple(st_tok[si - 1 - i] for i in range(4))

    hst = k.dma("g", dr["ag_h_in"], halo, deps=(ev_tok[si - 1],), slot="halo")
    halo_cc = k.coll("AllGather", ALU.bypass, PAIRS, dr["ag_h_in"], dr["ag_h_out"],
                     deps=(hst,))
    halo_rx = Bm[:, 12352:12406]
    t = k.dma("g", halo_rx, dr["ag_h_out"][0], deps=(halo_cc,), slot="halo")
    hmask = k.op("v", lambda e: e.tensor_scalar(halo_rx, halo_rx, k.sb["is_odd"], None,
                                                ALU.mult), deps=(t,))

    C_res = Dm[:, 0:2048]
    Acum_res = Dm[0:32, 2048:4096]
    S = Dm[:, 4096:6144]
    dtw = [Dm[0:32, 6144 + i * 512:6144 + (i + 1) * 512] for i in range(2)]
    CA_store = Dm[0:32, 7168:7184]
    CA_run = Dm[0:32, 7184:7185]
    lam_sb = Dm[0:1, 7186:7218]
    LamRep = Dm[:, 7220:7252]
    Btok = Dm[:, 7252:7764].rearrange("p (c n) -> p c n", c=4)
    convin = [Bm[:, i * 515:(i + 1) * 515] for i in range(2)]
    xconv = Bm[:, 1030:9222].rearrange("p (c l) -> p c l", c=16)
    acrow = [Bm[0:1, 9222 + i * 1024:9222 + (i + 1) * 1024] for i in range(2)]
    Bconv = Bm[:, 11270:11782]
    Cconv = Bm[:, 11782:12294]
    eo = k.eoff
    toks_ = E[:, eo:eo + 384].rearrange("p (c h) -> p c h", c=12)
    dt_tok = lambda cc: toks_[:, cc, :]
    decay_tok = lambda cc: toks_[:, 4 + cc, :]
    acum_tok = lambda cc: toks_[:, 8 + cc, :]
    Gm_t = E[:, eo + 384:eo + 512]
    L_t = E[:, eo + 512:eo + 640]
    GL_t = E[:, eo + 640:eo + 768]
    expA = E[0:32, eo + 768:eo + 896]
    eaF = E[:, eo + 896:eo + 1024]
    gptmp = E[:, eo + 1024:eo + 1536]
    L_t2 = Dm[:, 7892:8020]
    GL_t2 = Dm[:, 8020:8148]
    eaF_b = Bm[:, 12406:12534]
    xtok = C.rearrange("p (c q) -> p c q", c=4)

    s_zero = k.op("v", lambda e: e.memset(S, 0.0))
    CA_z = k.op("v", lambda e: e.memset(CA_run, 0.0))
    ones32 = Dm[0:32, 7764:7892]
    t_o32 = k.op("v", lambda e: e.memset(ones32, 1.0))
    prev = {}
    ydma = {}
    supd = {}
    prev["car"] = CA_z

    for lt in range(NLT):
        conv_done = []
        for ch in range(18):
            ci = convin[ch % 2]
            row0 = 2048 + ch * 128
            if lt == 0:
                t = k.dma("y", ci[:, 3:515], dr["zxb"][row0:row0 + 128, 0:512],
                          deps=zx_done + (prev.get(("cvslot", ch % 2)),),
                          slot=f"cv{ch % 2}")
                t = k.op("v", lambda e, c=ci, h=ch: e.tensor_copy(
                    c[:, 0:3], halo_rx[:, h * 3:h * 3 + 3]), deps=(hmask, t))
            else:
                t = k.dma("y", ci, dr["zxb"][row0:row0 + 128, lt * 512 - 3:(lt + 1) * 512],
                          deps=zx_done + (prev.get(("cvslot", ch % 2)),),
                          slot=f"cv{ch % 2}")
            dst = xconv[:, ch, :] if ch < 16 else (Bconv if ch == 16 else Cconv)
            dep_dst = ydma.get((lt - 1, 3)) if ch < 16 else prev.get(("bcfree", ch))
            t0 = k.op("v", lambda e, c=ci, d=dst, h=ch: e.tensor_scalar(
                d, c[:, 0:512], k.sb["conv_w"][:, h, 0:1], None, ALU.mult),
                deps=(t, dep_dst))
            for tp in range(1, 4):
                t0 = k.op("v", lambda e, c=ci, d=dst, h=ch, tp_=tp:
                          e.scalar_tensor_tensor(d, c[:, tp_:tp_ + 512],
                                                 k.sb["conv_w"][:, h, tp_:tp_ + 1], d,
                                                 ALU.mult, ALU.add), deps=(t0,))
            prev[("cvslot", ch % 2)] = t0
            t1 = k.op("s", lambda e, d=dst, h=ch: e.activation(
                out=d, in_=d, func=AF.Silu, bias=k.sb["conv_b"][:, h:h + 1], scale=1.0),
                deps=(t0,))
            conv_done.append(t1)
        ccp = k.op("v", lambda e, l=lt: e.tensor_copy(C_res[:, l * 512:(l + 1) * 512], Cconv),
                   deps=(conv_done[17],))
        prev[("bcfree", 17)] = ccp

        t = k.dma("y", dtw[0], dr["zxb"][4352:4384, lt * 512:(lt + 1) * 512],
                  deps=zx_done + (prev.get("dtwfree"),), slot="dt")
        t = k.op("v", lambda e: e.tensor_scalar(dtw[0], dtw[0], k.sb["dt_bias"], None,
                                                ALU.add), deps=(t,))
        t = k.op("s", lambda e: e.activation(out=dtw[1], in_=dtw[0], func=AF.Exp),
                 deps=(t, prev.get("dtw1free")))
        tdt = k.op("s", lambda e: e.activation(out=dtw[0], in_=dtw[1], func=AF.Ln, bias=1.0),
                   deps=(t,))
        dtt = k.tr_tiles([(dtw[0][:, cc * 128:(cc + 1) * 128], dt_tok(cc), tdt)
                          for cc in range(4)], (0, 1))
        adt = k.op("v", lambda e: e.tensor_scalar(dtw[0], dtw[0], k.sb["A_neg"], None,
                                                  ALU.mult), deps=(dtt,))
        sc_toks = []
        for cc in range(4):
            z = lt * 4 + cc
            ac = Acum_res[:, lt * 512 + cc * 128:lt * 512 + (cc + 1) * 128]
            sc = k.op("v", lambda e, a=ac, c=cc: e.tensor_tensor_scan(
                a, ones32, dtw[0][:, c * 128:(c + 1) * 128], 0.0, ALU.mult, ALU.add),
                deps=(adt, t_o32, prev.get(("acfree", z - 16))))
            t2 = k.op("v", lambda e, zz=z: e.tensor_copy(CA_store[:, zz:zz + 1], CA_run),
                      deps=(sc, prev.get("car")))
            t2 = k.op("v", lambda e, a=ac: e.tensor_tensor(CA_run, CA_run, a[:, 127:128],
                                                           ALU.add), deps=(t2,))
            prev["car"] = t2
            dd = k.op("v", lambda e, a=ac, c=cc: e.scalar_tensor_tensor(
                dtw[1][:, c * 128:(c + 1) * 128], a, -1.0,
                a[:, 127:128].to_broadcast((32, 128)), ALU.mult, ALU.add), deps=(sc,))
            dd = k.op("s", lambda e, c=cc: e.activation(
                out=dtw[1][:, c * 128:(c + 1) * 128], in_=dtw[1][:, c * 128:(c + 1) * 128],
                func=AF.Exp), deps=(dd,))
            sc_toks.append((sc, dd))
        att = k.tr_tiles([(Acum_res[:, lt * 512 + cc * 128:lt * 512 + (cc + 1) * 128],
                           acum_tok(cc), sc_toks[cc][0]) for cc in range(4)], (0, 1))
        dct = k.tr_tiles([(dtw[1][:, cc * 128:(cc + 1) * 128], decay_tok(cc),
                           sc_toks[cc][1]) for cc in range(4)], (0, 1))
        prev["dtwfree"] = att
        prev["dtw1free"] = dct

        xt_last = None
        for ch in range(16):
            xt_last = k.tr_tiles(
                [(xconv[:, ch, cc * 128:(cc + 1) * 128],
                  xtok[:, cc, ch * 128:(ch + 1) * 128], conv_done[ch]) for cc in range(4)],
                (0, 1), deps=(prev.get(("xtokfree", lt - 1)),))
        xs_tok = {}
        for cc in range(4):
            xs_tok[cc] = k.op("v", lambda e, c=cc: e.tensor_tensor(
                xtok[:, c].rearrange("p (h q) -> p h q", h=32),
                xtok[:, c].rearrange("p (h q) -> p h q", h=32),
                dt_tok(c).unsqueeze(2).to_broadcast((128, 32, 64)), ALU.mult),
                deps=(xt_last, dtt))
        btt = k.tr_tiles([(Bconv[:, cc * 128:(cc + 1) * 128], Btok[:, cc], conv_done[16])
                          for cc in range(4)], (0, 1))
        prev[("bcfree", 16)] = btt

        for cc in range(4):
            z = lt * 4 + cc
            ac = Acum_res[:, lt * 512 + cc * 128:lt * 512 + (cc + 1) * 128]
            t = k.op("t", lambda e, a=ac: e.transpose(ps[7][0:1, 256:288], a[:, 127:128],
                                                      k.sb["ident_f"][0:32, 0:32]),
                     deps=(sc_toks[cc][0],) + k.bank_deps(7))
            k.bank_clear(7)
            t = k.rd(7, "v", lambda e: e.tensor_copy(lam_sb, ps[7][0:1, 256:288]),
                     deps=(t, prev.get("lamfree")))
            t = k.op("s", lambda e: e.activation(out=lam_sb, in_=lam_sb, func=AF.Exp),
                     deps=(t,))
            t = k.op("t", lambda e: e.matmul(ps[7][:, 288:320], ones_row, lam_sb,
                                             start=True, stop=True, skip_group_check=True),
                     deps=(t,) + k.bank_deps(7))
            k.bank_clear(7)
            lamrep = k.rd(7, "v", lambda e: e.tensor_copy(LamRep, ps[7][:, 288:320]),
                          deps=(t, prev.get("lamrepfree")))
            prev["lamfree"] = lamrep
            g = k.op("t", lambda e, c=cc, l=lt: e.matmul(
                ps[7][:, 0:128], Bconv[:, c * 128:(c + 1) * 128],
                C_res[:, l * 512 + c * 128:l * 512 + (c + 1) * 128], start=True, stop=True,
                skip_group_check=True),
                deps=(conv_done[16], ccp) + k.bank_deps(7))
            k.bank_clear(7)
            gev = k.rd(7, "v", lambda e: e.tensor_tensor(Gm_t, ps[7][:, 0:128],
                                                         k.sb["tri_f"], ALU.mult),
                       deps=(g, prev.get("gmfree")))
            eat = k.op("s", lambda e, a=ac: e.activation(out=expA, in_=a, func=AF.Exp),
                       deps=(sc_toks[cc][0], prev.get("expAfree")))
            ac_t = {}
            ev_j = {}
            gl_last = None
            for j in range(16):
                if j % 4 == 0:
                    hg2 = j // 4
                    ac_t[hg2] = k.dma(
                        "g", acrow[hg2 % 2],
                        Acum_res[hg2 * 8:hg2 * 8 + 8,
                                 lt * 512 + cc * 128:lt * 512 + (cc + 1) * 128],
                        deps=(sc_toks[cc][0], prev.get(("acrowuse", hg2 % 2))),
                        slot=f"acr{hg2 % 2}")
                b = 2 + (j % 4)
                bank = ps[b]
                sl = slice((j // 4) * 128, (j // 4) * 128 + 128)
                yo = k.mm(b, bank[:, sl], S[:, j * 128:(j + 1) * 128],
                          C_res[:, lt * 512 + cc * 128:lt * 512 + (cc + 1) * 128],
                          j < 4, False,
                          deps=(ccp, s_zero, supd.get((z - 1, "done")),), skip=True)
                pj = 0
                ea_ps = ps[7]
                ea_bk = 7
                ea_reg = slice(128, 256)
                ea_buf = eaF
                ef = k.op("t", lambda e, jj=j, r=ea_reg, p_=ea_ps: e.matmul(
                    p_[:, r], k.sb["ind_hp"][:, jj * 128:(jj + 1) * 128], expA,
                    start=True, stop=True, skip_group_check=True),
                    deps=(eat, prev.get(("eaRfree", pj))) +
                    (k.bank_deps(7) if j == 0 else ()))
                efe = k.rd(ea_bk, "v", lambda e, r=ea_reg, eb=ea_buf, p_=ea_ps:
                           e.tensor_copy(eb, p_[:, r]),
                           deps=(ef, prev.get(("eaFfree", pj))))
                prev[("eaRfree", pj)] = efe
                fm = k.rd(b, "v", lambda e, bb=bank, s_=sl, eb=ea_buf: e.tensor_tensor(
                    bb[:, s_], bb[:, s_], eb, ALU.mult), deps=(yo, efe))
                prev[("eaFfree", pj)] = fm
                for hh in range(2):
                    h = 2 * j + hh
                    hg, hl = divmod(h, 8)
                    ph = h % 2
                    Lb = L_t if ph == 0 else L_t2
                    GLb = GL_t if ph == 0 else GL_t2
                    t6 = slice(0, 128) if ph == 0 else slice(256, 384)
                    tb = k.op("t", lambda e, a=acrow[hg % 2], hl_=hl, r=t6: e.matmul(
                        ps[6][:, r], ones_row, a[:, hl_ * 128:(hl_ + 1) * 128],
                        start=True, stop=True, skip_group_check=True),
                        deps=(ac_t[hg], prev.get(("t6free", ph))) +
                        (k.bank_deps(6) if j == 0 and hh == 0 else ()))
                    prev[("acrowuse", hg % 2)] = tb
                    lv = k.rd(6, "s", lambda e, c=cc, h_=h, r=t6, L_=Lb: e.activation(
                        out=L_, in_=ps[6][:, r], func=AF.Relu, scale=-1.0,
                        bias=acum_tok(c)[:, h_:h_ + 1]),
                        deps=(tb, att, prev.get(("ltfree", ph))))
                    prev[("t6free", ph)] = lv
                    lv = k.op("s", lambda e, L_=Lb: e.activation(out=L_, in_=L_,
                                                                 func=AF.Exp,
                                                                 scale=-1.0), deps=(lv,))
                    gl = k.op("v", lambda e, L_=Lb, G_=GLb: e.tensor_tensor(
                        G_, Gm_t, L_, ALU.mult),
                        deps=(lv, gev, prev.get(("glfree", ph))))
                    prev[("ltfree", ph)] = gl
                    yd = k.op("t", lambda e, bb=bank, s_=sl, c=cc, h_=h, pr=hh,
                              lst=(hh == 1), G_=GLb: e.matmul(
                                  bb[pr * 64:pr * 64 + 64, s_],
                                  xtok[:, c, h_ * 64:(h_ + 1) * 64], G_,
                                  start=False, stop=lst, skip_group_check=True),
                              deps=(fm, gl, xs_tok[cc]))
                    prev[("glfree", ph)] = yd
                    gl_last = yd
                ev_j[j] = k.rd(b, "v", lambda e, bb=bank, s_=sl, jj=j, c=cc:
                               e.scalar_tensor_tensor(
                                   xconv[:, jj, c * 128:(c + 1) * 128],
                                   xconv[:, jj, c * 128:(c + 1) * 128],
                                   k.sb["d_exp"][:, jj:jj + 1], bb[:, s_], ALU.mult,
                                   ALU.add), deps=(gl_last,))
            prev["gmfree"] = ev_j[15]
            prev["expAfree"] = ev_j[15]
            for j in range(16):
                prev[("evj", cc, j)] = ev_j[j]
            if cc == 3:
                ydma[(lt, 3)] = k.dma("s", dr["ystage"][lt], Bm[:, 1030:9222], slot="yst",
                                      deps=tuple(prev[("evj", cc2, j)]
                                                 for cc2 in range(4) for j in range(16)))
            xw = k.op("v", lambda e, c=cc: e.tensor_tensor(
                xtok[:, c].rearrange("p (h q) -> p h q", h=32),
                xtok[:, c].rearrange("p (h q) -> p h q", h=32),
                decay_tok(c).unsqueeze(2).to_broadcast((128, 32, 64)), ALU.mult),
                deps=(dct, gl_last))
            tsc = k.op("v", lambda e: e.tensor_tensor(
                S.rearrange("p (h q) -> p h q", h=32),
                S.rearrange("p (h q) -> p h q", h=32),
                LamRep[:, 0:32].unsqueeze(2).to_broadcast((128, 32, 64)), ALU.mult),
                deps=(lamrep, ev_j[15]))
            prev["lamrepfree"] = tsc
            sadd = None
            for sb2 in range(4):
                b = sb2 % 2
                st = k.mm(b, ps[b][:], Btok[:, cc], xtok[:, cc, sb2 * 512:(sb2 + 1) * 512],
                          True, True, deps=(btt, xw))
                sadd = k.rd(b, "v", lambda e, bb=ps[b], s2=sb2: e.tensor_tensor(
                    S[:, s2 * 512:(s2 + 1) * 512], S[:, s2 * 512:(s2 + 1) * 512], bb[:],
                    ALU.add), deps=(st, tsc))
                supd[(z, sb2)] = sadd
            supd[(z, "done")] = sadd
            prev[("acfree", z)] = sadd
        prev[("xtokfree", lt)] = supd[(lt * 4 + 3, "done")]

    t = k.dma("g", dr["ag_s_in"], S, deps=(supd[(15, "done")],), slot="sst")
    scc = k.coll("AllGather", ALU.bypass, PAIRS, dr["ag_s_in"], dr["ag_s_out"], deps=(t,))
    t = k.dma("g", S, dr["ag_s_out"][0], deps=(scc,), slot="sst")
    sprev_tok = k.op("v", lambda e: e.tensor_scalar(S, S, k.sb["is_odd"], None, ALU.mult),
                     deps=(t,))

    Yt = Bm[:, 0:8192].rearrange("p (c l) -> p c l", c=16)
    zring = Bm[:, 8192:8704]
    sqring = Bm[:, 8704:9216]
    wout = [Bm[:, 9216 + i * 1024:9216 + (i + 1) * 1024] for i in range(2)]
    tmp_row = Bm[0:1, 11264:11776]
    m_row = Dm[0:1, 6144:6656]
    q_row = Dm[0:1, 6656:7168]
    r_row = Dm[0:1, 7252:7764]
    x1 = uT
    k.ln_rows = (m_row, q_row, r_row, tmp_row)

    for lt in range(NLT):
        yld0 = k.dma("y", Bm[:, 0:8192], dr["ystage"][lt], slot="yld",
                     deps=(ydma[(lt, 3)], ydma[(3, 3)], supd[(15, "done")], sprev_tok) +
                     tuple(prev.get(("yfree", lt - 1, jj)) for jj in range(16)))
        yld = {jj: yld0 for jj in range(16)}
        for cc in range(4):
            z = lt * 4 + cc
            ac = Acum_res[:, lt * 512 + cc * 128:lt * 512 + (cc + 1) * 128]
            eag = k.op("s", lambda e, a=ac, zz=z: e.activation(
                out=expA, in_=a, func=AF.Exp, bias=CA_store[:, zz:zz + 1]),
                deps=(prev.get("expAfree"), sprev_tok))
            for j in range(16):
                b = 2 + (j % 4)
                bank = ps[b]
                sl = slice((j // 4) * 128, (j // 4) * 128 + 128)
                cm = k.mm(b, bank[:, sl], S[:, j * 128:(j + 1) * 128],
                          C_res[:, lt * 512 + cc * 128:lt * 512 + (cc + 1) * 128],
                          j < 4, j >= 12, deps=(sprev_tok,), skip=True)
                pj = 0
                ea_ps = ps[7]
                ea_bk = 7
                ea_reg = slice(128, 256)
                ea_buf = eaF
                ef = k.op("t", lambda e, jj=j, r=ea_reg, p_=ea_ps: e.matmul(
                    p_[:, r], k.sb["ind_hp"][:, jj * 128:(jj + 1) * 128], expA,
                    start=True, stop=True, skip_group_check=True),
                    deps=(eag, prev.get(("eaRfree", pj))) +
                    (k.bank_deps(7) if j == 0 else ()))
                efe = k.rd(ea_bk, "v", lambda e, r=ea_reg, eb=ea_buf, p_=ea_ps:
                           e.tensor_copy(eb, p_[:, r]),
                           deps=(ef, prev.get(("eaFfree", pj))))
                prev[("eaRfree", pj)] = efe
                fm = k.rd(b, "v", lambda e, bb=bank, s_=sl, eb=ea_buf: e.tensor_tensor(
                    bb[:, s_], bb[:, s_], eb, ALU.mult), deps=(cm, efe))
                prev[("eaFfree", pj)] = fm
                t = k.rd(b, "v", lambda e, bb=bank, s_=sl, jj=j, c=cc: e.tensor_tensor(
                    Yt[:, jj, c * 128:(c + 1) * 128], Yt[:, jj, c * 128:(c + 1) * 128],
                    bb[:, s_], ALU.add), deps=(fm, yld[j]))
                prev[("corr", z, j)] = t
            prev["expAfree"] = prev[("corr", z, 15)]
        corr_done = [prev[("corr", lt * 4 + 3, j)] for j in range(16)]
        gat = {}
        mmsq = None
        for jj in range(16):
            t = k.dma("y", zring, dr["zxb"][jj * 128:(jj + 1) * 128,
                                            lt * 512:(lt + 1) * 512],
                      deps=zx_done + (gat.get(jj - 1),), slot="zld")
            t = k.op("s", lambda e: e.activation(out=zring, in_=zring, func=AF.Silu),
                     deps=(t,))
            gat[jj] = k.op("v", lambda e, j_=jj: e.tensor_tensor(Yt[:, j_], Yt[:, j_],
                                                                 zring, ALU.mult),
                           deps=(t, corr_done[jj]))
            sq = k.op("s", lambda e, j_=jj: e.activation(out=sqring, in_=Yt[:, j_],
                                                         func=AF.Square),
                      deps=(gat[jj], mmsq))
            mmsq = k.mm(6, ps[6][0:1, 0:512], ones_col, sqring, jj == 0, jj == 15,
                        deps=(sq,), skip=True)
        t = k.rd(6, "v", lambda e: e.tensor_scalar(m_row, ps[6][0:1, 0:512], 1.0 / 2048.0,
                                                   None, ALU.mult),
                 deps=(mmsq, prev.get("mrowfree")))
        rst = k.newton_rsqrt(r_row, m_row, tmp_row, t, 1e-5)
        prev["mrowfree"] = rst
        rbc = k.mm(6, ps[6][:, 0:512], ones_row, r_row, True, True, deps=(rst,))
        app = {}
        for jj in range(16):
            app[jj] = k.rd(6, "v", lambda e, j_=jj: e.scalar_tensor_tensor(
                Yt[:, j_], Yt[:, j_], k.sb["normw_col"][:, j_:j_ + 1], ps[6][:, 0:512],
                ALU.mult, ALU.mult), deps=(rbc, gat[jj]))
        wo_tok = [None, None]
        wo_use = [None, None]
        opmm = None
        for ch in range(16):
            wo_tok[ch % 2] = k.dma("y", wout[ch % 2], dr["w_out_t"][ch],
                                   deps=(wo_use[ch % 2], supd[(15, "done")]),
                                   slot=f"wo{ch % 2}")
            for dc in range(7):
                k.mmq(dc, ps[dc][:], wout[ch % 2][:, dc * 128:(dc + 1) * 128], Yt[:, ch],
                      ch == 0, ch == 15,
                      deps=(wo_tok[ch % 2], app[ch]) +
                      ((app[15],) if ch == 0 and dc >= 6 else ()))
            wo_use[ch % 2] = k.mm(7, ps[7][:], wout[ch % 2][:, 7 * 128:8 * 128], Yt[:, ch],
                                  ch == 0, ch == 15,
                                  deps=(wo_tok[ch % 2], app[ch]) +
                                  ((app[15],) if ch == 0 else ()))
            if ch == 15:
                opmm = wo_use[ch % 2]
        for jj in range(16):
            prev[("yfree", lt, jj)] = opmm
        ln_feature_major(k, lambda dc, l=lt: x1[:, dc, l * 512:(l + 1) * 512],
                         lambda dc: ps[dc][:], opmm, lt, prev, "m3", gptmp)
    k.x1_ready = prev[("lnap", "m3", NLT - 1)]
    k.mprev = prev


def ln_feature_major(k, res_ap, psum_ap, mm_tok, lt, prev, tag, sqtile):
    ps = k.psum
    ones_row, ones_col = k.sb["ones_row_f"], k.sb["ones_col"]
    m_row, q_row, r_row, tmp_row = k.ln_rows
    radd = {}
    for dc in range(8):
        radd[dc] = k.rd(dc, "v", lambda e, d=dc: e.tensor_tensor(
            res_ap(d), res_ap(d), psum_ap(d), ALU.add), deps=(mm_tok,))
    smm = None
    qmm = None
    sqt = None
    for dc in range(8):
        smm = k.mm(6, ps[6][0:1, 0:512], ones_col, res_ap(dc), dc == 0, dc == 7,
                   deps=(radd[dc],), skip=True)
        sqt = k.op("s", lambda e, d=dc: e.activation(out=sqtile, in_=res_ap(d),
                                                     func=AF.Square), deps=(radd[dc], qmm))
        qmm = k.mm(7, ps[7][0:1, 0:512], ones_col, sqtile, dc == 0, dc == 7,
                   deps=(sqt,), skip=True)
    t = k.rd(6, "v", lambda e: e.tensor_scalar(m_row, ps[6][0:1, 0:512], 1.0 / 1024.0, None,
                                               ALU.mult), deps=(smm, prev.get("lnmfree")))
    t2 = k.rd(7, "v", lambda e: e.tensor_scalar(q_row, ps[7][0:1, 0:512], 1.0 / 1024.0, None,
                                                ALU.mult), deps=(qmm, prev.get("lnqfree")))
    t3 = k.op("v", lambda e: e.tensor_tensor(tmp_row, m_row, m_row, ALU.mult), deps=(t,))
    t3 = k.op("v", lambda e: e.tensor_tensor(q_row, q_row, tmp_row, ALU.subtract),
              deps=(t3, t2))
    rst = k.newton_rsqrt(r_row, q_row, tmp_row, t3, 1e-5)
    tb = k.mm(6, ps[6][:, 0:512], ones_row, m_row, True, True, deps=(rst,))
    tb2 = k.mm(7, ps[7][:, 0:512], ones_row, r_row, True, True, deps=(rst,))
    lnap = None
    for dc in range(8):
        t4 = k.rd(6, "v", lambda e, d=dc: e.tensor_tensor(
            res_ap(d), res_ap(d), ps[6][:, 0:512], ALU.subtract), deps=(tb, radd[dc]))
        lnap = k.rd(7, "v", lambda e, d=dc: e.tensor_tensor(
            res_ap(d), res_ap(d), ps[7][:, 0:512], ALU.mult), deps=(tb2, t4))
        prev[("lnapc", tag, lt, dc)] = lnap
    prev["lnmfree"] = lnap
    prev["lnqfree"] = lnap
    prev[("lnap", tag, lt)] = lnap
    return lnap


def l0ffn_phase(k):
    p, dr, ps = k.p, k.dr, k.psum
    A, Bm, C, Dm, E = k.A, k.Bm, k.C, k.Dm, k.E
    prev = dict(k.mprev) if hasattr(k, "mprev") else {}
    prev = {}
    x1 = A.rearrange("p (c l) -> p c l", c=8)
    w1sl = [Bm[:, i * 1024:(i + 1) * 1024].rearrange("p (c m) -> p c m", c=8)
            for i in range(2)]
    w2sl = [Bm[:, 2048 + i * 2048:2048 + (i + 1) * 2048].rearrange("p (c m) -> p c m", c=16)
            for i in range(2)]
    outacc = Bm[:, 6144:10240].rearrange("p (c l) -> p c l", c=8)
    vt = Bm[:, 10240:10752]
    st_ = Bm[:, 10752:11264]
    ktf = Bm[:, 11264:11776]
    kt = ktf.bitcast(I32)[:, 0:512]
    asx = Bm[:, 11776:12288]
    act = C.rearrange("p (c l) -> p c l", c=16)
    x2st = Dm[:, 0:4096].rearrange("p (c l) -> p c l", c=8)
    k.ln_rows = (Dm[0:1, 4096:4608], Dm[0:1, 4608:5120], Dm[0:1, 5120:5632],
                 Dm[0:1, 5632:6144])

    x2dma = {}
    for lt in range(NLT):
        for fh in range(2):
            actuse = prev.get(("actuse", fh))
            for ff in range(16):
                f = fh * 16 + ff
                w1t = k.dma("y", Bm[:, (ff % 2) * 1024:(ff % 2) * 1024 + 1024],
                            dr["w1"][f], deps=(k.x1_ready, prev.get(("w1use", ff % 2))),
                            slot=f"w1{ff % 2}")
                b = ff % 2
                for kc in range(7):
                    k.mmq(b, ps[b][:], w1sl[ff % 2][:, kc, :],
                          x1[:, kc, lt * 512:(lt + 1) * 512], kc == 0, False,
                          deps=(w1t,))
                hmm = k.mm(b, ps[b][:], w1sl[ff % 2][:, 7, :],
                           x1[:, 7, lt * 512:(lt + 1) * 512], False, True, deps=(w1t,))
                prev[("w1use", ff % 2)] = hmm
                t = k.rd(b, "v", lambda e, bb=ps[b], f_=f: e.tensor_scalar(
                    vt, bb[:], 0.1 * INV_2PI, k.sb["thp_col"][:, f_:f_ + 1], ALU.mult,
                    ALU.add), deps=(hmm, prev.get("vtfree")))
                t = k.op("v", lambda e: e.tensor_scalar(ktf, vt, MAGIC, MAGIC, ALU.add,
                                                        ALU.subtract),
                         deps=(t, prev.get("ktfree")))
                t = k.op("v", lambda e: e.tensor_copy(kt, ktf), deps=(t,))
                t = k.op("v", lambda e: e.scalar_tensor_tensor(st_, kt, -1.0, vt, ALU.mult,
                                                               ALU.add),
                         deps=(t, prev.get("stfree")))
                prev["ktfree"] = t
                t = k.op("s", lambda e: e.activation(out=st_, in_=st_, func=AF.Sin,
                                                     scale=TWO_PI), deps=(t,))
                t = k.op("v", lambda e, f_=f: e.tensor_scalar(
                    asx, st_, k.sb["alpha_col"][:, f_:f_ + 1], None, ALU.mult),
                    deps=(t, prev.get("asxfree")))
                t = k.rd(b, "v", lambda e, bb=ps[b]: e.scalar_tensor_tensor(
                    st_, bb[:], 1.0, asx, ALU.mult, ALU.add), deps=(t,))
                prev["vtfree"] = t
                prev["asxfree"] = t
                t = k.op("s", lambda e, f_=f, ff_=ff: e.activation(
                    out=act[:, ff_], in_=st_, func=AF.Relu,
                    bias=k.sb["b1_col"][:, f_:f_ + 1]), deps=(t, actuse))
                prev["stfree"] = t
                prev[("act", ff)] = t
            w2t = [None, None]
            w2use = [None, None]
            last_mm = None
            for dc in range(8):
                w2t[dc % 2] = k.dma("y", Bm[:, 2048 + (dc % 2) * 2048:
                                            2048 + (dc % 2) * 2048 + 2048],
                                    dr["w2"][fh][dc], deps=(w2use[dc % 2],),
                                    slot=f"w2{dc % 2}")
                b = 2 + (dc % 2)
                for ff in range(15):
                    k.mmq(b, ps[b][:], w2sl[dc % 2][:, ff, :], act[:, ff],
                          ff == 0, False, deps=(w2t[dc % 2], prev[("act", ff)]))
                last_mm = k.mm(b, ps[b][:], w2sl[dc % 2][:, 15, :], act[:, 15],
                               False, True, deps=(w2t[dc % 2], prev[("act", 15)]))
                w2use[dc % 2] = last_mm
                if fh == 0:
                    t = k.rd(b, "v", lambda e, bb=ps[b], d=dc: e.tensor_copy(
                        outacc[:, d], bb[:]), deps=(last_mm, prev.get(("oafree", dc))))
                else:
                    t = k.rd(b, "v", lambda e, bb=ps[b], d=dc: e.tensor_tensor(
                        outacc[:, d], outacc[:, d], bb[:], ALU.add),
                        deps=(last_mm, prev.get(("oafree", dc))))
                prev[("oacc", fh, dc)] = t
            prev[("actuse", fh)] = last_mm
        radd_dep = [prev[("oacc", 1, dc)] for dc in range(8)]
        for dc in range(8):
            t = k.op("v", lambda e, d=dc, l=lt: e.scalar_tensor_tensor(
                outacc[:, d], outacc[:, d], k.sb["b2_col"][:, d:d + 1],
                x1[:, d, l * 512:(l + 1) * 512], ALU.add, ALU.add), deps=(radd_dep[dc],))
            prev[("oafree", dc)] = t
            radd_dep[dc] = t
        lnin = ln_ffn(k, outacc, radd_dep, prev, lt)
        cps = []
        for dc in range(8):
            t = k.op("v", lambda e, d=dc, l=lt: e.tensor_copy(
                x1[:, d, l * 512:(l + 1) * 512], outacc[:, d]), deps=(lnin[dc],))
            cps.append(t)
            prev[("oafree", dc)] = t
        x2dma[lt] = k.dma("s", dr["x2own"][lt], Bm[:, 6144:10240], deps=tuple(cps), slot="x2st")
        for dc in range(8):
            prev[("oafree", dc)] = x2dma[lt]
    k.x2_ready = x2dma[NLT - 1]
    k.x2dma = x2dma


def ln_ffn(k, res, radd_dep, prev, lt, res_fn=None):
    if res_fn is None:
        res_fn = lambda dc: res[:, dc]
    ps = k.psum
    ones_row, ones_col = k.sb["ones_row_f"], k.sb["ones_col"]
    m_row, q_row, r_row, tmp_row = k.ln_rows
    sqtile = k.E[:, k.eoff + 1024:k.eoff + 1536]
    smm = None
    qmm = None
    for dc in range(8):
        smm = k.mm(6, ps[6][0:1, 0:512], ones_col, res_fn(dc), dc == 0, dc == 7,
                   deps=(radd_dep[dc],), skip=True)
        sqt = k.op("s", lambda e, d=dc: e.activation(out=sqtile, in_=res_fn(d),
                                                     func=AF.Square),
                   deps=(radd_dep[dc], qmm))
        qmm = k.mm(7, ps[7][0:1, 0:512], ones_col, sqtile, dc == 0, dc == 7,
                   deps=(sqt,), skip=True)
    t = k.rd(6, "v", lambda e: e.tensor_scalar(m_row, ps[6][0:1, 0:512], 1.0 / 1024.0, None,
                                               ALU.mult), deps=(smm, prev.get("lnmfree")))
    t2 = k.rd(7, "v", lambda e: e.tensor_scalar(q_row, ps[7][0:1, 0:512], 1.0 / 1024.0,
                                                None, ALU.mult),
              deps=(qmm, prev.get("lnqfree")))
    t3 = k.op("v", lambda e: e.tensor_tensor(tmp_row, m_row, m_row, ALU.mult), deps=(t,))
    t3 = k.op("v", lambda e: e.tensor_tensor(q_row, q_row, tmp_row, ALU.subtract),
              deps=(t3, t2))
    rst = k.newton_rsqrt(r_row, q_row, tmp_row, t3, 1e-5)
    tb = k.mm(6, ps[6][:, 0:512], ones_row, m_row, True, True, deps=(rst,))
    tb2 = k.mm(7, ps[7][:, 0:512], ones_row, r_row, True, True, deps=(rst,))
    outs = []
    for dc in range(8):
        t4 = k.rd(6, "v", lambda e, d=dc: e.tensor_tensor(
            res_fn(d), res_fn(d), ps[6][:, 0:512], ALU.subtract), deps=(tb, radd_dep[dc]))
        t5 = k.rd(7, "v", lambda e, d=dc: e.tensor_tensor(
            res_fn(d), res_fn(d), ps[7][:, 0:512], ALU.mult), deps=(tb2, t4))
        outs.append(t5)
    prev["lnmfree"] = outs[-1]
    prev["lnqfree"] = outs[-1]
    return outs


def attn_phase(k):
    p, dr, ps = k.p, k.dr, k.psum
    A, Bm, C, Dm, E = k.A, k.Bm, k.C, k.Dm, k.E
    prev = {}
    eo = k.eoff
    xcc = k.coll("AllGather", ALU.bypass, PAIRS, dr["x2own"], dr["ag_x_out"],
                 deps=(k.x2_ready,))
    xa = A.rearrange("p (b c l) -> p b c l", b=4, c=8)
    xc = C.rearrange("p (b c l) -> p b c l", b=2, c=8)
    xd = Dm.rearrange("p (b c l) -> p b c l", b=2, c=8)
    ldx = tuple(
        k.dma("y",
              (A[:, i * 4096:(i + 1) * 4096] if i < 4 else
               (C[:, (i - 4) * 4096:(i - 3) * 4096] if i < 6 else
                Dm[:, (i - 6) * 4096:(i - 5) * 4096])),
              dr["ag_x_out"][i // 4][i % 4], deps=(xcc,), slot=f"xf{i % 4}")
        for i in range(8))

    def x2ap(kc, blk):
        if blk < 4:
            return xa[:, blk, kc, :]
        if blk < 6:
            return xc[:, blk - 4, kc, :]
        return xd[:, blk - 6, kc, :]

    qt = Bm[:, 0:4096]
    ktile = Bm[:, 4096:8192]
    vtok = Bm[:, 8192:12352].rearrange("p (c m) -> p c m", c=32)
    io = k.indoff
    wslab = E[:, io:io + 1024].rearrange("p (c m) -> p c m", c=8)
    wslab_flat = E[:, io:io + 1024]
    Pt = [E[:, io + 1024:io + 1536], E[:, io + 1536:io + 2048]]
    den_c = E[:, eo:eo + 1]
    att_dma = {}

    vones = k.op("v", lambda e: e.memset(
        vtok[:, :, 64:65].rearrange("p c m -> p (c m)"), 1.0))
    vones2 = k.op("v", lambda e: e.memset(
        vtok[:, :, 129:130].rearrange("p c m -> p (c m)"), 1.0))

    for hp in range(4):
        vev = {}
        wld = k.dma("y", wslab_flat, dr["wv_t"][hp],
                    deps=ldx + (prev.get("wslab_use"),), slot="wsl")
        for blk in range(8):
            b = blk % 2
            for kc in range(7):
                k.mmq(b, ps[b][:], wslab[:, kc, :], x2ap(kc, blk), kc == 0, False,
                      deps=(wld,))
            mt = k.mm(b, ps[b][:], wslab[:, 7, :], x2ap(7, blk), False, True,
                      deps=(wld,))
            vev[blk] = k.rd(b, "v", lambda e, bb=ps[b], b2=blk: e.tensor_copy(
                qt[:, b2 * 512:(b2 + 1) * 512], bb[:]),
                deps=(mt, prev.get(("qfree", hp))))
            if blk == 7:
                prev["wslab_use"] = mt
        vtt = None
        for i in range(32):
            b = 2 + (i % 2)
            dep = (vev[i // 4],) + k.bank_deps(b) + (vones, vones2,
                                                     prev.get(("vtokfree", hp)))
            k.bank_clear(b)
            tt = k.op("t", lambda e, s=qt[:, i * 128:(i + 1) * 128], b3=ps[b]:
                      e.transpose(b3[:, 0:128], s, k.sb["ident_f"]), deps=dep)
            for hh in range(2):
                vtt = k.rd(b, "v", lambda e, b3=ps[b], i_=i, h_=hh: e.tensor_copy(
                    vtok[:, i_, h_ * 65:h_ * 65 + 64], b3[:, h_ * 64:h_ * 64 + 64]),
                    deps=(tt,))
        qev = {}
        kev = {}
        for (dst, wnm, evd) in ((qt, "wq_t", qev), (ktile, "wk_t", kev)):
            wld = k.dma("y", wslab_flat, dr[wnm][hp],
                        deps=((vtt,) if dst is qt else ()) + (prev["wslab_use"],),
                        slot="wsl")
            for blk in range(8):
                b = blk % 2
                for kc in range(7):
                    k.mmq(b, ps[b][:], wslab[:, kc, :], x2ap(kc, blk), kc == 0, False,
                          deps=(wld,) + ((vtt,) if dst is qt else
                                         (prev.get(("kfree", hp)),)))
                mt = k.mm(b, ps[b][:], wslab[:, 7, :], x2ap(7, blk), False, True,
                          deps=(wld,) + ((vtt,) if dst is qt else
                                         (prev.get(("kfree", hp)),)))
                evd[blk] = k.rd(b, "v", lambda e, bb=ps[b], b2=blk, d2=dst: e.tensor_copy(
                    d2[:, b2 * 512:(b2 + 1) * 512], bb[:]), deps=(mt,))
                if blk == 7:
                    prev["wslab_use"] = mt
        qk_done = tuple(qev.values()) + tuple(kev.values())

        for lb in range(8):
            nsc = lb * 4 + 4
            scb = [(0, 1), (2, 3)]
            pev = {}
            avlast = {0: None, 1: None}
            pmuse = {}
            for sc in range(nsc):
                for hh in range(2):
                    b = scb[sc % 2][hh]
                    diag = lb * 512 <= sc * 128 < (lb + 1) * 512
                    smm = k.mm(b, ps[b][:],
                               ktile[hh * 64:hh * 64 + 64, sc * 128:(sc + 1) * 128],
                               qt[hh * 64:hh * 64 + 64, lb * 512:(lb + 1) * 512],
                               True, True, deps=qk_done)
                    if diag:
                        off = sc * 128 - lb * 512
                        smm = k.op("t", lambda e, bb=ps[b], o=off: e.matmul(
                            bb[:, o:o + 128], k.sb["ident_b"], k.sb["mask_b"],
                            start=False, stop=True, skip_group_check=True), deps=())
                    pev[(sc, hh)] = k.rd(b, "s", lambda e, bb=ps[b], h_=hh: e.activation(
                        out=Pt[h_], in_=bb[:], func=AF.Exp, scale=0.125),
                        deps=(smm, pmuse.get((sc - 1, hh)),
                              (prev.get("last_att") if sc == 0 else None),
                              (prev.get(("ptuse", hh)) if sc == 0 else None)))
                for hh in range(2):
                    bav = 6 + hh
                    last = None
                    for j in range(4):
                        if lb * 4 + j < sc:
                            continue
                        last = k.mm(bav, ps[bav][:, j * 65:j * 65 + 65],
                                    Pt[hh][:, j * 128:(j + 1) * 128],
                                    vtok[:, sc, hh * 65:hh * 65 + 65],
                                    sc == 0 and j == 0, False,
                                    deps=(pev[(sc, hh)], vtt), skip=True)
                    pmuse[(sc, hh)] = last
                    avlast[hh] = last
            out_sl = Pt[0]
            outw = []
            for hh in range(2):
                bav = 6 + hh
                for j in range(4):
                    t = k.rd(bav, "v", lambda e, bb=ps[bav], j_=j: e.tensor_scalar(
                        den_c, bb[:, j_ * 65 + 64:j_ * 65 + 65], 1e-8, None, ALU.add),
                        deps=(avlast[hh], pev[(nsc - 1, 0)], pev[(nsc - 1, 1)],
                              prev.get("denfree")))
                    t = k.op("v", lambda e: e.reciprocal(den_c, den_c), deps=(t,))
                    t = k.rd(bav, "v", lambda e, bb=ps[bav], j_=j, h_=hh:
                             e.tensor_scalar(
                                 out_sl[:, j_ * 128 + h_ * 64:j_ * 128 + h_ * 64 + 64],
                                 bb[:, j_ * 65:j_ * 65 + 64], den_c, None, ALU.mult),
                             deps=(t,))
                    prev["denfree"] = t
                    outw.append(t)
            att_dma[(hp, lb)] = k.dma(
                "s", dr["attn_heads"][hp][lb], out_sl, deps=tuple(outw),
                slot=f"att{lb % 2}")
            prev[("ptuse", 0)] = avlast[0]
            prev[("ptuse", 1)] = avlast[1]
            prev["last_att"] = att_dma[(hp, lb)]
        prev[("qfree", hp)] = att_dma[(hp, 7)]
        prev[("kfree", hp)] = att_dma[(hp, 7)]
        prev[("vtokfree", hp)] = att_dma[(hp, 7)]

    attn_outproj(k, att_dma[(3, 7)])


def attn_outproj(k, attn_done):
    p, dr, ps = k.p, k.dr, k.psum
    A, Bm, C, Dm, E = k.A, k.Bm, k.C, k.Dm, k.E
    prev = {}
    acc = k.coll("AllGather", ALU.bypass, PAIRS, dr["attn_heads"], dr["ag_a_out"],
                 deps=(attn_done,))
    wo = Bm[:, 0:8192].rearrange("p (c m) -> p c m", c=8)
    attT = Bm[:, 8192:9216].rearrange("p (c m) -> p c m", c=8)
    cand = C.rearrange("p (a h l) -> p a h l", a=4, h=4)
    outacc = Dm[:, 0:4096].rearrange("p (c l) -> p c l", c=8)
    x2res = Dm[:, 4096:8192].rearrange("p (c l) -> p c l", c=8)
    x3 = A.rearrange("p (c l) -> p c l", c=8)
    io = k.indoff
    k.ln_rows = (E[0:1, io:io + 512], E[0:1, io + 512:io + 1024],
                 E[0:1, io + 1024:io + 1536], E[0:1, io + 1536:io + 2048])
    wotok = []
    for c in range(8):
        wotok.append(k.dma("y", Bm[:, c * 1024:(c + 1) * 1024], dr["wo_t"][c],
                           deps=(attn_done,), slot=f"wo{c % 2}"))
    woldt = (wotok[-1], wotok[-2])

    for lbo in range(4):
        lds = {}
        for a, (slot, lbc) in enumerate(((0, lbo), (1, lbo), (0, lbo + 4), (1, lbo + 4))):
            for hp in range(4):
                lds[(a, hp)] = k.dma("y", cand[:, a, hp, :], dr["ag_a_out"][slot][hp][lbc],
                                     deps=(acc, prev.get(("canduse", a, hp))),
                                     slot=f"cnd{a}")
        bl = {}
        for slot in range(2):
            for hp in range(4):
                a0, a1 = slot, slot + 2
                t = k.op("v", lambda e, a0_=a0, a1_=a1, h=hp: e.tensor_tensor(
                    cand[:, a1_, h, :], cand[:, a1_, h, :], cand[:, a0_, h, :],
                    ALU.subtract), deps=(lds[(a0, hp)], lds[(a1, hp)],
                                         prev.get(("canduse", slot, hp))))
                t = k.op("v", lambda e, a0_=a0, a1_=a1, h=hp: e.scalar_tensor_tensor(
                    cand[:, a1_, h, :], cand[:, a1_, h, :], k.sb["is_odd"],
                    cand[:, a0_, h, :], ALU.mult, ALU.add), deps=(t,))
                bl[slot * 4 + hp] = t
        for rr in range(4):
            r = lbo * 4 + rr
            tt = k.tr_tiles(
                [(cand[:, 2 + c // 4, c % 4, rr * 128:(rr + 1) * 128], attT[:, c, :],
                  bl[c]) for c in range(8)], (2, 3), deps=(prev.get("attTuse"),))
            opmm = None
            for kc in range(8):
                for dc in range(8):
                    b = 4 + dc // 4
                    if kc == 7 and dc == 7:
                        opmm = k.mm(b, ps[b][:, (dc % 4) * 128:(dc % 4) * 128 + 128],
                                    wo[:, kc, dc * 128:(dc + 1) * 128], attT[:, kc, :],
                                    False, True, deps=(tt,) + woldt, skip=True)
                    else:
                        k.mmq(b, ps[b][:, (dc % 4) * 128:(dc % 4) * 128 + 128],
                              wo[:, kc, dc * 128:(dc + 1) * 128], attT[:, kc, :],
                              kc == 0 and dc % 4 == 0, False,
                              deps=(tt,) + woldt, skip=True)
            prev["attTuse"] = opmm
            for dc in range(8):
                b = 4 + dc // 4
                prev[("opev", rr, dc)] = k.rd(
                    b, "v", lambda e, bb=ps[b], d=dc, r_=rr: e.tensor_copy(
                        outacc[:, d, r_ * 128:(r_ + 1) * 128],
                        bb[:, (d % 4) * 128:(d % 4) * 128 + 128]),
                    deps=(opmm, prev.get(("oafree", dc))))
        for a in range(4):
            for hp in range(4):
                prev[("canduse", a, hp)] = prev[("opev", 3, 7)]
        xr = k.dma("y", Dm[:, 4096:8192], dr["x2own"][lbo], deps=(prev.get("x2resuse"),), slot="x2r")
        radd = []
        for dc in range(8):
            t = k.op("v", lambda e, d=dc: e.tensor_tensor(
                outacc[:, d], outacc[:, d], x2res[:, d], ALU.add),
                deps=(xr,) + tuple(prev[("opev", rr, dc)] for rr in range(4)))
            radd.append(t)
        prev["x2resuse"] = radd[-1]
        lnt = ln_ffn(k, outacc, radd, prev, lbo)
        for dc in range(8):
            t = k.op("v", lambda e, d=dc, l=lbo: e.tensor_copy(
                x3[:, d, l * 512:(l + 1) * 512], outacc[:, d]), deps=(lnt[dc],))
            prev[("oafree", dc)] = t
    k.x3_ready = prev[("oafree", 7)]
    k.p.waits("s", [prev[("oafree", dc)] for dc in range(8)])
    k.x3dbg = k.dma("s", dr["dbg"], A, deps=(), slot="out")


def l1ffn_phase(k):
    p, dr, ps = k.p, k.dr, k.psum
    A, Bm, C, Dm, E = k.A, k.Bm, k.C, k.Dm, k.E
    prev = {}
    x3 = A.rearrange("p (c l) -> p c l", c=8)
    io = k.indoff
    mx = E[:, io:io + 1]
    mx2 = E[:, io + 1:io + 2]
    mx3 = E[:, io + 2:io + 3]
    srow = E[0:1, io + 4:io + 136]
    scol = {}
    rcol = {}
    for i, nm in enumerate(("s1", "s2", "s3", "s4", "s5")):
        scol[nm] = E[:, io + 140 + 2 * i:io + 141 + 2 * i]
        rcol[nm] = E[:, io + 141 + 2 * i:io + 142 + 2 * i]
    bmax = E[:, io + 152:io + 248]
    bscale = E[:, io + 248:io + 344]
    brecip = E[:, io + 344:io + 440]
    x3q = [C.rearrange("p (c l) -> p c l", c=4), Dm.rearrange("p (c l) -> p c l", c=4)]

    def x3qap(kc):
        return x3q[kc // 4][:, kc % 4, :]

    def global_max_scale(nm, slot, part_tok, qmax, deps):
        d = tuple(deps)
        t = k.op("t", lambda e: e.transpose(ps[6][0:1, 0:128], mx, k.sb["ident_f"]),
                 deps=d + k.bank_deps(6))
        k.bank_clear(6)
        t = k.rd(6, "v", lambda e: e.tensor_reduce(srow[:, 0:1], ps[6][0:1, 0:128], AX,
                                                   ALU.max), deps=(t,))
        t = k.dma("g", dr["ar_in"][:, slot:slot + 1], srow[:, 0:1], deps=(t,), slot="arg")
        t = k.coll("AllReduce", ALU.max, ALLC, dr["ar_in"][:, slot:slot + 1],
                   dr["ar_out"][:, slot:slot + 1], deps=(t,))
        t = k.dma("y", srow[:, 1:2], dr["ar_out"][:, slot:slot + 1], deps=(t,), slot="ary")
        t = k.op("t", lambda e: e.matmul(ps[6][:, 0:1], k.sb["ones_row_f"], srow[:, 1:2],
                                         start=True, stop=True), deps=(t,) + k.bank_deps(6))
        k.bank_clear(6)
        t = k.rd(6, "v", lambda e, n=nm, q=qmax: e.tensor_scalar(
            scol[n], ps[6][:, 0:1], 1.0 / q, 1e-12, ALU.mult, ALU.max), deps=(t,))
        t = k.op("v", lambda e, n=nm: e.reciprocal(rcol[n], scol[n]), deps=(t,))
        return t

    t = None
    for kc in range(8):
        t = k.op("v", lambda e, c=kc: e.tensor_reduce(
            mx2, x3[:, c, :], AX, ALU.max, apply_absolute_value=True),
            deps=(k.x3_ready, t))
        t = k.op("v", lambda e, c=kc: (e.tensor_copy(mx, mx2) if c == 0 else
                                       e.tensor_tensor(mx, mx, mx2, ALU.max)), deps=(t,))
    s1 = global_max_scale("s1", 0, mx, 32767.0, (t,))
    xqt = []
    xqh = C.bitcast(BF16).rearrange("p (c l) -> p c l", c=8)
    xql = Dm.bitcast(BF16).rearrange("p (c l) -> p c l", c=8)
    qt_ = Bm[:, 4096:6144]
    t = None
    for kc in range(8):
        t = k.op("v", lambda e, c=kc: e.tensor_scalar(
            qt_, x3[:, c, :], rcol["s1"], None, ALU.mult), deps=(s1, t))
        t = k.op("v", lambda e: e.tensor_scalar(
            qt_.bitcast(I32)[:, 0:2048], qt_, MAGIC, MAGIC, ALU.add, ALU.subtract),
            deps=(t,))
        t = k.op("v", lambda e: e.tensor_scalar(
            qt_, qt_.bitcast(I32)[:, 0:2048], scol["s1"], None, ALU.mult), deps=(t,))
        t = k.op("v", lambda e, c=kc: e.tensor_copy(xqh[:, c, :], qt_), deps=(t,))
        t = k.op("v", lambda e, c=kc: e.tensor_tensor(xql[:, c, :], qt_, xqh[:, c, :],
                                                      ALU.subtract), deps=(t,))
        xqt.append(t)
    xq_done = tuple(xqt)

    wnat = [Bm[:, i * 1024:(i + 1) * 1024] for i in range(2)]
    wi32 = Bm[:, 2048:3072].bitcast(I32)[:, 0:1024]
    wTh = Bm[:, 3072:3584].bitcast(BF16).rearrange("p (c m) -> p c m", c=8)
    wTl = Bm[:, 3584:4096].bitcast(BF16).rearrange("p (c m) -> p c m", c=8)
    stg = [Bm[:, 4096 + i * 512:4096 + (i + 1) * 512] for i in range(4)]

    def dequant_slab(src_dram, o, wslot, deps):
        t = k.dma("y", wnat[wslot], src_dram[o], deps=deps, slot=f"wn{wslot}")
        t = k.op("v", lambda e, w=wnat[wslot]: e.tensor_reduce(
            bmax[:, 0:32], w.rearrange("p (b q) -> p b q", q=32), AX, ALU.max,
            apply_absolute_value=True), deps=(t, prev.get("bmaxfree")))
        t = k.op("v", lambda e: e.tensor_scalar(bscale[:, 0:32], bmax[:, 0:32], 1.0 / 7.0,
                                                1e-12, ALU.mult, ALU.max), deps=(t,))
        t = k.op("v", lambda e: e.reciprocal(brecip[:, 0:32], bscale[:, 0:32]), deps=(t,))
        t = k.op("v", lambda e, w=wnat[wslot]: e.tensor_tensor(
            Bm[:, 2048:3072].rearrange("p (b q) -> p b q", q=32),
            w.rearrange("p (b q) -> p b q", q=32),
            brecip[:, 0:32].unsqueeze(2).to_broadcast((128, 32, 32)), ALU.mult),
            deps=(t, prev.get("wi32free")))
        t = k.op("v", lambda e: e.tensor_scalar(
            wi32, Bm[:, 2048:3072], MAGIC, MAGIC, ALU.add, ALU.subtract), deps=(t,))
        t = k.op("v", lambda e, w=wnat[wslot]: e.tensor_tensor(
            w.rearrange("p (b q) -> p b q", q=32),
            wi32.rearrange("p (b q) -> p b q", q=32),
            bscale[:, 0:32].unsqueeze(2).to_broadcast((128, 32, 32)), ALU.mult),
            deps=(t,))
        prev["bmaxfree"] = t
        prev["wi32free"] = t
        last = None
        for i in range(8):
            b = i % 2
            bk = ps[b]
            d = (t, prev.get("wTuse")) + k.bank_deps(b)
            k.bank_clear(b)
            tt = k.op("t", lambda e, w=wnat[wslot], i_=i, bb=bk: e.transpose(
                bb[0:128, 0:128], w[:, i_ * 128:(i_ + 1) * 128], k.sb["ident_f"]),
                deps=d)
            t1 = k.rd(b, "v", lambda e, bb=bk, i_=i: e.tensor_copy(
                wTh[:, i_, :], bb[0:128, 0:128]), deps=(tt,))
            last = k.rd(b, "v", lambda e, bb=bk, i_=i: e.tensor_tensor(
                wTl[:, i_, :], bb[0:128, 0:128], wTh[:, i_, :], ALU.subtract),
                deps=(t1,))
        return last

    mmax = {"up": None, "gate": None}
    stok = {}
    for o in range(24):
        for im, (nm, src, dst) in enumerate((("gate", dr["wg_nat"], dr["gatest"]),
                                             ("up", dr["wu_nat"], dr["upst"]))):
            wdq = dequant_slab(src, o, im, deps=(prev.get(("wnatuse", im)),))
            for lb in range(4):
                b = 2 + lb % 2
                sl_ = slice(lb * 512, (lb + 1) * 512)
                mt = None
                for kc in range(8):
                    for wTx, xx in ((wTh, xqh), (wTl, xqh), (wTh, xql)):
                        first = kc == 0 and xx is xqh and wTx is wTh
                        lastm = kc == 7 and wTx is wTh and xx is xql
                        if lastm:
                            mt = k.mm(b, ps[b][:], wTx[:, kc, :], xx[:, kc, sl_],
                                      False, True, deps=(wdq,) + xq_done)
                        else:
                            k.mmq(b, ps[b][:], wTx[:, kc, :], xx[:, kc, sl_],
                                  first, False, deps=(wdq,) + xq_done)
                if lb == 3:
                    prev[("wnatuse", im)] = mt
                    prev["wTuse"] = mt
                s_ = stg[lb]
                t = k.rd(b, "v", lambda e, bb=ps[b], s2=s_: e.tensor_copy(s2, bb[:]),
                         deps=(mt, stok.get(("last", lb))))
                t2 = k.op("v", lambda e, s2=s_: e.tensor_reduce(
                    mx2, s2, AX, ALU.max, apply_absolute_value=True), deps=(t,))
                t2 = k.op("v", lambda e, first=(o == 0 and lb == 0), m=mmax[nm], n=nm:
                          (e.tensor_copy(mx if n == "gate" else mx3, mx2) if first else
                           e.tensor_tensor(mx if n == "gate" else mx3,
                                           mx if n == "gate" else mx3, mx2, ALU.max)),
                          deps=(t2, mmax[nm]))
                mmax[nm] = t2
                stok[(nm, o, lb)] = k.dma("s", dst[o][:, lb * 512:(lb + 1) * 512], s_,
                                          deps=(t,), slot=f"gu{lb}")
                stok[("last", lb)] = stok[(nm, o, lb)]
    gu_done = tuple(stok[(nm, 23, lb)] for nm in ("gate", "up") for lb in range(4))
    s3 = global_max_scale("s3", 2, mx, 32767.0, (mmax["gate"],))
    t = k.op("v", lambda e: e.tensor_copy(mx, mx3), deps=(s3, mmax["up"]))
    s2 = global_max_scale("s2", 1, mx, 32767.0, (t,))

    wdn = C[:, 0:3072]
    wdf = C[:, 3072:6144]
    wdi = wdf.bitcast(I32)[:, 0:3072]
    wdtt = Dm[:, 0:3072]
    xq_war = (prev[("wnatuse", 0)], prev[("wnatuse", 1)])
    for dc in range(8):
        t = k.dma("y", wdn, dr["wd_nat"][dc],
                  deps=(prev.get("wdnatuse"),) + (xq_war if dc == 0 else ()), slot="wd")
        t = k.op("v", lambda e: e.tensor_reduce(
            bmax[:, 0:96], wdn.rearrange("p (b q) -> p b q", q=32), AX, ALU.max,
            apply_absolute_value=True), deps=(t, prev.get("bmaxfree")))
        t = k.op("v", lambda e: e.tensor_scalar(bscale[:, 0:96], bmax[:, 0:96], 1.0 / 7.0,
                                                1e-12, ALU.mult, ALU.max), deps=(t,))
        t = k.op("v", lambda e: e.reciprocal(brecip[:, 0:96], bscale[:, 0:96]), deps=(t,))
        t = k.op("v", lambda e: e.tensor_tensor(
            wdf.rearrange("p (b q) -> p b q", q=32),
            wdn.rearrange("p (b q) -> p b q", q=32),
            brecip[:, 0:96].unsqueeze(2).to_broadcast((128, 96, 32)), ALU.mult),
            deps=(t, prev.get("wdi32free")))
        t = k.op("v", lambda e: e.tensor_scalar(
            wdi, wdf, MAGIC, MAGIC, ALU.add, ALU.subtract), deps=(t,))
        t = k.op("v", lambda e: e.tensor_tensor(
            wdn.rearrange("p (b q) -> p b q", q=32),
            wdi.rearrange("p (b q) -> p b q", q=32),
            bscale[:, 0:96].unsqueeze(2).to_broadcast((128, 96, 32)), ALU.mult), deps=(t,))
        prev["bmaxfree"] = t
        prev["wdi32free"] = t
        wdh = wdtt[:, 0:1536].bitcast(BF16).rearrange("p (c m) -> p c m", c=24)
        wdl = wdtt[:, 1536:3072].bitcast(BF16).rearrange("p (c m) -> p c m", c=24)
        tt = None
        for i in range(24):
            b = i % 2
            bk = ps[b]
            d = (t, prev.get("wdTuse")) + k.bank_deps(b)
            k.bank_clear(b)
            t1 = k.op("t", lambda e, i_=i, bb=bk: e.transpose(
                bb[0:128, 0:128], wdn[:, i_ * 128:(i_ + 1) * 128], k.sb["ident_f"]),
                deps=d)
            t2 = k.rd(b, "v", lambda e, bb=bk, i_=i: e.tensor_copy(
                wdh[:, i_, :], bb[0:128, 0:128]), deps=(t1,))
            tt = k.rd(b, "v", lambda e, bb=bk, i_=i: e.tensor_tensor(
                wdl[:, i_, :], bb[0:128, 0:128], wdh[:, i_, :], ALU.subtract),
                deps=(t2,))
        prev["wdnatuse"] = tt
        prev["wdTuse"] = k.dma("s", dr["wdt_scr"][dc], wdtt, deps=(tt,), slot="wdt")
    wdT_done = prev["wdTuse"]

    g2st = dr["zxb"]
    g2stok = {}
    gmax = None
    gldb = [Bm[:, 10240:12288], Bm[:, 0:2048]]
    sigb = [Bm[:, 6144:8192], Bm[:, 2048:4096]]
    g2b = [Bm[:, 8192:10240], Bm[:, 4096:6144]]
    for o in range(24):
        pa = o % 2
        g, sg, g2_ = gldb[pa], sigb[pa], g2b[pa]
        t = k.dma("y", g, dr["gatest"][o], deps=(gu_done if o < 2 else ()) +
                  (s3, prev.get(("glduse", pa))), slot=f"gld{pa}")
        t = k.op("v", lambda e, w=g: e.tensor_scalar(w, w, rcol["s3"], None, ALU.mult),
                 deps=(t,))
        t = k.op("v", lambda e, w=g: e.tensor_scalar(
            w.bitcast(I32)[:, 0:2048], w, MAGIC, MAGIC, ALU.add, ALU.subtract),
            deps=(t,))
        t = k.op("v", lambda e, w=g: e.tensor_scalar(
            w, w.bitcast(I32)[:, 0:2048], scol["s3"], None, ALU.mult), deps=(t,))
        t = k.op("s", lambda e, w=g, s_=sg: e.activation(out=s_, in_=w, func=AF.Sigmoid),
                 deps=(t, prev.get(("sigfree", pa))))
        t = k.op("v", lambda e, s_=sg: e.tensor_scalar(s_, s_, 65536.0, None, ALU.mult),
                 deps=(t,))
        t = k.op("v", lambda e, s_=sg: e.tensor_scalar(
            s_.bitcast(I32)[:, 0:2048], s_, MAGIC, MAGIC, ALU.add, ALU.subtract),
            deps=(t,))
        t = k.op("v", lambda e, s_=sg: e.tensor_scalar(s_, s_.bitcast(I32)[:, 0:2048],
                                                       float(2.0 ** -16),
                                                       float(65535.0 * 2.0 ** -16),
                                                       ALU.mult, ALU.min), deps=(t,))
        t = k.op("v", lambda e, w=g, s_=sg, g_=g2_: e.tensor_tensor(g_, w, s_, ALU.mult),
                 deps=(t, prev.get(("g2free", pa))))
        prev[("sigfree", pa)] = t
        prev[("glduse", pa)] = t
        g2stok[o] = k.dma("s", g2st[o * 128:(o + 1) * 128, :], g2_, deps=(t,),
                          slot=f"g2s{pa}")
        t2 = k.op("v", lambda e, g_=g2_: e.tensor_reduce(
            mx2, g_, AX, ALU.max, apply_absolute_value=True), deps=(t,))
        gmax = k.op("v", lambda e, first=(o == 0): (
            e.tensor_copy(mx, mx2) if first else e.tensor_tensor(mx, mx, mx2, ALU.max)),
            deps=(t2, gmax))
        prev[("g2free", pa)] = g2stok[o]
    s4 = global_max_scale("s4", 3, mx, 32767.0, (gmax,))

    omax = None
    uldb = [Bm[:, 0:2048], Bm[:, 2048:4096]]
    gldc = [Bm[:, 10240:12288], Bm[:, 8192:10240]]
    opre = [Bm[:, 4096:6144], Bm[:, 6144:8192]]
    ot = {}
    for o in range(24):
        pa = o % 2
        g, u, g2_ = gldc[pa], uldb[pa], opre[pa]
        tg = k.dma("y", g, g2st[o * 128:(o + 1) * 128, :],
                   deps=(s4, g2stok[o], prev.get(("gldcuse", pa))) +
                   ((g2stok[22],) if o == 1 else ()), slot=f"gld{pa}")
        tu = k.dma("y", u, dr["upst"][o],
                   deps=(s4, prev.get(("ulduse", pa))), slot=f"uld{pa}")
        t = k.op("v", lambda e, w=g: e.tensor_scalar(w, w, rcol["s4"], None, ALU.mult),
                 deps=(tg,))
        t = k.op("v", lambda e, w=g: e.tensor_scalar(
            w.bitcast(I32)[:, 0:2048], w, MAGIC, MAGIC, ALU.add, ALU.subtract),
            deps=(t,))
        t = k.op("v", lambda e, w=g: e.tensor_scalar(w, w.bitcast(I32)[:, 0:2048],
                                                     scol["s4"], None, ALU.mult), deps=(t,))
        t2 = k.op("v", lambda e, w=u: e.tensor_scalar(w, w, rcol["s2"], None, ALU.mult),
                  deps=(tu,))
        t2 = k.op("v", lambda e, w=u: e.tensor_scalar(
            w.bitcast(I32)[:, 0:2048], w, MAGIC, MAGIC, ALU.add, ALU.subtract),
            deps=(t2,))
        t2 = k.op("v", lambda e, w=u: e.tensor_scalar(w, w.bitcast(I32)[:, 0:2048],
                                                      scol["s2"], None, ALU.mult),
                  deps=(t2,))
        t = k.op("v", lambda e, w=g, w2=u, g_=g2_: e.tensor_tensor(g_, w, w2, ALU.mult),
                 deps=(t, t2, prev.get(("opfree", pa))) +
                 ((g2stok[23],) if o == 0 else ()))
        prev[("gldcuse", pa)] = t
        prev[("ulduse", pa)] = t
        t2 = k.op("v", lambda e, g_=g2_: e.tensor_reduce(
            mx2, g_, AX, ALU.max, apply_absolute_value=True), deps=(t,))
        omax = k.op("v", lambda e, first=(o == 0): (
            e.tensor_copy(mx, mx2) if first else e.tensor_tensor(mx, mx, mx2, ALU.max)),
            deps=(t2, omax))
        ot[o] = k.dma("s", dr["ost"][o], g2_, deps=(t,), slot=f"ost{pa}")
        prev[("opfree", pa)] = ot[o]
    s5 = global_max_scale("s5", 3, mx, 32767.0, (omax,))

    oq = x3q
    oqv = C.rearrange("p (c l) -> p c l", c=16)
    oqv2 = Dm[:, 0:4096].rearrange("p (c l) -> p c l", c=8)
    k.ln_rows = (Dm[0:1, 4096:4608], Dm[0:1, 4608:5120], Dm[0:1, 5120:5632],
                 Dm[0:1, 5632:6144])
    wdt = [Bm[:, i * 3072:(i + 1) * 3072] for i in range(2)]
    ostage = Bm[:, 6144 + 512:6144 + 512 + 1024]

    def oqap(fi):
        return oqv[:, fi, :] if fi < 16 else oqv2[:, fi - 16, :]

    oqtmp = Bm[:, 7680:8192]
    out_dma = []
    for lb in range(4):
        oqd = []
        for fi in range(24):
            a = oqap(fi)
            ah = a[:, 0:256].bitcast(BF16)
            al = a[:, 256:512].bitcast(BF16)
            t = k.dma("y", a, dr["ost"][fi][:, lb * 512:(lb + 1) * 512],
                      deps=(s5, wdT_done, ot[fi], prev.get(("oquse", fi))),
                      slot=f"oq{fi % 4}")
            t = k.op("v", lambda e, a_=a: e.tensor_scalar(
                oqtmp, a_, rcol["s5"], None, ALU.mult),
                deps=(t, prev.get("oqtfree")) +
                ((ot[23],) if lb == 0 and fi == 0 else ()))
            t = k.op("v", lambda e: e.tensor_scalar(
                oqtmp.bitcast(I32)[:, 0:512], oqtmp, MAGIC, MAGIC, ALU.add,
                ALU.subtract), deps=(t,))
            t = k.op("v", lambda e: e.tensor_scalar(
                oqtmp, oqtmp.bitcast(I32)[:, 0:512], scol["s5"], None, ALU.mult),
                deps=(t,))
            t = k.op("v", lambda e, h=ah: e.tensor_copy(h, oqtmp), deps=(t,))
            t = k.op("v", lambda e, h=ah, l=al: e.tensor_tensor(
                l, oqtmp, h, ALU.subtract), deps=(t,))
            prev["oqtfree"] = t
            oqd.append(t)
        radd = []
        for dc in range(8):
            wldt = k.dma("y", wdt[dc % 2], dr["wdt_scr"][dc],
                         deps=(prev.get(("wdtuse", dc % 2)),) +
                         ((prev.get(("ulduse", 0)), prev.get(("ulduse", 1)),
                           prev.get(("opfree", 0))) if lb == 0 and dc < 2 else ()),
                         slot=f"wdl{dc % 2}")
            wdth = wdt[dc % 2][:, 0:1536].bitcast(BF16).rearrange(
                "p (c m) -> p c m", c=24)
            wdtl = wdt[dc % 2][:, 1536:3072].bitcast(BF16).rearrange(
                "p (c m) -> p c m", c=24)
            b = 2 + dc % 2
            mt = None
            for fi in range(24):
                a = oqap(fi)
                ah = a[:, 0:256].bitcast(BF16)
                al = a[:, 256:512].bitcast(BF16)
                for wx, xx in ((wdth, ah), (wdtl, ah), (wdth, al)):
                    first = fi == 0 and wx is wdth and xx is ah
                    lastm = fi == 23 and wx is wdth and xx is al
                    if lastm:
                        mt = k.mm(b, ps[b][:], wx[:, fi, :], xx, False, True,
                                  deps=(wldt, oqd[fi]))
                    else:
                        k.mmq(b, ps[b][:], wx[:, fi, :], xx, first, False,
                              deps=(wldt, oqd[fi]))
            prev[("wdtuse", dc % 2)] = mt
            t = k.rd(b, "v", lambda e, bb=ps[b], d=dc, l=lb: e.tensor_tensor(
                x3[:, d, l * 512:(l + 1) * 512], x3[:, d, l * 512:(l + 1) * 512], bb[:],
                ALU.add), deps=(mt,))
            radd.append(t)
        for fi in range(24):
            prev[("oquse", fi)] = radd[-1]
        lnt = ln_ffn(k, None, radd, prev, lb,
                     res_fn=lambda dc, l=lb: x3[:, dc, l * 512:(l + 1) * 512])
        for rr in range(4):
            r = lb * 4 + rr
            tt = k.tr_tiles(
                [(x3[:, i, r * 128:(r + 1) * 128], ostage[:, i * 128:(i + 1) * 128],
                  lnt[i]) for i in range(8)], (0, 1), deps=(prev.get("ostuse"),))
            od = k.dma("s", dr["out_own"][r * 128:(r + 1) * 128, :], ostage, deps=(tt,), slot="out")
            prev["ostuse"] = od
            out_dma.append(od)
    k.out_done = out_dma[-1]


_NC_CACHE = {}


def _get_nc():
    if "nc" not in _NC_CACHE:
        _NC_CACHE["nc"] = build_program()
    return _NC_CACHE["nc"]


def kernel(**inputs):
    import ml_dtypes
    ins = {kk: np.asarray(v) for kk, v in inputs.items()}
    nc = _get_nc()

    ident_f = np.eye(128, dtype=np.float32)
    ident_b = np.eye(128, dtype=np.float32)
    sl = np.arange(128)
    tri_f = (sl[:, None] <= sl[None, :]).astype(np.float32)
    mask_b = np.where(sl[:, None] > sl[None, :], np.float32(-1e30), np.float32(0.0))
    ones_row_f = np.ones((1, 128), np.float32)
    ind = np.zeros((32, 16, 128), np.float32)
    for h in range(32):
        for j in range(16):
            ind[h, j, :] = ((2 * j + (np.arange(128) // 64)) == h).astype(np.float32)
    ind = ind.reshape(32, 2048)

    w_in = ins["L0_attn_W_in"]
    w_in_pad = np.zeros((ZXB_PAD, D), np.float32)
    w_in_pad[:4384] = w_in
    w_in_t = np.ascontiguousarray(
        w_in_pad.T.reshape(8, 128, 35, 128).transpose(2, 1, 0, 3).reshape(35, 128, 1024))
    w_out_t = np.ascontiguousarray(ins["L0_attn_W_out"].T.reshape(16, 128, D))
    conv_w = np.zeros((18 * 128, DCONV), np.float32)
    conv_w[:2304] = ins["L0_attn_conv_w"][:, 0, :]
    conv_w = np.ascontiguousarray(
        conv_w.reshape(18, 128, DCONV).transpose(1, 0, 2).reshape(128, 72))
    conv_b = np.zeros((18 * 128,), np.float32)
    conv_b[:2304] = ins["L0_attn_conv_b"]
    conv_b = np.ascontiguousarray(conv_b.reshape(18, 128).T)
    d_exp = np.zeros((128, 16), np.float32)
    for j in range(16):
        d_exp[:, j] = ins["L0_attn_D_param"][2 * j + np.arange(128) // 64]
    normw_col = np.ascontiguousarray(ins["L0_attn_norm_w"].reshape(16, 128).T)

    w1 = np.ascontiguousarray(
        ins["L0_ffn_W1"].reshape(8, 128, 32, 128).transpose(2, 1, 0, 3).reshape(32, 128, 1024))
    w2 = np.ascontiguousarray(
        ins["L0_ffn_W2"].reshape(2, 16, 128, 8, 128).transpose(0, 3, 2, 1, 4)
        .reshape(2, 8, 128, 2048))
    b1_col = np.ascontiguousarray(ins["L0_ffn_b1"].reshape(32, 128).T)
    qp_col = np.ascontiguousarray(ins["L0_ffn_quantum_phases"].reshape(32, 128).T)
    qr_col = np.ascontiguousarray(ins["L0_ffn_quantum_weights_real"].reshape(32, 128).T)
    qi_col = np.ascontiguousarray(ins["L0_ffn_quantum_weights_imag"].reshape(32, 128).T)
    b2_col = np.ascontiguousarray(ins["L0_ffn_b2"].reshape(8, 128).T)

    wq_t_full = ins["L1_attn_wq"].T
    wk_t_full = ins["L1_attn_wk"].T
    wv_t_full = ins["L1_attn_wv"].T
    wo_t = np.ascontiguousarray(ins["L1_attn_wo"].T.reshape(8, 128, D))
    wg_nat = np.ascontiguousarray(ins["L1_ffn_w_gate"].reshape(24, 128, D))
    wu_nat = np.ascontiguousarray(ins["L1_ffn_w_up"].reshape(24, 128, D))
    wd_nat = np.ascontiguousarray(ins["L1_ffn_w_down"].reshape(8, 128, FI))
    u = ins["L0_attn_u"]

    def qkv_tile(w_t, half):
        s = w_t[:, half * 512:(half + 1) * 512]
        a = s.reshape(8, 128, 4, 128).transpose(2, 1, 0, 3)
        return np.ascontiguousarray(a.reshape(4, 128, 1024))

    in_maps = []
    for c in range(8):
        b, half = c // 2, c % 2
        m = dict(
            u_own=np.ascontiguousarray(u[b, half * HALF:(half + 1) * HALF, :]),
            w_in_t=w_in_t, w_out_t=w_out_t, conv_w=conv_w, conv_b=conv_b,
            dt_bias=np.ascontiguousarray(ins["L0_attn_dt_bias"].reshape(NH, 1)),
            a_log=np.ascontiguousarray(ins["L0_attn_A_log"].reshape(NH, 1)),
            d_exp=d_exp, normw_col=normw_col,
            w1=w1, w2=w2, b1_col=b1_col, qp_col=qp_col, qr_col=qr_col, qi_col=qi_col,
            b2_col=b2_col,
            wq_t=qkv_tile(wq_t_full, half), wk_t=qkv_tile(wk_t_full, half),
            wv_t=qkv_tile(wv_t_full, half),
            wo_t=wo_t, wg_nat=wg_nat, wu_nat=wu_nat, wd_nat=wd_nat,
            ident_f=ident_f, ident_b=ident_b, tri_f=tri_f, mask_b=mask_b,
            ones_row_f=ones_row_f,
            is_odd=np.full((128, 1), float(half), np.float32),
            ind_hp=ind,
        )
        in_maps.append(m)

    res = run_bass_kernel_spmd(nc, in_maps, core_ids=list(range(8)))
    out = np.zeros((B, L, D), np.float32)
    for c in range(8):
        b, half = c // 2, c % 2
        out[b, half * HALF:(half + 1) * HALF, :] = res.results[c]["out_own"]
    return out



# version 08
# speedup vs baseline: 1.1822x; 1.0036x over previous
import numpy as np
import concourse.bass as bass
import concourse.mybir as mybir
from concourse.bass_utils import run_bass_kernel_spmd

F32 = mybir.dt.float32
BF16 = mybir.dt.bfloat16
I32 = mybir.dt.int32
AF = mybir.ActivationFunctionType
ALU = mybir.AluOpType
AX = mybir.AxisListType.X

B, L, D = 4, 4096, 1024
HALF = 2048
DIN, NH, HD, DCONV = 2048, 32, 64, 4
ZXB_PAD = 4480
LT, NLT = 512, 4
FI = 3072
INV_2PI = float(1.0 / (2.0 * np.pi))
TWO_PI = float(2.0 * np.pi)
MAGIC = 12582912.0
PAIRS = [[0, 1], [2, 3], [4, 5], [6, 7]]
ALLC = [list(range(8))]


class Plan:
    SEMKEYS = ("t", "v", "s", "g", "dy", "ds", "dg", "cc")

    def __init__(self):
        self.q = {k: [] for k in "tvsgy"}
        self.cnt = {k: 0 for k in self.SEMKEYS}
        self.step = dict(t=1, v=1, s=1, g=1, dy=16, ds=16, dg=16, cc=1)
        self.last_wait = {e: {k: 0 for k in self.SEMKEYS} for e in "tvsgy"}
        self.sems = {}
        self.slotmap = {}
        self.dsem_pool = []

    def emit(self, ek, fn):
        self.q[ek].append(fn)

    def mark(self, ek, key, fn):
        self.cnt[key] += self.step[key]
        sems, step = self.sems, self.step[key]
        self.q[ek].append(lambda e: fn(e).then_inc(sems[key], step))
        return (key, self.cnt[key])

    def wait(self, ek, tok):
        if tok is None:
            return
        key, v = tok
        if self.last_wait[ek][key] >= v:
            return
        self.last_wait[ek][key] = v
        sems = self.sems
        self.q[ek].append(lambda e: e.wait_ge(sems[key], v))

    def waits(self, ek, toks):
        for t in toks:
            if t is not None:
                self.wait(ek, t)

    def barrier(self):
        own = {}
        for ek in "tvsg":
            self.wait(ek, (ek, self.cnt[ek]))
            own[ek] = self.mark(ek, ek, lambda e: e.nop())
        toks = [(kk, v) for kk, v in self.cnt.items() if v > 0]
        for ek in "tvsgy":
            for tok in toks:
                if tok[0] != ek:
                    self.wait(ek, tok)


DMAQ = {"y": "dy", "s": "ds", "g": "dg"}


class K:
    def __init__(self, nc, p, dram, psum):
        self.nc, self.p, self.dr, self.psum = nc, p, dram, psum
        self.sb = {}
        self.bank_readers = [dict() for _ in range(8)]

    def bank_deps(self, b):
        return tuple(self.bank_readers[b].values())

    def bank_read(self, b, ek, tok):
        self.bank_readers[b][tok[0]] = tok

    def bank_clear(self, b):
        self.bank_readers[b] = {}

    def dma(self, ek, dst_ap, src_ap, deps=(), slot=None):
        p = self.p
        assert slot is not None
        if slot not in p.slotmap:
            idx = len(p.slotmap)
            sem = p.dsem_pool[idx]
            p.slotmap[slot] = sem
            p.sems[slot] = sem
            p.cnt[slot] = 0
            p.step[slot] = 16
            for e2 in "tvsgy":
                p.last_wait[e2][slot] = 0
        p.waits(ek, deps)
        if p.cnt[slot] > 0:
            p.wait(ek, (slot, p.cnt[slot]))
        return p.mark(ek, slot, lambda e: e.dma_start(out=dst_ap, in_=src_ap))

    def op(self, ek, fn, deps=()):
        self.p.waits(ek, deps)
        return self.p.mark(ek, ek, fn)

    def opq(self, ek, fn, deps=()):
        self.p.waits(ek, deps)
        self.p.emit(ek, fn)

    def coll(self, kind, op, groups, in_ap, out_ap, deps=()):
        self.p.waits("g", deps)
        return self.p.mark("g", "cc", lambda e: e.collective_compute(
            kind, op, replica_groups=groups, ins=[in_ap.opt()], outs=[out_ap.opt()]))

    def mm(self, b, out_ap, lhsT, rhs, start, stop, deps=(), skip=False):
        d = tuple(deps) + (self.bank_deps(b) if start else ())
        if start:
            self.bank_clear(b)
        return self.op("t", lambda e: e.matmul(out_ap, lhsT, rhs, start=start, stop=stop,
                                               skip_group_check=skip), deps=d)

    def mmq(self, b, out_ap, lhsT, rhs, start, stop, deps=(), skip=False):
        d = tuple(deps) + (self.bank_deps(b) if start else ())
        if start:
            self.bank_clear(b)
        self.opq("t", lambda e: e.matmul(out_ap, lhsT, rhs, start=start, stop=stop,
                                         skip_group_check=skip), deps=d)

    def rd(self, b, ek, fn, deps=()):
        t = self.op(ek, fn, deps=deps)
        self.bank_read(b, ek, t)
        return t

    def tr_tiles(self, tiles, bank_pair, deps=(), ek="v"):
        ident = self.sb["ident_f"]
        last = None
        for i, (src, dst, dep) in enumerate(tiles):
            b = bank_pair[i % 2]
            bk = self.psum[b]
            r, c = src.shape[-2], src.shape[-1]
            self.mm(b, None, None, None, True, True) if False else None
            d = tuple(deps) + (dep,) + self.bank_deps(b)
            self.bank_clear(b)
            tt = self.op("t", lambda e, s=src, bb=bk, c_=c, r_=r: e.transpose(
                bb[0:c_, 0:r_], s, ident[0:r_, 0:r_]), deps=d)
            last = self.rd(b, ek, lambda e, bb=bk, d2=dst, c_=c, r_=r: e.tensor_copy(
                d2, bb[0:c_, 0:r_]), deps=(tt,))
        return last

    def newton_rsqrt(self, row, m_row, tmp_row, dep, eps):
        t = self.op("s", lambda e: e.activation(out=row, in_=m_row, func=AF.Sqrt, bias=eps),
                    deps=(dep,))
        t = self.op("v", lambda e: e.reciprocal(row, row), deps=(t,))
        for _ in range(2):
            t1 = self.op("v", lambda e: e.tensor_tensor(tmp_row, row, row, ALU.mult),
                         deps=(t,))
            t1 = self.op("v", lambda e: e.tensor_tensor(tmp_row, tmp_row, m_row, ALU.mult),
                         deps=(t1,))
            t1 = self.op("v", lambda e: e.tensor_scalar(tmp_row, tmp_row, -0.5,
                                                        1.5 - 0.5 * eps, ALU.mult, ALU.add),
                         deps=(t1,))
            t = self.op("v", lambda e: e.tensor_tensor(row, row, tmp_row, ALU.mult),
                        deps=(t1,))
        return t


def build_program():
    nc = bass.Bass("TRN2", target_bir_lowering=False)
    p = Plan()
    dram = {}

    def din(name, shape, dtype=F32):
        dram[name] = nc.dram_tensor(name, list(shape), dtype, kind="ExternalInput").ap()

    def dout(name, shape, dtype=F32):
        dram[name] = nc.dram_tensor(name, list(shape), dtype, kind="ExternalOutput").ap()

    def dscr(name, shape, dtype=F32):
        dram[name] = nc.dram_tensor(name, list(shape), dtype).ap()

    din("u_own", (HALF, D))
    din("w_in_t", (35, 128, 1024))
    din("w_out_t", (16, 128, D))
    din("conv_w", (128, 72))
    din("conv_b", (128, 18))
    din("dt_bias", (NH, 1))
    din("a_log", (NH, 1))
    din("d_exp", (128, 16))
    din("normw_col", (128, 16))
    din("w1", (32, 128, 1024))
    din("w2", (2, 8, 128, 2048))
    din("b1_col", (128, 32))
    din("qp_col", (128, 32))
    din("qr_col", (128, 32))
    din("qi_col", (128, 32))
    din("b2_col", (128, 8))
    din("wq_t", (4, 128, 1024))
    din("wk_t", (4, 128, 1024))
    din("wv_t", (4, 128, 1024))
    din("wo_t", (8, 128, D))
    din("wg_nat", (24, 128, D))
    din("wu_nat", (24, 128, D))
    din("wd_nat", (8, 128, FI))
    din("ident_f", (128, 128))
    din("ident_b", (128, 128))
    din("tri_f", (128, 128))
    din("mask_b", (128, 128))
    din("ones_row_f", (1, 128))
    din("is_odd", (128, 1))
    din("ind_hp", (32, 16 * 128))

    dout("out_own", (HALF, D))

    dscr("zxb", (ZXB_PAD, HALF))
    dscr("ystage", (NLT, 128, 8192))
    dscr("x2own", (NLT, 128, 4096))
    dscr("attn_heads", (4, 8, 128, 512))
    dscr("upst", (24, 128, HALF))
    dscr("gatest", (24, 128, HALF))
    dscr("ost", (24, 128, HALF))
    dscr("wdt_scr", (8, 128, FI))
    dscr("dbg", (128, 16384))
    dscr("ag_s_in", (128, DIN))
    dscr("ag_s_out", (2, 128, DIN))
    dscr("ag_h_in", (128, 54))
    dscr("ag_h_out", (2, 128, 54))
    dscr("ag_x_out", (2, NLT, 128, 4096))
    dscr("ag_a_out", (2, 4, 8, 128, 512))
    dscr("ar_in", (1, 4))
    dscr("ar_out", (1, 4))

    NA, NB, NC, ND, NE = 16384, 12544, 8192, 8192, 4864
    import contextlib
    with contextlib.ExitStack() as stack:
        ent = stack.enter_context
        t_a = ent(nc.sbuf_tensor("AR_A", [128, NA], F32))
        t_b = ent(nc.sbuf_tensor("AR_B", [128, NB], F32))
        t_c = ent(nc.sbuf_tensor("AR_C", [128, NC], F32))
        t_d = ent(nc.sbuf_tensor("AR_D", [128, ND], F32))
        t_e = ent(nc.sbuf_tensor("AR_E", [128, NE], F32))
        psl = [ent(nc.psum_tensor(f"psb{i}", [128, 512], F32)) for i in range(8)]
        ps0, ps1, ps2, ps3, ps4, ps5, ps6, ps7 = psl
        sm_t = ent(nc.semaphore("sm_t"))
        sm_v = ent(nc.semaphore("sm_v"))
        sm_s = ent(nc.semaphore("sm_s"))
        sm_g = ent(nc.semaphore("sm_g"))
        sm_dy = ent(nc.semaphore("sm_dy"))
        sm_ds = ent(nc.semaphore("sm_ds"))
        sm_dg = ent(nc.semaphore("sm_dg"))
        sm_cc = ent(nc.semaphore("sm_cc"))
        dsem_pool = [ent(nc.semaphore(f"dm{i}")) for i in range(72)]
        block = ent(nc.Block())
        p.sems = dict(t=sm_t, v=sm_v, s=sm_s, g=sm_g, dy=sm_dy, ds=sm_ds, dg=sm_dg, cc=sm_cc)
        p.dsem_pool = dsem_pool
        k = K(nc, p, dram, [ps0, ps1, ps2, ps3, ps4, ps5, ps6, ps7])
        k.A, k.Bm, k.C, k.Dm, k.E = (t.ap() for t in (t_a, t_b, t_c, t_d, t_e))

        cvals = (0.0, 1.0, 1e-5)
        for i, cv in enumerate(cvals):
            capn = k.E[:, NE - 3 + i:NE - 2 + i]
            nc.const_aps.register(F32, cv, capn)
            k.op("v", lambda e, a=capn, v=cv: e.memset(a, v))
        p.barrier()
        setup_consts(k)
        mamba_phase(k)
        p.barrier()
        for b in range(8):
            k.bank_clear(b)
        l0ffn_phase(k)
        p.barrier()
        for b in range(8):
            k.bank_clear(b)
        attn_phase(k)
        p.barrier()
        for b in range(8):
            k.bank_clear(b)
        l1ffn_phase(k)
        p.barrier()

        @block.tensor
        def _(e):
            for fn in p.q["t"]:
                fn(e)

        @block.vector
        def _(e):
            for fn in p.q["v"]:
                fn(e)

        @block.scalar
        def _(e):
            for fn in p.q["s"]:
                fn(e)

        @block.gpsimd
        def _(e):
            for fn in p.q["g"]:
                fn(e)

        @block.sync
        def _(e):
            for fn in p.q["y"]:
                fn(e)

    return nc


def setup_consts(k):
    E, dr = k.E, k.dr
    off = 0
    toks = {}

    def ld(name, n, src=None, parts=None, dtype=F32):
        nonlocal off
        ap = E[:, off:off + n]
        if dtype == BF16:
            ap = ap.bitcast(BF16)[:, 0:2 * n]
        if parts is not None:
            ap = ap[0:parts]
        k.sb[name] = ap
        toks[name] = k.dma("y", ap, src if src is not None else dr[name], slot="cst")
        off += n

    ld("ident_f", 128)
    ld("tri_f", 128)
    for nm in ("ident_b", "mask_b"):
        stg32 = E[:, off:off + 128]
        tld = k.dma("y", stg32, dr[nm], slot="cst")
        off += 128
        bap = E[:, off:off + 64].bitcast(BF16)[:, 0:128]
        k.sb[nm] = bap
        toks[nm] = k.op("v", lambda e, a=bap, s=stg32: e.tensor_copy(a, s), deps=(tld,))
        off += 64
    ld("ones_row_f", 128, parts=1)
    ld("is_odd", 1)
    k.indoff = off
    ld("ind_hp", 2048, parts=32)
    ld("d_exp", 16)
    ld("normw_col", 16)
    ld("dt_bias", 1, parts=32)
    ld("a_log", 1, parts=32)
    k.sb["conv_w"] = E[:, off:off + 72].rearrange("p (c t) -> p c t", c=18)
    toks["conv_w"] = k.dma("y", E[:, off:off + 72], dr["conv_w"], slot="cst")
    off += 72
    k.sb["conv_b"] = E[:, off:off + 18]
    toks["conv_b"] = k.dma("y", E[:, off:off + 18], dr["conv_b"], slot="cst")
    off += 18
    ld("b1_col", 32)
    ld("qp_col", 32)
    ld("qr_col", 32)
    ld("qi_col", 32)
    ld("b2_col", 8)
    k.sb["ones_col"] = E[:, off:off + 1]
    t_ones = k.op("v", lambda e: e.memset(k.sb["ones_col"], 1.0))
    off += 1
    k.sb["A_neg"] = E[0:32, off:off + 1]
    off += 1
    t1 = k.op("s", lambda e: e.activation(out=k.sb["A_neg"], in_=k.sb["a_log"], func=AF.Exp),
              deps=(toks["a_log"],))
    t1 = k.op("v", lambda e: e.tensor_scalar(k.sb["A_neg"], k.sb["A_neg"], -1.0, None,
                                             ALU.mult), deps=(t1,))
    thp = E[:, off:off + 32]; off += 32
    alpha = E[:, off:off + 32]; off += 32
    w1_ = E[:, off:off + 32]; off += 32
    w2_ = E[:, off:off + 32]; off += 32
    w3f = E[:, off:off + 32]; off += 32
    w3_ = w3f.bitcast(I32)[:, 0:32]
    k.sb["thp_col"], k.sb["alpha_col"] = thp, alpha

    def rr_sin(dst, src_ap, phase, dep):
        t = k.op("v", lambda e: e.tensor_scalar(w1_, src_ap, INV_2PI, phase * INV_2PI,
                                                ALU.mult, ALU.add), deps=(dep,))
        t = k.op("v", lambda e: e.tensor_scalar(w3f, w1_, MAGIC, MAGIC, ALU.add,
                                                ALU.subtract), deps=(t,))
        t = k.op("v", lambda e: e.tensor_copy(w3_, w3f), deps=(t,))
        t = k.op("v", lambda e: e.scalar_tensor_tensor(w2_, w3_, -1.0, w1_, ALU.mult,
                                                       ALU.add), deps=(t,))
        t = k.op("v", lambda e: e.tensor_scalar(w2_, w2_, 0.4999999, -0.4999999,
                                                ALU.min, ALU.max), deps=(t,))
        return k.op("s", lambda e, d=dst: e.activation(out=d, in_=w2_, func=AF.Sin,
                                                       scale=TWO_PI), deps=(t,))

    t = rr_sin(thp, k.sb["qp_col"], 0.0, toks["qp_col"])
    t = k.op("v", lambda e: e.tensor_tensor(alpha, thp, k.sb["qi_col"], ALU.mult),
             deps=(t, toks["qi_col"]))
    t2 = rr_sin(thp, k.sb["qp_col"], float(np.pi / 2), t)
    t2 = k.op("v", lambda e: e.tensor_tensor(w2_, thp, k.sb["qr_col"], ALU.mult),
              deps=(t2, toks["qr_col"]))
    t2 = k.op("v", lambda e: e.tensor_tensor(alpha, alpha, w2_, ALU.add), deps=(t2,))
    t2 = k.op("v", lambda e: e.scalar_tensor_tensor(alpha, alpha, 0.1, k.sb["qi_col"],
                                                    ALU.mult, ALU.mult), deps=(t2,))
    t3 = k.op("v", lambda e: e.scalar_tensor_tensor(thp, k.sb["b1_col"], 0.1, k.sb["qp_col"],
                                                    ALU.mult, ALU.add),
              deps=(t2, toks["b1_col"]))
    t3 = k.op("v", lambda e: e.tensor_scalar(thp, thp, INV_2PI, None, ALU.mult), deps=(t3,))
    k.consts_tok = tuple(toks.values()) + (t_ones, t1, t3)
    k.eoff = off


def mamba_phase(k):
    p, dr, ps = k.p, k.dr, k.psum
    A, Bm, C, Dm, E = k.A, k.Bm, k.C, k.Dm, k.E
    ones_row = k.sb["ones_row_f"]
    ones_col = k.sb["ones_col"]
    consts = k.consts_tok

    uT = A.rearrange("p (c l) -> p c l", c=8)
    utok = [Bm[:, i * 1024:(i + 1) * 1024] for i in range(2)]
    ld = [None, None]
    ev_last = {}
    for i in range(16):
        ld[i % 2] = k.dma("y", utok[i % 2], dr["u_own"][i * 128:(i + 1) * 128, :],
                          deps=consts + (ev_last.get(i - 2),), slot=f"u{i % 2}")
        ev_last[i] = k.tr_tiles(
            [(utok[i % 2][:, j * 128:(j + 1) * 128], uT[:, j, i * 128:(i + 1) * 128], None)
             for j in range(8)], (0, 1), deps=(ld[i % 2],))
    u_done = ev_last[15]

    wsl = [Dm[:, i * 1024:(i + 1) * 1024].rearrange("p (c m) -> p c m", c=8) for i in range(2)]
    stg = [Bm[:, 2048 + i * 512:2048 + (i + 1) * 512] for i in range(4)]
    halo = Bm[:, 12294:12348]
    wtok = [None, None]
    wuse = [None, None]
    ev_tok = {}
    st_tok = {}
    si = 0
    for r in range(35):
        wtok[r % 2] = k.dma("y", Dm[:, (r % 2) * 1024:(r % 2) * 1024 + 1024],
                            dr["w_in_t"][r], deps=(u_done, wuse[r % 2]),
                            slot=f"win{r % 2}")
        for lb in range(4):
            b = 2 + (si % 2)
            bk = ps[b]
            for kc in range(7):
                k.mmq(b, bk[:], wsl[r % 2][:, kc, :], uT[:, kc, lb * 512:(lb + 1) * 512],
                      kc == 0, False, deps=(wtok[r % 2],))
            mt = k.mm(b, bk[:], wsl[r % 2][:, 7, :], uT[:, 7, lb * 512:(lb + 1) * 512],
                      False, True, deps=(wtok[r % 2],))
            if lb == 3:
                wuse[r % 2] = mt
            s_ = stg[si % 4]
            t = k.rd(b, "v", lambda e, bb=bk, d=s_: e.tensor_copy(d, bb[:]),
                     deps=(mt, st_tok.get(si - 4)))
            if lb == 3 and 16 <= r < 34:
                t = k.op("v", lambda e, s2=s_, c=r - 16: e.tensor_copy(
                    halo[:, c * 3:c * 3 + 3], s2[:, 509:512]), deps=(t,))
            ev_tok[si] = t
            st_tok[si] = k.dma("s", dr["zxb"][r * 128:(r + 1) * 128, lb * 512:(lb + 1) * 512],
                               s_, deps=(t,), slot=f"stg{si % 4}")
            si += 1
    zx_done = tuple(st_tok[si - 1 - i] for i in range(4))

    hst = k.dma("g", dr["ag_h_in"], halo, deps=(ev_tok[si - 1],), slot="halo")
    halo_cc = k.coll("AllGather", ALU.bypass, PAIRS, dr["ag_h_in"], dr["ag_h_out"],
                     deps=(hst,))
    halo_rx = Bm[:, 12352:12406]
    t = k.dma("g", halo_rx, dr["ag_h_out"][0], deps=(halo_cc,), slot="halo")
    hmask = k.op("v", lambda e: e.tensor_scalar(halo_rx, halo_rx, k.sb["is_odd"], None,
                                                ALU.mult), deps=(t,))

    C_res = Dm[:, 0:2048]
    Acum_res = Dm[0:32, 2048:4096]
    S = Dm[:, 4096:6144]
    dtw = [Dm[0:32, 6144 + i * 512:6144 + (i + 1) * 512] for i in range(2)]
    CA_store = Dm[0:32, 7168:7184]
    CA_run = Dm[0:32, 7184:7185]
    lam_sb = Dm[0:1, 7186:7218]
    LamRep = Dm[:, 7220:7252]
    Btok = Dm[:, 7252:7764].rearrange("p (c n) -> p c n", c=4)
    convin = [Bm[:, i * 515:(i + 1) * 515] for i in range(2)]
    xconv = Bm[:, 1030:9222].rearrange("p (c l) -> p c l", c=16)
    acrow = [Bm[0:1, 9222 + i * 1024:9222 + (i + 1) * 1024] for i in range(2)]
    Bconv = Bm[:, 11270:11782]
    Cconv = Bm[:, 11782:12294]
    eo = k.eoff
    toks_ = E[:, eo:eo + 384].rearrange("p (c h) -> p c h", c=12)
    dt_tok = lambda cc: toks_[:, cc, :]
    decay_tok = lambda cc: toks_[:, 4 + cc, :]
    acum_tok = lambda cc: toks_[:, 8 + cc, :]
    Gm_t = E[:, eo + 384:eo + 512]
    L_t = E[:, eo + 512:eo + 640]
    GL_t = E[:, eo + 640:eo + 768]
    expA = E[0:32, eo + 768:eo + 896]
    eaF = E[:, eo + 896:eo + 1024]
    gptmp = E[:, eo + 1024:eo + 1536]
    L_t2 = Dm[:, 7892:8020]
    GL_t2 = Dm[:, 8020:8148]
    eaF_b = Bm[:, 12406:12534]
    xtok = C.rearrange("p (c q) -> p c q", c=4)

    s_zero = k.op("v", lambda e: e.memset(S, 0.0))
    CA_z = k.op("v", lambda e: e.memset(CA_run, 0.0))
    ones32 = Dm[0:32, 7764:7892]
    t_o32 = k.op("v", lambda e: e.memset(ones32, 1.0))
    prev = {}
    ydma = {}
    supd = {}
    prev["car"] = CA_z

    for lt in range(NLT):
        conv_done = []
        for ch in range(18):
            ci = convin[ch % 2]
            row0 = 2048 + ch * 128
            if lt == 0:
                t = k.dma("y", ci[:, 3:515], dr["zxb"][row0:row0 + 128, 0:512],
                          deps=zx_done + (prev.get(("cvslot", ch % 2)),),
                          slot=f"cv{ch % 2}")
                t = k.op("v", lambda e, c=ci, h=ch: e.tensor_copy(
                    c[:, 0:3], halo_rx[:, h * 3:h * 3 + 3]), deps=(hmask, t))
            else:
                t = k.dma("y", ci, dr["zxb"][row0:row0 + 128, lt * 512 - 3:(lt + 1) * 512],
                          deps=zx_done + (prev.get(("cvslot", ch % 2)),),
                          slot=f"cv{ch % 2}")
            dst = xconv[:, ch, :] if ch < 16 else (Bconv if ch == 16 else Cconv)
            dep_dst = ydma.get((lt - 1, 3)) if ch < 16 else prev.get(("bcfree", ch))
            t0 = k.op("v", lambda e, c=ci, d=dst, h=ch: e.tensor_scalar(
                d, c[:, 0:512], k.sb["conv_w"][:, h, 0:1], None, ALU.mult),
                deps=(t, dep_dst))
            for tp in range(1, 4):
                t0 = k.op("v", lambda e, c=ci, d=dst, h=ch, tp_=tp:
                          e.scalar_tensor_tensor(d, c[:, tp_:tp_ + 512],
                                                 k.sb["conv_w"][:, h, tp_:tp_ + 1], d,
                                                 ALU.mult, ALU.add), deps=(t0,))
            prev[("cvslot", ch % 2)] = t0
            t1 = k.op("s", lambda e, d=dst, h=ch: e.activation(
                out=d, in_=d, func=AF.Silu, bias=k.sb["conv_b"][:, h:h + 1], scale=1.0),
                deps=(t0,))
            conv_done.append(t1)
        ccp = k.op("v", lambda e, l=lt: e.tensor_copy(C_res[:, l * 512:(l + 1) * 512], Cconv),
                   deps=(conv_done[17],))
        prev[("bcfree", 17)] = ccp

        t = k.dma("y", dtw[0], dr["zxb"][4352:4384, lt * 512:(lt + 1) * 512],
                  deps=zx_done + (prev.get("dtwfree"),), slot="dt")
        t = k.op("v", lambda e: e.tensor_scalar(dtw[0], dtw[0], k.sb["dt_bias"], None,
                                                ALU.add), deps=(t,))
        t = k.op("s", lambda e: e.activation(out=dtw[1], in_=dtw[0], func=AF.Exp),
                 deps=(t, prev.get("dtw1free")))
        tdt = k.op("s", lambda e: e.activation(out=dtw[0], in_=dtw[1], func=AF.Ln, bias=1.0),
                   deps=(t,))
        dtt = k.tr_tiles([(dtw[0][:, cc * 128:(cc + 1) * 128], dt_tok(cc), tdt)
                          for cc in range(4)], (0, 1))
        adt = k.op("v", lambda e: e.tensor_scalar(dtw[0], dtw[0], k.sb["A_neg"], None,
                                                  ALU.mult), deps=(dtt,))
        sc_toks = []
        for cc in range(4):
            z = lt * 4 + cc
            ac = Acum_res[:, lt * 512 + cc * 128:lt * 512 + (cc + 1) * 128]
            sc = k.op("v", lambda e, a=ac, c=cc: e.tensor_tensor_scan(
                a, ones32, dtw[0][:, c * 128:(c + 1) * 128], 0.0, ALU.mult, ALU.add),
                deps=(adt, t_o32, prev.get(("acfree", z - 16))))
            t2 = k.op("v", lambda e, zz=z: e.tensor_copy(CA_store[:, zz:zz + 1], CA_run),
                      deps=(sc, prev.get("car")))
            t2 = k.op("v", lambda e, a=ac: e.tensor_tensor(CA_run, CA_run, a[:, 127:128],
                                                           ALU.add), deps=(t2,))
            prev["car"] = t2
            dd = k.op("v", lambda e, a=ac, c=cc: e.scalar_tensor_tensor(
                dtw[1][:, c * 128:(c + 1) * 128], a, -1.0,
                a[:, 127:128].to_broadcast((32, 128)), ALU.mult, ALU.add), deps=(sc,))
            dd = k.op("s", lambda e, c=cc: e.activation(
                out=dtw[1][:, c * 128:(c + 1) * 128], in_=dtw[1][:, c * 128:(c + 1) * 128],
                func=AF.Exp), deps=(dd,))
            sc_toks.append((sc, dd))
        att = k.tr_tiles([(Acum_res[:, lt * 512 + cc * 128:lt * 512 + (cc + 1) * 128],
                           acum_tok(cc), sc_toks[cc][0]) for cc in range(4)], (0, 1))
        dct = k.tr_tiles([(dtw[1][:, cc * 128:(cc + 1) * 128], decay_tok(cc),
                           sc_toks[cc][1]) for cc in range(4)], (0, 1))
        prev["dtwfree"] = att
        prev["dtw1free"] = dct

        xt_last = None
        for ch in range(16):
            xt_last = k.tr_tiles(
                [(xconv[:, ch, cc * 128:(cc + 1) * 128],
                  xtok[:, cc, ch * 128:(ch + 1) * 128], conv_done[ch]) for cc in range(4)],
                (0, 1), deps=(prev.get(("xtokfree", lt - 1)),))
        xs_tok = {}
        for cc in range(4):
            xs_tok[cc] = k.op("v", lambda e, c=cc: e.tensor_tensor(
                xtok[:, c].rearrange("p (h q) -> p h q", h=32),
                xtok[:, c].rearrange("p (h q) -> p h q", h=32),
                dt_tok(c).unsqueeze(2).to_broadcast((128, 32, 64)), ALU.mult),
                deps=(xt_last, dtt))
        btt = k.tr_tiles([(Bconv[:, cc * 128:(cc + 1) * 128], Btok[:, cc], conv_done[16])
                          for cc in range(4)], (0, 1))
        prev[("bcfree", 16)] = btt

        for cc in range(4):
            z = lt * 4 + cc
            ac = Acum_res[:, lt * 512 + cc * 128:lt * 512 + (cc + 1) * 128]
            t = k.op("t", lambda e, a=ac: e.transpose(ps[7][0:1, 256:288], a[:, 127:128],
                                                      k.sb["ident_f"][0:32, 0:32]),
                     deps=(sc_toks[cc][0],) + k.bank_deps(7))
            k.bank_clear(7)
            t = k.rd(7, "v", lambda e: e.tensor_copy(lam_sb, ps[7][0:1, 256:288]),
                     deps=(t, prev.get("lamfree")))
            t = k.op("s", lambda e: e.activation(out=lam_sb, in_=lam_sb, func=AF.Exp),
                     deps=(t,))
            t = k.op("t", lambda e: e.matmul(ps[7][:, 288:320], ones_row, lam_sb,
                                             start=True, stop=True, skip_group_check=True),
                     deps=(t,) + k.bank_deps(7))
            k.bank_clear(7)
            lamrep = k.rd(7, "v", lambda e: e.tensor_copy(LamRep, ps[7][:, 288:320]),
                          deps=(t, prev.get("lamrepfree")))
            prev["lamfree"] = lamrep
            g = k.op("t", lambda e, c=cc, l=lt: e.matmul(
                ps[7][:, 0:128], Bconv[:, c * 128:(c + 1) * 128],
                C_res[:, l * 512 + c * 128:l * 512 + (c + 1) * 128], start=True, stop=True,
                skip_group_check=True),
                deps=(conv_done[16], ccp) + k.bank_deps(7))
            k.bank_clear(7)
            gev = k.rd(7, "v", lambda e: e.tensor_tensor(Gm_t, ps[7][:, 0:128],
                                                         k.sb["tri_f"], ALU.mult),
                       deps=(g, prev.get("gmfree")))
            eat = k.op("s", lambda e, a=ac: e.activation(out=expA, in_=a, func=AF.Exp),
                       deps=(sc_toks[cc][0], prev.get("expAfree")))
            ac_t = {}
            ev_j = {}
            gl_last = None
            for j in range(16):
                if j % 4 == 0:
                    hg2 = j // 4
                    ac_t[hg2] = k.dma(
                        "g", acrow[hg2 % 2],
                        Acum_res[hg2 * 8:hg2 * 8 + 8,
                                 lt * 512 + cc * 128:lt * 512 + (cc + 1) * 128],
                        deps=(sc_toks[cc][0], prev.get(("acrowuse", hg2 % 2))),
                        slot=f"acr{hg2 % 2}")
                b = 2 + (j % 4)
                bank = ps[b]
                sl = slice((j // 4) * 128, (j // 4) * 128 + 128)
                yo = k.mm(b, bank[:, sl], S[:, j * 128:(j + 1) * 128],
                          C_res[:, lt * 512 + cc * 128:lt * 512 + (cc + 1) * 128],
                          j < 4, False,
                          deps=(ccp, s_zero, supd.get((z - 1, "done")),), skip=True)
                pj = 0
                ea_ps = ps[7]
                ea_bk = 7
                ea_reg = slice(128, 256)
                ea_buf = eaF
                ef = k.op("t", lambda e, jj=j, r=ea_reg, p_=ea_ps: e.matmul(
                    p_[:, r], k.sb["ind_hp"][:, jj * 128:(jj + 1) * 128], expA,
                    start=True, stop=True, skip_group_check=True),
                    deps=(eat, prev.get(("eaRfree", pj))) +
                    (k.bank_deps(7) if j == 0 else ()))
                efe = k.rd(ea_bk, "v", lambda e, r=ea_reg, eb=ea_buf, p_=ea_ps:
                           e.tensor_copy(eb, p_[:, r]),
                           deps=(ef, prev.get(("eaFfree", pj))))
                prev[("eaRfree", pj)] = efe
                fm = k.rd(b, "v", lambda e, bb=bank, s_=sl, eb=ea_buf: e.tensor_tensor(
                    bb[:, s_], bb[:, s_], eb, ALU.mult), deps=(yo, efe))
                prev[("eaFfree", pj)] = fm
                for hh in range(2):
                    h = 2 * j + hh
                    hg, hl = divmod(h, 8)
                    ph = h % 2
                    Lb = L_t if ph == 0 else L_t2
                    GLb = GL_t if ph == 0 else GL_t2
                    t6 = slice(0, 128) if ph == 0 else slice(256, 384)
                    tb = k.op("t", lambda e, a=acrow[hg % 2], hl_=hl, r=t6: e.matmul(
                        ps[6][:, r], ones_row, a[:, hl_ * 128:(hl_ + 1) * 128],
                        start=True, stop=True, skip_group_check=True),
                        deps=(ac_t[hg], prev.get(("t6free", ph))) +
                        (k.bank_deps(6) if j == 0 and hh == 0 else ()))
                    prev[("acrowuse", hg % 2)] = tb
                    lv = k.rd(6, "s", lambda e, c=cc, h_=h, r=t6, L_=Lb: e.activation(
                        out=L_, in_=ps[6][:, r], func=AF.Relu, scale=-1.0,
                        bias=acum_tok(c)[:, h_:h_ + 1]),
                        deps=(tb, att, prev.get(("ltfree", ph))))
                    prev[("t6free", ph)] = lv
                    lv = k.op("s", lambda e, L_=Lb: e.activation(out=L_, in_=L_,
                                                                 func=AF.Exp,
                                                                 scale=-1.0), deps=(lv,))
                    gl = k.op("v", lambda e, L_=Lb, G_=GLb: e.tensor_tensor(
                        G_, Gm_t, L_, ALU.mult),
                        deps=(lv, gev, prev.get(("glfree", ph))))
                    prev[("ltfree", ph)] = gl
                    yd = k.op("t", lambda e, bb=bank, s_=sl, c=cc, h_=h, pr=hh,
                              lst=(hh == 1), G_=GLb: e.matmul(
                                  bb[pr * 64:pr * 64 + 64, s_],
                                  xtok[:, c, h_ * 64:(h_ + 1) * 64], G_,
                                  start=False, stop=lst, skip_group_check=True),
                              deps=(fm, gl, xs_tok[cc]))
                    prev[("glfree", ph)] = yd
                    gl_last = yd
                ev_j[j] = k.rd(b, "v", lambda e, bb=bank, s_=sl, jj=j, c=cc:
                               e.scalar_tensor_tensor(
                                   xconv[:, jj, c * 128:(c + 1) * 128],
                                   xconv[:, jj, c * 128:(c + 1) * 128],
                                   k.sb["d_exp"][:, jj:jj + 1], bb[:, s_], ALU.mult,
                                   ALU.add), deps=(gl_last,))
            prev["gmfree"] = ev_j[15]
            prev["expAfree"] = ev_j[15]
            for j in range(16):
                prev[("evj", cc, j)] = ev_j[j]
            if cc == 3:
                ydma[(lt, 3)] = k.dma("s", dr["ystage"][lt], Bm[:, 1030:9222], slot="yst",
                                      deps=tuple(prev[("evj", cc2, j)]
                                                 for cc2 in range(4) for j in range(16)))
            xw = k.op("v", lambda e, c=cc: e.tensor_tensor(
                xtok[:, c].rearrange("p (h q) -> p h q", h=32),
                xtok[:, c].rearrange("p (h q) -> p h q", h=32),
                decay_tok(c).unsqueeze(2).to_broadcast((128, 32, 64)), ALU.mult),
                deps=(dct, gl_last))
            tsc = k.op("v", lambda e: e.tensor_tensor(
                S.rearrange("p (h q) -> p h q", h=32),
                S.rearrange("p (h q) -> p h q", h=32),
                LamRep[:, 0:32].unsqueeze(2).to_broadcast((128, 32, 64)), ALU.mult),
                deps=(lamrep, ev_j[15]))
            prev["lamrepfree"] = tsc
            sadd = None
            for sb2 in range(4):
                b = sb2 % 2
                st = k.mm(b, ps[b][:], Btok[:, cc], xtok[:, cc, sb2 * 512:(sb2 + 1) * 512],
                          True, True, deps=(btt, xw))
                sadd = k.rd(b, "v", lambda e, bb=ps[b], s2=sb2: e.tensor_tensor(
                    S[:, s2 * 512:(s2 + 1) * 512], S[:, s2 * 512:(s2 + 1) * 512], bb[:],
                    ALU.add), deps=(st, tsc))
                supd[(z, sb2)] = sadd
            supd[(z, "done")] = sadd
            prev[("acfree", z)] = sadd
        prev[("xtokfree", lt)] = supd[(lt * 4 + 3, "done")]

    t = k.dma("g", dr["ag_s_in"], S, deps=(supd[(15, "done")],), slot="sst")
    scc = k.coll("AllGather", ALU.bypass, PAIRS, dr["ag_s_in"], dr["ag_s_out"], deps=(t,))
    t = k.dma("g", S, dr["ag_s_out"][0], deps=(scc,), slot="sst")
    sprev_tok = k.op("v", lambda e: e.tensor_scalar(S, S, k.sb["is_odd"], None, ALU.mult),
                     deps=(t,))

    Yt = Bm[:, 0:8192].rearrange("p (c l) -> p c l", c=16)
    zring = Bm[:, 8192:8704]
    sqring = Bm[:, 8704:9216]
    wout = [Bm[:, 9216 + i * 1024:9216 + (i + 1) * 1024] for i in range(2)]
    tmp_row = Bm[0:1, 11264:11776]
    m_row = Dm[0:1, 6144:6656]
    q_row = Dm[0:1, 6656:7168]
    r_row = Dm[0:1, 7252:7764]
    x1 = uT
    k.ln_rows = (m_row, q_row, r_row, tmp_row)

    for lt in range(NLT):
        yld0 = k.dma("y", Bm[:, 0:8192], dr["ystage"][lt], slot="yld",
                     deps=(ydma[(lt, 3)], ydma[(3, 3)], supd[(15, "done")], sprev_tok) +
                     tuple(prev.get(("yfree", lt - 1, jj)) for jj in range(16)))
        yld = {jj: yld0 for jj in range(16)}
        for cc in range(4):
            z = lt * 4 + cc
            ac = Acum_res[:, lt * 512 + cc * 128:lt * 512 + (cc + 1) * 128]
            eag = k.op("s", lambda e, a=ac, zz=z: e.activation(
                out=expA, in_=a, func=AF.Exp, bias=CA_store[:, zz:zz + 1]),
                deps=(prev.get("expAfree"), sprev_tok))
            for j in range(16):
                b = 2 + (j % 4)
                bank = ps[b]
                sl = slice((j // 4) * 128, (j // 4) * 128 + 128)
                cm = k.mm(b, bank[:, sl], S[:, j * 128:(j + 1) * 128],
                          C_res[:, lt * 512 + cc * 128:lt * 512 + (cc + 1) * 128],
                          j < 4, j >= 12, deps=(sprev_tok,), skip=True)
                pj = 0
                ea_ps = ps[7]
                ea_bk = 7
                ea_reg = slice(128, 256)
                ea_buf = eaF
                ef = k.op("t", lambda e, jj=j, r=ea_reg, p_=ea_ps: e.matmul(
                    p_[:, r], k.sb["ind_hp"][:, jj * 128:(jj + 1) * 128], expA,
                    start=True, stop=True, skip_group_check=True),
                    deps=(eag, prev.get(("eaRfree", pj))) +
                    (k.bank_deps(7) if j == 0 else ()))
                efe = k.rd(ea_bk, "v", lambda e, r=ea_reg, eb=ea_buf, p_=ea_ps:
                           e.tensor_copy(eb, p_[:, r]),
                           deps=(ef, prev.get(("eaFfree", pj))))
                prev[("eaRfree", pj)] = efe
                fm = k.rd(b, "v", lambda e, bb=bank, s_=sl, eb=ea_buf: e.tensor_tensor(
                    bb[:, s_], bb[:, s_], eb, ALU.mult), deps=(cm, efe))
                prev[("eaFfree", pj)] = fm
                t = k.rd(b, "v", lambda e, bb=bank, s_=sl, jj=j, c=cc: e.tensor_tensor(
                    Yt[:, jj, c * 128:(c + 1) * 128], Yt[:, jj, c * 128:(c + 1) * 128],
                    bb[:, s_], ALU.add), deps=(fm, yld[j]))
                prev[("corr", z, j)] = t
            prev["expAfree"] = prev[("corr", z, 15)]
        corr_done = [prev[("corr", lt * 4 + 3, j)] for j in range(16)]
        gat = {}
        mmsq = None
        for jj in range(16):
            t = k.dma("y", zring, dr["zxb"][jj * 128:(jj + 1) * 128,
                                            lt * 512:(lt + 1) * 512],
                      deps=zx_done + (gat.get(jj - 1),), slot="zld")
            t = k.op("s", lambda e: e.activation(out=zring, in_=zring, func=AF.Silu),
                     deps=(t,))
            gat[jj] = k.op("v", lambda e, j_=jj: e.tensor_tensor(Yt[:, j_], Yt[:, j_],
                                                                 zring, ALU.mult),
                           deps=(t, corr_done[jj]))
            sq = k.op("s", lambda e, j_=jj: e.activation(out=sqring, in_=Yt[:, j_],
                                                         func=AF.Square),
                      deps=(gat[jj], mmsq))
            mmsq = k.mm(6, ps[6][0:1, 0:512], ones_col, sqring, jj == 0, jj == 15,
                        deps=(sq,), skip=True)
        t = k.rd(6, "v", lambda e: e.tensor_scalar(m_row, ps[6][0:1, 0:512], 1.0 / 2048.0,
                                                   None, ALU.mult),
                 deps=(mmsq, prev.get("mrowfree")))
        rst = k.newton_rsqrt(r_row, m_row, tmp_row, t, 1e-5)
        prev["mrowfree"] = rst
        rbc = k.mm(6, ps[6][:, 0:512], ones_row, r_row, True, True, deps=(rst,))
        app = {}
        for jj in range(16):
            app[jj] = k.rd(6, "v", lambda e, j_=jj: e.scalar_tensor_tensor(
                Yt[:, j_], Yt[:, j_], k.sb["normw_col"][:, j_:j_ + 1], ps[6][:, 0:512],
                ALU.mult, ALU.mult), deps=(rbc, gat[jj]))
        wo_tok = [None, None]
        wo_use = [None, None]
        opmm = None
        for ch in range(16):
            wo_tok[ch % 2] = k.dma("y", wout[ch % 2], dr["w_out_t"][ch],
                                   deps=(wo_use[ch % 2], supd[(15, "done")]),
                                   slot=f"wo{ch % 2}")
            for dc in range(7):
                k.mmq(dc, ps[dc][:], wout[ch % 2][:, dc * 128:(dc + 1) * 128], Yt[:, ch],
                      ch == 0, ch == 15,
                      deps=(wo_tok[ch % 2], app[ch]) +
                      ((app[15],) if ch == 0 and dc >= 6 else ()))
            wo_use[ch % 2] = k.mm(7, ps[7][:], wout[ch % 2][:, 7 * 128:8 * 128], Yt[:, ch],
                                  ch == 0, ch == 15,
                                  deps=(wo_tok[ch % 2], app[ch]) +
                                  ((app[15],) if ch == 0 else ()))
            if ch == 15:
                opmm = wo_use[ch % 2]
        for jj in range(16):
            prev[("yfree", lt, jj)] = opmm
        ln_feature_major(k, lambda dc, l=lt: x1[:, dc, l * 512:(l + 1) * 512],
                         lambda dc: ps[dc][:], opmm, lt, prev, "m3", gptmp)
    k.x1_ready = prev[("lnap", "m3", NLT - 1)]
    k.mprev = prev


def ln_feature_major(k, res_ap, psum_ap, mm_tok, lt, prev, tag, sqtile):
    ps = k.psum
    ones_row, ones_col = k.sb["ones_row_f"], k.sb["ones_col"]
    m_row, q_row, r_row, tmp_row = k.ln_rows
    radd = {}
    for dc in range(8):
        radd[dc] = k.rd(dc, "v", lambda e, d=dc: e.tensor_tensor(
            res_ap(d), res_ap(d), psum_ap(d), ALU.add), deps=(mm_tok,))
    smm = None
    qmm = None
    sqt = None
    for dc in range(8):
        smm = k.mm(6, ps[6][0:1, 0:512], ones_col, res_ap(dc), dc == 0, dc == 7,
                   deps=(radd[dc],), skip=True)
        sqt = k.op("s", lambda e, d=dc: e.activation(out=sqtile, in_=res_ap(d),
                                                     func=AF.Square), deps=(radd[dc], qmm))
        qmm = k.mm(7, ps[7][0:1, 0:512], ones_col, sqtile, dc == 0, dc == 7,
                   deps=(sqt,), skip=True)
    t = k.rd(6, "v", lambda e: e.tensor_scalar(m_row, ps[6][0:1, 0:512], 1.0 / 1024.0, None,
                                               ALU.mult), deps=(smm, prev.get("lnmfree")))
    t2 = k.rd(7, "v", lambda e: e.tensor_scalar(q_row, ps[7][0:1, 0:512], 1.0 / 1024.0, None,
                                                ALU.mult), deps=(qmm, prev.get("lnqfree")))
    t3 = k.op("v", lambda e: e.tensor_tensor(tmp_row, m_row, m_row, ALU.mult), deps=(t,))
    t3 = k.op("v", lambda e: e.tensor_tensor(q_row, q_row, tmp_row, ALU.subtract),
              deps=(t3, t2))
    rst = k.newton_rsqrt(r_row, q_row, tmp_row, t3, 1e-5)
    tb = k.mm(6, ps[6][:, 0:512], ones_row, m_row, True, True, deps=(rst,))
    tb2 = k.mm(7, ps[7][:, 0:512], ones_row, r_row, True, True, deps=(rst,))
    lnap = None
    for dc in range(8):
        t4 = k.rd(6, "v", lambda e, d=dc: e.tensor_tensor(
            res_ap(d), res_ap(d), ps[6][:, 0:512], ALU.subtract), deps=(tb, radd[dc]))
        lnap = k.rd(7, "v", lambda e, d=dc: e.tensor_tensor(
            res_ap(d), res_ap(d), ps[7][:, 0:512], ALU.mult), deps=(tb2, t4))
        prev[("lnapc", tag, lt, dc)] = lnap
    prev["lnmfree"] = lnap
    prev["lnqfree"] = lnap
    prev[("lnap", tag, lt)] = lnap
    return lnap


def l0ffn_phase(k):
    p, dr, ps = k.p, k.dr, k.psum
    A, Bm, C, Dm, E = k.A, k.Bm, k.C, k.Dm, k.E
    prev = dict(k.mprev) if hasattr(k, "mprev") else {}
    prev = {}
    x1 = A.rearrange("p (c l) -> p c l", c=8)
    w1sl = [Bm[:, i * 1024:(i + 1) * 1024].rearrange("p (c m) -> p c m", c=8)
            for i in range(2)]
    w2sl = [Bm[:, 2048 + i * 2048:2048 + (i + 1) * 2048].rearrange("p (c m) -> p c m", c=16)
            for i in range(2)]
    outacc = Bm[:, 6144:10240].rearrange("p (c l) -> p c l", c=8)
    vt = Bm[:, 10240:10752]
    st_ = Bm[:, 10752:11264]
    ktf = Bm[:, 11264:11776]
    kt = ktf.bitcast(I32)[:, 0:512]
    asx = Bm[:, 11776:12288]
    act = C.rearrange("p (c l) -> p c l", c=16)
    x2st = Dm[:, 0:4096].rearrange("p (c l) -> p c l", c=8)
    k.ln_rows = (Dm[0:1, 4096:4608], Dm[0:1, 4608:5120], Dm[0:1, 5120:5632],
                 Dm[0:1, 5632:6144])

    x2dma = {}
    for lt in range(NLT):
        for fh in range(2):
            actuse = prev.get(("actuse", fh))
            for ff in range(16):
                f = fh * 16 + ff
                w1t = k.dma("y", Bm[:, (ff % 2) * 1024:(ff % 2) * 1024 + 1024],
                            dr["w1"][f], deps=(k.x1_ready, prev.get(("w1use", ff % 2))),
                            slot=f"w1{ff % 2}")
                b = ff % 2
                for kc in range(7):
                    k.mmq(b, ps[b][:], w1sl[ff % 2][:, kc, :],
                          x1[:, kc, lt * 512:(lt + 1) * 512], kc == 0, False,
                          deps=(w1t,))
                hmm = k.mm(b, ps[b][:], w1sl[ff % 2][:, 7, :],
                           x1[:, 7, lt * 512:(lt + 1) * 512], False, True, deps=(w1t,))
                prev[("w1use", ff % 2)] = hmm
                t = k.rd(b, "v", lambda e, bb=ps[b], f_=f: e.tensor_scalar(
                    vt, bb[:], 0.1 * INV_2PI, k.sb["thp_col"][:, f_:f_ + 1], ALU.mult,
                    ALU.add), deps=(hmm, prev.get("vtfree")))
                t = k.op("v", lambda e: e.tensor_scalar(ktf, vt, MAGIC, MAGIC, ALU.add,
                                                        ALU.subtract),
                         deps=(t, prev.get("ktfree")))
                t = k.op("v", lambda e: e.tensor_copy(kt, ktf), deps=(t,))
                t = k.op("v", lambda e: e.scalar_tensor_tensor(st_, kt, -1.0, vt, ALU.mult,
                                                               ALU.add),
                         deps=(t, prev.get("stfree")))
                prev["ktfree"] = t
                t = k.op("s", lambda e: e.activation(out=st_, in_=st_, func=AF.Sin,
                                                     scale=TWO_PI), deps=(t,))
                t = k.op("v", lambda e, f_=f: e.tensor_scalar(
                    asx, st_, k.sb["alpha_col"][:, f_:f_ + 1], None, ALU.mult),
                    deps=(t, prev.get("asxfree")))
                t = k.rd(b, "v", lambda e, bb=ps[b]: e.scalar_tensor_tensor(
                    st_, bb[:], 1.0, asx, ALU.mult, ALU.add), deps=(t,))
                prev["vtfree"] = t
                prev["asxfree"] = t
                t = k.op("s", lambda e, f_=f, ff_=ff: e.activation(
                    out=act[:, ff_], in_=st_, func=AF.Relu,
                    bias=k.sb["b1_col"][:, f_:f_ + 1]), deps=(t, actuse))
                prev["stfree"] = t
                prev[("act", ff)] = t
            w2t = [None, None]
            w2use = [None, None]
            last_mm = None
            for dc in range(8):
                w2t[dc % 2] = k.dma("y", Bm[:, 2048 + (dc % 2) * 2048:
                                            2048 + (dc % 2) * 2048 + 2048],
                                    dr["w2"][fh][dc], deps=(w2use[dc % 2],),
                                    slot=f"w2{dc % 2}")
                b = 2 + (dc % 2)
                for ff in range(15):
                    k.mmq(b, ps[b][:], w2sl[dc % 2][:, ff, :], act[:, ff],
                          ff == 0, False, deps=(w2t[dc % 2], prev[("act", ff)]))
                last_mm = k.mm(b, ps[b][:], w2sl[dc % 2][:, 15, :], act[:, 15],
                               False, True, deps=(w2t[dc % 2], prev[("act", 15)]))
                w2use[dc % 2] = last_mm
                if fh == 0:
                    t = k.rd(b, "v", lambda e, bb=ps[b], d=dc: e.tensor_copy(
                        outacc[:, d], bb[:]), deps=(last_mm, prev.get(("oafree", dc))))
                else:
                    t = k.rd(b, "v", lambda e, bb=ps[b], d=dc: e.tensor_tensor(
                        outacc[:, d], outacc[:, d], bb[:], ALU.add),
                        deps=(last_mm, prev.get(("oafree", dc))))
                prev[("oacc", fh, dc)] = t
            prev[("actuse", fh)] = last_mm
        radd_dep = [prev[("oacc", 1, dc)] for dc in range(8)]
        for dc in range(8):
            t = k.op("v", lambda e, d=dc, l=lt: e.scalar_tensor_tensor(
                outacc[:, d], outacc[:, d], k.sb["b2_col"][:, d:d + 1],
                x1[:, d, l * 512:(l + 1) * 512], ALU.add, ALU.add), deps=(radd_dep[dc],))
            prev[("oafree", dc)] = t
            radd_dep[dc] = t
        lnin = ln_ffn(k, outacc, radd_dep, prev, lt)
        cps = []
        for dc in range(8):
            t = k.op("v", lambda e, d=dc, l=lt: e.tensor_copy(
                x1[:, d, l * 512:(l + 1) * 512], outacc[:, d]), deps=(lnin[dc],))
            cps.append(t)
            prev[("oafree", dc)] = t
        x2dma[lt] = k.dma("s", dr["x2own"][lt], Bm[:, 6144:10240], deps=tuple(cps), slot="x2st")
        for dc in range(8):
            prev[("oafree", dc)] = x2dma[lt]
    k.x2_ready = x2dma[NLT - 1]
    k.x2dma = x2dma


def ln_ffn(k, res, radd_dep, prev, lt, res_fn=None):
    if res_fn is None:
        res_fn = lambda dc: res[:, dc]
    ps = k.psum
    ones_row, ones_col = k.sb["ones_row_f"], k.sb["ones_col"]
    m_row, q_row, r_row, tmp_row = k.ln_rows
    sqtile = k.E[:, k.eoff + 1024:k.eoff + 1536]
    smm = None
    qmm = None
    for dc in range(8):
        smm = k.mm(6, ps[6][0:1, 0:512], ones_col, res_fn(dc), dc == 0, dc == 7,
                   deps=(radd_dep[dc],), skip=True)
        sqt = k.op("s", lambda e, d=dc: e.activation(out=sqtile, in_=res_fn(d),
                                                     func=AF.Square),
                   deps=(radd_dep[dc], qmm))
        qmm = k.mm(7, ps[7][0:1, 0:512], ones_col, sqtile, dc == 0, dc == 7,
                   deps=(sqt,), skip=True)
    t = k.rd(6, "v", lambda e: e.tensor_scalar(m_row, ps[6][0:1, 0:512], 1.0 / 1024.0, None,
                                               ALU.mult), deps=(smm, prev.get("lnmfree")))
    t2 = k.rd(7, "v", lambda e: e.tensor_scalar(q_row, ps[7][0:1, 0:512], 1.0 / 1024.0,
                                                None, ALU.mult),
              deps=(qmm, prev.get("lnqfree")))
    t3 = k.op("v", lambda e: e.tensor_tensor(tmp_row, m_row, m_row, ALU.mult), deps=(t,))
    t3 = k.op("v", lambda e: e.tensor_tensor(q_row, q_row, tmp_row, ALU.subtract),
              deps=(t3, t2))
    rst = k.newton_rsqrt(r_row, q_row, tmp_row, t3, 1e-5)
    tb = k.mm(6, ps[6][:, 0:512], ones_row, m_row, True, True, deps=(rst,))
    tb2 = k.mm(7, ps[7][:, 0:512], ones_row, r_row, True, True, deps=(rst,))
    outs = []
    for dc in range(8):
        t4 = k.rd(6, "v", lambda e, d=dc: e.tensor_tensor(
            res_fn(d), res_fn(d), ps[6][:, 0:512], ALU.subtract), deps=(tb, radd_dep[dc]))
        t5 = k.rd(7, "v", lambda e, d=dc: e.tensor_tensor(
            res_fn(d), res_fn(d), ps[7][:, 0:512], ALU.mult), deps=(tb2, t4))
        outs.append(t5)
    prev["lnmfree"] = outs[-1]
    prev["lnqfree"] = outs[-1]
    return outs


def attn_phase(k):
    p, dr, ps = k.p, k.dr, k.psum
    A, Bm, C, Dm, E = k.A, k.Bm, k.C, k.Dm, k.E
    prev = {}
    eo = k.eoff
    xcc = k.coll("AllGather", ALU.bypass, PAIRS, dr["x2own"], dr["ag_x_out"],
                 deps=(k.x2_ready,))
    xa = A.rearrange("p (b c l) -> p b c l", b=4, c=8)
    xc = C.rearrange("p (b c l) -> p b c l", b=2, c=8)
    xd = Dm.rearrange("p (b c l) -> p b c l", b=2, c=8)
    ldx = tuple(
        k.dma("y",
              (A[:, i * 4096:(i + 1) * 4096] if i < 4 else
               (C[:, (i - 4) * 4096:(i - 3) * 4096] if i < 6 else
                Dm[:, (i - 6) * 4096:(i - 5) * 4096])),
              dr["ag_x_out"][i // 4][i % 4], deps=(xcc,), slot=f"xf{i % 4}")
        for i in range(8))

    def x2ap(kc, blk):
        if blk < 4:
            return xa[:, blk, kc, :]
        if blk < 6:
            return xc[:, blk - 4, kc, :]
        return xd[:, blk - 6, kc, :]

    qt = Bm[:, 0:4096]
    ktile = Bm[:, 4096:8192]
    vtok = Bm[:, 8192:12352].rearrange("p (c m) -> p c m", c=32)
    io = k.indoff
    wslab = E[:, io:io + 1024].rearrange("p (c m) -> p c m", c=8)
    wslab_flat = E[:, io:io + 1024]
    Pt = [E[:, io + 1024:io + 1536], E[:, io + 1536:io + 2048]]
    den_c = E[:, eo:eo + 1]
    att_dma = {}

    vones = k.op("v", lambda e: e.memset(
        vtok[:, :, 64:65].rearrange("p c m -> p (c m)"), 1.0))
    vones2 = k.op("v", lambda e: e.memset(
        vtok[:, :, 129:130].rearrange("p c m -> p (c m)"), 1.0))

    for hp in range(4):
        vev = {}
        wld = k.dma("y", wslab_flat, dr["wv_t"][hp],
                    deps=ldx + (prev.get("wslab_use"),), slot="wsl")
        for blk in range(8):
            b = blk % 2
            for kc in range(7):
                k.mmq(b, ps[b][:], wslab[:, kc, :], x2ap(kc, blk), kc == 0, False,
                      deps=(wld,))
            mt = k.mm(b, ps[b][:], wslab[:, 7, :], x2ap(7, blk), False, True,
                      deps=(wld,))
            vev[blk] = k.rd(b, "v", lambda e, bb=ps[b], b2=blk: e.tensor_copy(
                qt[:, b2 * 512:(b2 + 1) * 512], bb[:]),
                deps=(mt, prev.get(("qfree", hp))))
            if blk == 7:
                prev["wslab_use"] = mt
        vtt = None
        for i in range(32):
            b = 2 + (i % 2)
            dep = (vev[i // 4],) + k.bank_deps(b) + (vones, vones2,
                                                     prev.get(("vtokfree", hp)))
            k.bank_clear(b)
            tt = k.op("t", lambda e, s=qt[:, i * 128:(i + 1) * 128], b3=ps[b]:
                      e.transpose(b3[:, 0:128], s, k.sb["ident_f"]), deps=dep)
            for hh in range(2):
                vtt = k.rd(b, "v", lambda e, b3=ps[b], i_=i, h_=hh: e.tensor_copy(
                    vtok[:, i_, h_ * 65:h_ * 65 + 64], b3[:, h_ * 64:h_ * 64 + 64]),
                    deps=(tt,))
        qev = {}
        kev = {}
        for (dst, wnm, evd) in ((qt, "wq_t", qev), (ktile, "wk_t", kev)):
            wld = k.dma("y", wslab_flat, dr[wnm][hp],
                        deps=((vtt,) if dst is qt else ()) + (prev["wslab_use"],),
                        slot="wsl")
            for blk in range(8):
                b = blk % 2
                for kc in range(7):
                    k.mmq(b, ps[b][:], wslab[:, kc, :], x2ap(kc, blk), kc == 0, False,
                          deps=(wld,) + ((vtt,) if dst is qt else
                                         (prev.get(("kfree", hp)),)))
                mt = k.mm(b, ps[b][:], wslab[:, 7, :], x2ap(7, blk), False, True,
                          deps=(wld,) + ((vtt,) if dst is qt else
                                         (prev.get(("kfree", hp)),)))
                evd[blk] = k.rd(b, "v", lambda e, bb=ps[b], b2=blk, d2=dst: e.tensor_copy(
                    d2[:, b2 * 512:(b2 + 1) * 512], bb[:]), deps=(mt,))
                if blk == 7:
                    prev["wslab_use"] = mt
        qk_done = tuple(qev.values()) + tuple(kev.values())

        for lb in range(8):
            nsc = lb * 4 + 4
            scb = [(0, 1), (2, 3)]
            pev = {}
            avlast = {0: None, 1: None}
            pmuse = {}
            for sc in range(nsc):
                for hh in range(2):
                    b = scb[sc % 2][hh]
                    diag = lb * 512 <= sc * 128 < (lb + 1) * 512
                    smm = k.mm(b, ps[b][:],
                               ktile[hh * 64:hh * 64 + 64, sc * 128:(sc + 1) * 128],
                               qt[hh * 64:hh * 64 + 64, lb * 512:(lb + 1) * 512],
                               True, True, deps=qk_done)
                    if diag:
                        off = sc * 128 - lb * 512
                        smm = k.op("t", lambda e, bb=ps[b], o=off: e.matmul(
                            bb[:, o:o + 128], k.sb["ident_b"], k.sb["mask_b"],
                            start=False, stop=True, skip_group_check=True), deps=())
                    pev[(sc, hh)] = k.rd(b, "s", lambda e, bb=ps[b], h_=hh: e.activation(
                        out=Pt[h_], in_=bb[:], func=AF.Exp, scale=0.125),
                        deps=(smm, pmuse.get((sc - 1, hh)),
                              (prev.get("last_att") if sc == 0 else None),
                              (prev.get(("ptuse", hh)) if sc == 0 else None)))
                for hh in range(2):
                    bav = 6 + hh
                    last = None
                    for j in range(4):
                        if lb * 4 + j < sc:
                            continue
                        last = k.mm(bav, ps[bav][:, j * 65:j * 65 + 65],
                                    Pt[hh][:, j * 128:(j + 1) * 128],
                                    vtok[:, sc, hh * 65:hh * 65 + 65],
                                    sc == 0 and j == 0, False,
                                    deps=(pev[(sc, hh)], vtt), skip=True)
                    pmuse[(sc, hh)] = last
                    avlast[hh] = last
            out_sl = Pt[0]
            outw = []
            for hh in range(2):
                bav = 6 + hh
                for j in range(4):
                    t = k.rd(bav, "v", lambda e, bb=ps[bav], j_=j: e.tensor_scalar(
                        den_c, bb[:, j_ * 65 + 64:j_ * 65 + 65], 1e-8, None, ALU.add),
                        deps=(avlast[hh], pev[(nsc - 1, 0)], pev[(nsc - 1, 1)],
                              prev.get("denfree")))
                    t = k.op("v", lambda e: e.reciprocal(den_c, den_c), deps=(t,))
                    t = k.rd(bav, "v", lambda e, bb=ps[bav], j_=j, h_=hh:
                             e.tensor_scalar(
                                 out_sl[:, j_ * 128 + h_ * 64:j_ * 128 + h_ * 64 + 64],
                                 bb[:, j_ * 65:j_ * 65 + 64], den_c, None, ALU.mult),
                             deps=(t,))
                    prev["denfree"] = t
                    outw.append(t)
            att_dma[(hp, lb)] = k.dma(
                "s", dr["attn_heads"][hp][lb], out_sl, deps=tuple(outw),
                slot=f"att{lb % 2}")
            prev[("ptuse", 0)] = avlast[0]
            prev[("ptuse", 1)] = avlast[1]
            prev["last_att"] = att_dma[(hp, lb)]
        prev[("qfree", hp)] = att_dma[(hp, 7)]
        prev[("kfree", hp)] = att_dma[(hp, 7)]
        prev[("vtokfree", hp)] = att_dma[(hp, 7)]

    attn_outproj(k, att_dma[(3, 7)])


def attn_outproj(k, attn_done):
    p, dr, ps = k.p, k.dr, k.psum
    A, Bm, C, Dm, E = k.A, k.Bm, k.C, k.Dm, k.E
    prev = {}
    acc = k.coll("AllGather", ALU.bypass, PAIRS, dr["attn_heads"], dr["ag_a_out"],
                 deps=(attn_done,))
    wo = Bm[:, 0:8192].rearrange("p (c m) -> p c m", c=8)
    attT4 = Bm[:, 8192:12288].rearrange("p (c m) -> p c m", c=8)
    cand = C.rearrange("p (a h l) -> p a h l", a=4, h=4)
    outacc = Dm[:, 0:4096].rearrange("p (c l) -> p c l", c=8)
    x2res = Dm[:, 4096:8192].rearrange("p (c l) -> p c l", c=8)
    x3 = A.rearrange("p (c l) -> p c l", c=8)
    io = k.indoff
    k.ln_rows = (E[0:1, io:io + 512], E[0:1, io + 512:io + 1024],
                 E[0:1, io + 1024:io + 1536], E[0:1, io + 1536:io + 2048])
    wotok = []
    for c in range(8):
        wotok.append(k.dma("y", Bm[:, c * 1024:(c + 1) * 1024], dr["wo_t"][c],
                           deps=(attn_done,), slot=f"wo{c % 2}"))
    woldt = (wotok[-1], wotok[-2])

    for lbo in range(4):
        lds = {}
        for a, (slot, lbc) in enumerate(((0, lbo), (1, lbo), (0, lbo + 4), (1, lbo + 4))):
            for hp in range(4):
                lds[(a, hp)] = k.dma("y", cand[:, a, hp, :], dr["ag_a_out"][slot][hp][lbc],
                                     deps=(acc, prev.get(("canduse", a, hp))),
                                     slot=f"cnd{a}")
        bl = {}
        for slot in range(2):
            for hp in range(4):
                a0, a1 = slot, slot + 2
                t = k.op("v", lambda e, a0_=a0, a1_=a1, h=hp: e.tensor_tensor(
                    cand[:, a1_, h, :], cand[:, a1_, h, :], cand[:, a0_, h, :],
                    ALU.subtract), deps=(lds[(a0, hp)], lds[(a1, hp)],
                                         prev.get(("canduse", slot, hp))))
                t = k.op("v", lambda e, a0_=a0, a1_=a1, h=hp: e.scalar_tensor_tensor(
                    cand[:, a1_, h, :], cand[:, a1_, h, :], k.sb["is_odd"],
                    cand[:, a0_, h, :], ALU.mult, ALU.add), deps=(t,))
                bl[slot * 4 + hp] = t
        tt = k.tr_tiles(
            [(cand[:, 2 + c // 4, c % 4, rr * 128:(rr + 1) * 128],
              attT4[:, c, rr * 128:(rr + 1) * 128], bl[c])
             for rr in range(4) for c in range(8)], (2, 3), deps=(prev.get("attTuse"),))
        for a in range(4):
            for hp in range(4):
                prev[("canduse", a, hp)] = tt
        opmm = None
        for dcp in range(4):
            mt = {}
            for half in range(2):
                dc = 2 * dcp + half
                b = 4 + half
                for kc in range(7):
                    k.mmq(b, ps[b][:], wo[:, kc, dc * 128:(dc + 1) * 128],
                          attT4[:, kc, :], kc == 0, False, deps=(tt,) + woldt)
                mt[half] = k.mm(b, ps[b][:], wo[:, 7, dc * 128:(dc + 1) * 128],
                                attT4[:, 7, :], False, True, deps=(tt,) + woldt)
            for half in range(2):
                dc = 2 * dcp + half
                b = 4 + half
                prev[("opev", dc)] = k.rd(
                    b, "v", lambda e, bb=ps[b], d=dc: e.tensor_copy(outacc[:, d], bb[:]),
                    deps=(mt[half], prev.get(("oafree", dc))))
            opmm = mt[1]
        prev["attTuse"] = opmm
        xr = k.dma("y", Dm[:, 4096:8192], dr["x2own"][lbo], deps=(prev.get("x2resuse"),), slot="x2r")
        radd = []
        for dc in range(8):
            t = k.op("v", lambda e, d=dc: e.tensor_tensor(
                outacc[:, d], outacc[:, d], x2res[:, d], ALU.add),
                deps=(xr, prev[("opev", dc)]))
            radd.append(t)
        prev["x2resuse"] = radd[-1]
        lnt = ln_ffn(k, outacc, radd, prev, lbo)
        for dc in range(8):
            t = k.op("v", lambda e, d=dc, l=lbo: e.tensor_copy(
                x3[:, d, l * 512:(l + 1) * 512], outacc[:, d]), deps=(lnt[dc],))
            prev[("oafree", dc)] = t
    k.x3_ready = prev[("oafree", 7)]
    k.p.waits("s", [prev[("oafree", dc)] for dc in range(8)])


def l1ffn_phase(k):
    p, dr, ps = k.p, k.dr, k.psum
    A, Bm, C, Dm, E = k.A, k.Bm, k.C, k.Dm, k.E
    prev = {}
    x3 = A.rearrange("p (c l) -> p c l", c=8)
    io = k.indoff
    mx = E[:, io:io + 1]
    mx2 = E[:, io + 1:io + 2]
    mx3 = E[:, io + 2:io + 3]
    srow = E[0:1, io + 4:io + 136]
    scol = {}
    rcol = {}
    for i, nm in enumerate(("s1", "s2", "s3", "s4", "s5")):
        scol[nm] = E[:, io + 140 + 2 * i:io + 141 + 2 * i]
        rcol[nm] = E[:, io + 141 + 2 * i:io + 142 + 2 * i]
    bmax = E[:, io + 152:io + 248]
    bscale = E[:, io + 248:io + 344]
    brecip = E[:, io + 344:io + 440]
    x3q = [C.rearrange("p (c l) -> p c l", c=4), Dm.rearrange("p (c l) -> p c l", c=4)]

    def x3qap(kc):
        return x3q[kc // 4][:, kc % 4, :]

    def global_max_scale(nm, slot, part_tok, qmax, deps):
        d = tuple(deps)
        t = k.op("t", lambda e: e.transpose(ps[6][0:1, 0:128], mx, k.sb["ident_f"]),
                 deps=d + k.bank_deps(6))
        k.bank_clear(6)
        t = k.rd(6, "v", lambda e: e.tensor_reduce(srow[:, 0:1], ps[6][0:1, 0:128], AX,
                                                   ALU.max), deps=(t,))
        t = k.dma("g", dr["ar_in"][:, slot:slot + 1], srow[:, 0:1], deps=(t,), slot="arg")
        t = k.coll("AllReduce", ALU.max, ALLC, dr["ar_in"][:, slot:slot + 1],
                   dr["ar_out"][:, slot:slot + 1], deps=(t,))
        t = k.dma("y", srow[:, 1:2], dr["ar_out"][:, slot:slot + 1], deps=(t,), slot="ary")
        t = k.op("t", lambda e: e.matmul(ps[6][:, 0:1], k.sb["ones_row_f"], srow[:, 1:2],
                                         start=True, stop=True), deps=(t,) + k.bank_deps(6))
        k.bank_clear(6)
        t = k.rd(6, "v", lambda e, n=nm, q=qmax: e.tensor_scalar(
            scol[n], ps[6][:, 0:1], 1.0 / q, 1e-12, ALU.mult, ALU.max), deps=(t,))
        t = k.op("v", lambda e, n=nm: e.reciprocal(rcol[n], scol[n]), deps=(t,))
        return t

    t = None
    for kc in range(8):
        t = k.op("v", lambda e, c=kc: e.tensor_reduce(
            mx2, x3[:, c, :], AX, ALU.max, apply_absolute_value=True),
            deps=(k.x3_ready, t))
        t = k.op("v", lambda e, c=kc: (e.tensor_copy(mx, mx2) if c == 0 else
                                       e.tensor_tensor(mx, mx, mx2, ALU.max)), deps=(t,))
    s1 = global_max_scale("s1", 0, mx, 32767.0, (t,))
    xqt = []
    xqh = C.bitcast(BF16).rearrange("p (c l) -> p c l", c=8)
    xql = Dm.bitcast(BF16).rearrange("p (c l) -> p c l", c=8)
    qt_ = Bm[:, 4096:6144]
    t = None
    for kc in range(8):
        t = k.op("v", lambda e, c=kc: e.tensor_scalar(
            qt_, x3[:, c, :], rcol["s1"], None, ALU.mult), deps=(s1, t))
        t = k.op("v", lambda e: e.tensor_scalar(
            qt_.bitcast(I32)[:, 0:2048], qt_, MAGIC, MAGIC, ALU.add, ALU.subtract),
            deps=(t,))
        t = k.op("v", lambda e: e.tensor_scalar(
            qt_, qt_.bitcast(I32)[:, 0:2048], scol["s1"], None, ALU.mult), deps=(t,))
        t = k.op("v", lambda e, c=kc: e.tensor_copy(xqh[:, c, :], qt_), deps=(t,))
        t = k.op("v", lambda e, c=kc: e.tensor_tensor(xql[:, c, :], qt_, xqh[:, c, :],
                                                      ALU.subtract), deps=(t,))
        xqt.append(t)
    xq_done = tuple(xqt)

    wnat = [Bm[:, i * 1024:(i + 1) * 1024] for i in range(2)]
    wi32 = Bm[:, 2048:3072].bitcast(I32)[:, 0:1024]
    wTh = Bm[:, 3072:3584].bitcast(BF16).rearrange("p (c m) -> p c m", c=8)
    wTl = Bm[:, 3584:4096].bitcast(BF16).rearrange("p (c m) -> p c m", c=8)
    stg = [Bm[:, 4096 + i * 512:4096 + (i + 1) * 512] for i in range(4)]

    def dequant_slab(src_dram, o, wslot, deps):
        t = k.dma("y", wnat[wslot], src_dram[o], deps=deps, slot=f"wn{wslot}")
        t = k.op("v", lambda e, w=wnat[wslot]: e.tensor_reduce(
            bmax[:, 0:32], w.rearrange("p (b q) -> p b q", q=32), AX, ALU.max,
            apply_absolute_value=True), deps=(t, prev.get("bmaxfree")))
        t = k.op("v", lambda e: e.tensor_scalar(bscale[:, 0:32], bmax[:, 0:32], 1.0 / 7.0,
                                                1e-12, ALU.mult, ALU.max), deps=(t,))
        t = k.op("v", lambda e: e.reciprocal(brecip[:, 0:32], bscale[:, 0:32]), deps=(t,))
        t = k.op("v", lambda e, w=wnat[wslot]: e.tensor_tensor(
            Bm[:, 2048:3072].rearrange("p (b q) -> p b q", q=32),
            w.rearrange("p (b q) -> p b q", q=32),
            brecip[:, 0:32].unsqueeze(2).to_broadcast((128, 32, 32)), ALU.mult),
            deps=(t, prev.get("wi32free")))
        t = k.op("v", lambda e: e.tensor_scalar(
            wi32, Bm[:, 2048:3072], MAGIC, MAGIC, ALU.add, ALU.subtract), deps=(t,))
        t = k.op("v", lambda e, w=wnat[wslot]: e.tensor_tensor(
            w.rearrange("p (b q) -> p b q", q=32),
            wi32.rearrange("p (b q) -> p b q", q=32),
            bscale[:, 0:32].unsqueeze(2).to_broadcast((128, 32, 32)), ALU.mult),
            deps=(t,))
        prev["bmaxfree"] = t
        prev["wi32free"] = t
        last = None
        for i in range(8):
            b = i % 2
            bk = ps[b]
            d = (t, prev.get("wTuse")) + k.bank_deps(b)
            k.bank_clear(b)
            tt = k.op("t", lambda e, w=wnat[wslot], i_=i, bb=bk: e.transpose(
                bb[0:128, 0:128], w[:, i_ * 128:(i_ + 1) * 128], k.sb["ident_f"]),
                deps=d)
            t1 = k.rd(b, "v", lambda e, bb=bk, i_=i: e.tensor_copy(
                wTh[:, i_, :], bb[0:128, 0:128]), deps=(tt,))
            last = k.rd(b, "v", lambda e, bb=bk, i_=i: e.tensor_tensor(
                wTl[:, i_, :], bb[0:128, 0:128], wTh[:, i_, :], ALU.subtract),
                deps=(t1,))
        return last

    mmax = {"up": None, "gate": None}
    stok = {}
    for o in range(24):
        for im, (nm, src, dst) in enumerate((("gate", dr["wg_nat"], dr["gatest"]),
                                             ("up", dr["wu_nat"], dr["upst"]))):
            wdq = dequant_slab(src, o, im, deps=(prev.get(("wnatuse", im)),))
            for lb in range(4):
                b = 2 + lb % 2
                sl_ = slice(lb * 512, (lb + 1) * 512)
                mt = None
                for kc in range(8):
                    for wTx, xx in ((wTh, xqh), (wTl, xqh), (wTh, xql)):
                        first = kc == 0 and xx is xqh and wTx is wTh
                        lastm = kc == 7 and wTx is wTh and xx is xql
                        if lastm:
                            mt = k.mm(b, ps[b][:], wTx[:, kc, :], xx[:, kc, sl_],
                                      False, True, deps=(wdq,) + xq_done)
                        else:
                            k.mmq(b, ps[b][:], wTx[:, kc, :], xx[:, kc, sl_],
                                  first, False, deps=(wdq,) + xq_done)
                if lb == 3:
                    prev[("wnatuse", im)] = mt
                    prev["wTuse"] = mt
                s_ = stg[lb]
                t = k.rd(b, "v", lambda e, bb=ps[b], s2=s_: e.tensor_copy(s2, bb[:]),
                         deps=(mt, stok.get(("last", lb))))
                t2 = k.op("v", lambda e, s2=s_: e.tensor_reduce(
                    mx2, s2, AX, ALU.max, apply_absolute_value=True), deps=(t,))
                t2 = k.op("v", lambda e, first=(o == 0 and lb == 0), m=mmax[nm], n=nm:
                          (e.tensor_copy(mx if n == "gate" else mx3, mx2) if first else
                           e.tensor_tensor(mx if n == "gate" else mx3,
                                           mx if n == "gate" else mx3, mx2, ALU.max)),
                          deps=(t2, mmax[nm]))
                mmax[nm] = t2
                stok[(nm, o, lb)] = k.dma("s", dst[o][:, lb * 512:(lb + 1) * 512], s_,
                                          deps=(t,), slot=f"gu{lb}")
                stok[("last", lb)] = stok[(nm, o, lb)]
    gu_done = tuple(stok[(nm, 23, lb)] for nm in ("gate", "up") for lb in range(4))
    s3 = global_max_scale("s3", 2, mx, 32767.0, (mmax["gate"],))
    t = k.op("v", lambda e: e.tensor_copy(mx, mx3), deps=(s3, mmax["up"]))
    s2 = global_max_scale("s2", 1, mx, 32767.0, (t,))

    wdn = C[:, 0:3072]
    wdf = C[:, 3072:6144]
    wdi = wdf.bitcast(I32)[:, 0:3072]
    wdtt = Dm[:, 0:3072]
    xq_war = (prev[("wnatuse", 0)], prev[("wnatuse", 1)])
    for dc in range(8):
        t = k.dma("y", wdn, dr["wd_nat"][dc],
                  deps=(prev.get("wdnatuse"),) + (xq_war if dc == 0 else ()), slot="wd")
        t = k.op("v", lambda e: e.tensor_reduce(
            bmax[:, 0:96], wdn.rearrange("p (b q) -> p b q", q=32), AX, ALU.max,
            apply_absolute_value=True), deps=(t, prev.get("bmaxfree")))
        t = k.op("v", lambda e: e.tensor_scalar(bscale[:, 0:96], bmax[:, 0:96], 1.0 / 7.0,
                                                1e-12, ALU.mult, ALU.max), deps=(t,))
        t = k.op("v", lambda e: e.reciprocal(brecip[:, 0:96], bscale[:, 0:96]), deps=(t,))
        t = k.op("v", lambda e: e.tensor_tensor(
            wdf.rearrange("p (b q) -> p b q", q=32),
            wdn.rearrange("p (b q) -> p b q", q=32),
            brecip[:, 0:96].unsqueeze(2).to_broadcast((128, 96, 32)), ALU.mult),
            deps=(t, prev.get("wdi32free")))
        t = k.op("v", lambda e: e.tensor_scalar(
            wdi, wdf, MAGIC, MAGIC, ALU.add, ALU.subtract), deps=(t,))
        t = k.op("v", lambda e: e.tensor_tensor(
            wdn.rearrange("p (b q) -> p b q", q=32),
            wdi.rearrange("p (b q) -> p b q", q=32),
            bscale[:, 0:96].unsqueeze(2).to_broadcast((128, 96, 32)), ALU.mult), deps=(t,))
        prev["bmaxfree"] = t
        prev["wdi32free"] = t
        wdh = wdtt[:, 0:1536].bitcast(BF16).rearrange("p (c m) -> p c m", c=24)
        wdl = wdtt[:, 1536:3072].bitcast(BF16).rearrange("p (c m) -> p c m", c=24)
        tt = None
        for i in range(24):
            b = i % 2
            bk = ps[b]
            d = (t, prev.get("wdTuse")) + k.bank_deps(b)
            k.bank_clear(b)
            t1 = k.op("t", lambda e, i_=i, bb=bk: e.transpose(
                bb[0:128, 0:128], wdn[:, i_ * 128:(i_ + 1) * 128], k.sb["ident_f"]),
                deps=d)
            t2 = k.rd(b, "v", lambda e, bb=bk, i_=i: e.tensor_copy(
                wdh[:, i_, :], bb[0:128, 0:128]), deps=(t1,))
            tt = k.rd(b, "v", lambda e, bb=bk, i_=i: e.tensor_tensor(
                wdl[:, i_, :], bb[0:128, 0:128], wdh[:, i_, :], ALU.subtract),
                deps=(t2,))
        prev["wdnatuse"] = tt
        prev["wdTuse"] = k.dma("s", dr["wdt_scr"][dc], wdtt, deps=(tt,), slot="wdt")
    wdT_done = prev["wdTuse"]

    g2st = dr["zxb"]
    g2stok = {}
    gmax = None
    gldb = [Bm[:, 10240:12288], Bm[:, 0:2048]]
    sigb = [Bm[:, 6144:8192], Bm[:, 2048:4096]]
    g2b = [Bm[:, 8192:10240], Bm[:, 4096:6144]]
    for o in range(24):
        pa = o % 2
        g, sg, g2_ = gldb[pa], sigb[pa], g2b[pa]
        t = k.dma("y", g, dr["gatest"][o], deps=(gu_done if o < 2 else ()) +
                  (s3, prev.get(("glduse", pa))), slot=f"gld{pa}")
        t = k.op("v", lambda e, w=g: e.tensor_scalar(w, w, rcol["s3"], None, ALU.mult),
                 deps=(t,))
        t = k.op("v", lambda e, w=g: e.tensor_scalar(
            w.bitcast(I32)[:, 0:2048], w, MAGIC, MAGIC, ALU.add, ALU.subtract),
            deps=(t,))
        t = k.op("v", lambda e, w=g: e.tensor_scalar(
            w, w.bitcast(I32)[:, 0:2048], scol["s3"], None, ALU.mult), deps=(t,))
        t = k.op("s", lambda e, w=g, s_=sg: e.activation(out=s_, in_=w, func=AF.Sigmoid),
                 deps=(t, prev.get(("sigfree", pa))))
        t = k.op("v", lambda e, s_=sg: e.tensor_scalar(s_, s_, 65536.0, None, ALU.mult),
                 deps=(t,))
        t = k.op("v", lambda e, s_=sg: e.tensor_scalar(
            s_.bitcast(I32)[:, 0:2048], s_, MAGIC, MAGIC, ALU.add, ALU.subtract),
            deps=(t,))
        t = k.op("v", lambda e, s_=sg: e.tensor_scalar(s_, s_.bitcast(I32)[:, 0:2048],
                                                       float(2.0 ** -16),
                                                       float(65535.0 * 2.0 ** -16),
                                                       ALU.mult, ALU.min), deps=(t,))
        t = k.op("v", lambda e, w=g, s_=sg, g_=g2_: e.tensor_tensor(g_, w, s_, ALU.mult),
                 deps=(t, prev.get(("g2free", pa))))
        prev[("sigfree", pa)] = t
        prev[("glduse", pa)] = t
        g2stok[o] = k.dma("s", g2st[o * 128:(o + 1) * 128, :], g2_, deps=(t,),
                          slot=f"g2s{pa}")
        t2 = k.op("v", lambda e, g_=g2_: e.tensor_reduce(
            mx2, g_, AX, ALU.max, apply_absolute_value=True), deps=(t,))
        gmax = k.op("v", lambda e, first=(o == 0): (
            e.tensor_copy(mx, mx2) if first else e.tensor_tensor(mx, mx, mx2, ALU.max)),
            deps=(t2, gmax))
        prev[("g2free", pa)] = g2stok[o]
    s4 = global_max_scale("s4", 3, mx, 32767.0, (gmax,))

    omax = None
    uldb = [Bm[:, 0:2048], Bm[:, 2048:4096]]
    gldc = [Bm[:, 10240:12288], Bm[:, 8192:10240]]
    opre = [Bm[:, 4096:6144], Bm[:, 6144:8192]]
    ot = {}
    for o in range(24):
        pa = o % 2
        g, u, g2_ = gldc[pa], uldb[pa], opre[pa]
        tg = k.dma("y", g, g2st[o * 128:(o + 1) * 128, :],
                   deps=(s4, g2stok[o], prev.get(("gldcuse", pa))) +
                   ((g2stok[22],) if o == 1 else ()), slot=f"gld{pa}")
        tu = k.dma("y", u, dr["upst"][o],
                   deps=(s4, prev.get(("ulduse", pa))), slot=f"uld{pa}")
        t = k.op("v", lambda e, w=g: e.tensor_scalar(w, w, rcol["s4"], None, ALU.mult),
                 deps=(tg,))
        t = k.op("v", lambda e, w=g: e.tensor_scalar(
            w.bitcast(I32)[:, 0:2048], w, MAGIC, MAGIC, ALU.add, ALU.subtract),
            deps=(t,))
        t = k.op("v", lambda e, w=g: e.tensor_scalar(w, w.bitcast(I32)[:, 0:2048],
                                                     scol["s4"], None, ALU.mult), deps=(t,))
        t2 = k.op("v", lambda e, w=u: e.tensor_scalar(w, w, rcol["s2"], None, ALU.mult),
                  deps=(tu,))
        t2 = k.op("v", lambda e, w=u: e.tensor_scalar(
            w.bitcast(I32)[:, 0:2048], w, MAGIC, MAGIC, ALU.add, ALU.subtract),
            deps=(t2,))
        t2 = k.op("v", lambda e, w=u: e.tensor_scalar(w, w.bitcast(I32)[:, 0:2048],
                                                      scol["s2"], None, ALU.mult),
                  deps=(t2,))
        t = k.op("v", lambda e, w=g, w2=u, g_=g2_: e.tensor_tensor(g_, w, w2, ALU.mult),
                 deps=(t, t2, prev.get(("opfree", pa))) +
                 ((g2stok[23],) if o == 0 else ()))
        prev[("gldcuse", pa)] = t
        prev[("ulduse", pa)] = t
        t2 = k.op("v", lambda e, g_=g2_: e.tensor_reduce(
            mx2, g_, AX, ALU.max, apply_absolute_value=True), deps=(t,))
        omax = k.op("v", lambda e, first=(o == 0): (
            e.tensor_copy(mx, mx2) if first else e.tensor_tensor(mx, mx, mx2, ALU.max)),
            deps=(t2, omax))
        ot[o] = k.dma("s", dr["ost"][o], g2_, deps=(t,), slot=f"ost{pa}")
        prev[("opfree", pa)] = ot[o]
    s5 = global_max_scale("s5", 3, mx, 32767.0, (omax,))

    oq = x3q
    oqv = C.rearrange("p (c l) -> p c l", c=16)
    oqv2 = Dm[:, 0:4096].rearrange("p (c l) -> p c l", c=8)
    k.ln_rows = (Dm[0:1, 4096:4608], Dm[0:1, 4608:5120], Dm[0:1, 5120:5632],
                 Dm[0:1, 5632:6144])
    wdt = [Bm[:, i * 3072:(i + 1) * 3072] for i in range(2)]
    ostage = Bm[:, 6144 + 512:6144 + 512 + 1024]

    def oqap(fi):
        return oqv[:, fi, :] if fi < 16 else oqv2[:, fi - 16, :]

    oqtmp = Bm[:, 7680:8192]
    out_dma = []
    for lb in range(4):
        oqd = []
        for fi in range(24):
            a = oqap(fi)
            ah = a[:, 0:256].bitcast(BF16)
            al = a[:, 256:512].bitcast(BF16)
            t = k.dma("y", a, dr["ost"][fi][:, lb * 512:(lb + 1) * 512],
                      deps=(s5, wdT_done, ot[fi], prev.get(("oquse", fi))),
                      slot=f"oq{fi % 4}")
            t = k.op("v", lambda e, a_=a: e.tensor_scalar(
                oqtmp, a_, rcol["s5"], None, ALU.mult),
                deps=(t, prev.get("oqtfree")) +
                ((ot[23],) if lb == 0 and fi == 0 else ()))
            t = k.op("v", lambda e: e.tensor_scalar(
                oqtmp.bitcast(I32)[:, 0:512], oqtmp, MAGIC, MAGIC, ALU.add,
                ALU.subtract), deps=(t,))
            t = k.op("v", lambda e: e.tensor_scalar(
                oqtmp, oqtmp.bitcast(I32)[:, 0:512], scol["s5"], None, ALU.mult),
                deps=(t,))
            t = k.op("v", lambda e, h=ah: e.tensor_copy(h, oqtmp), deps=(t,))
            t = k.op("v", lambda e, h=ah, l=al: e.tensor_tensor(
                l, oqtmp, h, ALU.subtract), deps=(t,))
            prev["oqtfree"] = t
            oqd.append(t)
        radd = []
        for dc in range(8):
            wldt = k.dma("y", wdt[dc % 2], dr["wdt_scr"][dc],
                         deps=(prev.get(("wdtuse", dc % 2)),) +
                         ((prev.get(("ulduse", 0)), prev.get(("ulduse", 1)),
                           prev.get(("opfree", 0))) if lb == 0 and dc < 2 else ()),
                         slot=f"wdl{dc % 2}")
            wdth = wdt[dc % 2][:, 0:1536].bitcast(BF16).rearrange(
                "p (c m) -> p c m", c=24)
            wdtl = wdt[dc % 2][:, 1536:3072].bitcast(BF16).rearrange(
                "p (c m) -> p c m", c=24)
            b = 2 + dc % 2
            mt = None
            for fi in range(24):
                a = oqap(fi)
                ah = a[:, 0:256].bitcast(BF16)
                al = a[:, 256:512].bitcast(BF16)
                for wx, xx in ((wdth, ah), (wdtl, ah), (wdth, al)):
                    first = fi == 0 and wx is wdth and xx is ah
                    lastm = fi == 23 and wx is wdth and xx is al
                    if lastm:
                        mt = k.mm(b, ps[b][:], wx[:, fi, :], xx, False, True,
                                  deps=(wldt, oqd[fi]))
                    else:
                        k.mmq(b, ps[b][:], wx[:, fi, :], xx, first, False,
                              deps=(wldt, oqd[fi]))
            prev[("wdtuse", dc % 2)] = mt
            t = k.rd(b, "v", lambda e, bb=ps[b], d=dc, l=lb: e.tensor_tensor(
                x3[:, d, l * 512:(l + 1) * 512], x3[:, d, l * 512:(l + 1) * 512], bb[:],
                ALU.add), deps=(mt,))
            radd.append(t)
        for fi in range(24):
            prev[("oquse", fi)] = radd[-1]
        lnt = ln_ffn(k, None, radd, prev, lb,
                     res_fn=lambda dc, l=lb: x3[:, dc, l * 512:(l + 1) * 512])
        for rr in range(4):
            r = lb * 4 + rr
            tt = k.tr_tiles(
                [(x3[:, i, r * 128:(r + 1) * 128], ostage[:, i * 128:(i + 1) * 128],
                  lnt[i]) for i in range(8)], (0, 1), deps=(prev.get("ostuse"),))
            od = k.dma("s", dr["out_own"][r * 128:(r + 1) * 128, :], ostage, deps=(tt,), slot="out")
            prev["ostuse"] = od
            out_dma.append(od)
    k.out_done = out_dma[-1]


_NC_CACHE = {}


def _get_nc():
    if "nc" not in _NC_CACHE:
        _NC_CACHE["nc"] = build_program()
    return _NC_CACHE["nc"]


def kernel(**inputs):
    import ml_dtypes
    ins = {kk: np.asarray(v) for kk, v in inputs.items()}
    nc = _get_nc()

    ident_f = np.eye(128, dtype=np.float32)
    ident_b = np.eye(128, dtype=np.float32)
    sl = np.arange(128)
    tri_f = (sl[:, None] <= sl[None, :]).astype(np.float32)
    mask_b = np.where(sl[:, None] > sl[None, :], np.float32(-1e30), np.float32(0.0))
    ones_row_f = np.ones((1, 128), np.float32)
    ind = np.zeros((32, 16, 128), np.float32)
    for h in range(32):
        for j in range(16):
            ind[h, j, :] = ((2 * j + (np.arange(128) // 64)) == h).astype(np.float32)
    ind = ind.reshape(32, 2048)

    w_in = ins["L0_attn_W_in"]
    w_in_pad = np.zeros((ZXB_PAD, D), np.float32)
    w_in_pad[:4384] = w_in
    w_in_t = np.ascontiguousarray(
        w_in_pad.T.reshape(8, 128, 35, 128).transpose(2, 1, 0, 3).reshape(35, 128, 1024))
    w_out_t = np.ascontiguousarray(ins["L0_attn_W_out"].T.reshape(16, 128, D))
    conv_w = np.zeros((18 * 128, DCONV), np.float32)
    conv_w[:2304] = ins["L0_attn_conv_w"][:, 0, :]
    conv_w = np.ascontiguousarray(
        conv_w.reshape(18, 128, DCONV).transpose(1, 0, 2).reshape(128, 72))
    conv_b = np.zeros((18 * 128,), np.float32)
    conv_b[:2304] = ins["L0_attn_conv_b"]
    conv_b = np.ascontiguousarray(conv_b.reshape(18, 128).T)
    d_exp = np.zeros((128, 16), np.float32)
    for j in range(16):
        d_exp[:, j] = ins["L0_attn_D_param"][2 * j + np.arange(128) // 64]
    normw_col = np.ascontiguousarray(ins["L0_attn_norm_w"].reshape(16, 128).T)

    w1 = np.ascontiguousarray(
        ins["L0_ffn_W1"].reshape(8, 128, 32, 128).transpose(2, 1, 0, 3).reshape(32, 128, 1024))
    w2 = np.ascontiguousarray(
        ins["L0_ffn_W2"].reshape(2, 16, 128, 8, 128).transpose(0, 3, 2, 1, 4)
        .reshape(2, 8, 128, 2048))
    b1_col = np.ascontiguousarray(ins["L0_ffn_b1"].reshape(32, 128).T)
    qp_col = np.ascontiguousarray(ins["L0_ffn_quantum_phases"].reshape(32, 128).T)
    qr_col = np.ascontiguousarray(ins["L0_ffn_quantum_weights_real"].reshape(32, 128).T)
    qi_col = np.ascontiguousarray(ins["L0_ffn_quantum_weights_imag"].reshape(32, 128).T)
    b2_col = np.ascontiguousarray(ins["L0_ffn_b2"].reshape(8, 128).T)

    wq_t_full = ins["L1_attn_wq"].T
    wk_t_full = ins["L1_attn_wk"].T
    wv_t_full = ins["L1_attn_wv"].T
    wo_t = np.ascontiguousarray(ins["L1_attn_wo"].T.reshape(8, 128, D))
    wg_nat = np.ascontiguousarray(ins["L1_ffn_w_gate"].reshape(24, 128, D))
    wu_nat = np.ascontiguousarray(ins["L1_ffn_w_up"].reshape(24, 128, D))
    wd_nat = np.ascontiguousarray(ins["L1_ffn_w_down"].reshape(8, 128, FI))
    u = ins["L0_attn_u"]

    def qkv_tile(w_t, half):
        s = w_t[:, half * 512:(half + 1) * 512]
        a = s.reshape(8, 128, 4, 128).transpose(2, 1, 0, 3)
        return np.ascontiguousarray(a.reshape(4, 128, 1024))

    in_maps = []
    for c in range(8):
        b, half = c // 2, c % 2
        m = dict(
            u_own=np.ascontiguousarray(u[b, half * HALF:(half + 1) * HALF, :]),
            w_in_t=w_in_t, w_out_t=w_out_t, conv_w=conv_w, conv_b=conv_b,
            dt_bias=np.ascontiguousarray(ins["L0_attn_dt_bias"].reshape(NH, 1)),
            a_log=np.ascontiguousarray(ins["L0_attn_A_log"].reshape(NH, 1)),
            d_exp=d_exp, normw_col=normw_col,
            w1=w1, w2=w2, b1_col=b1_col, qp_col=qp_col, qr_col=qr_col, qi_col=qi_col,
            b2_col=b2_col,
            wq_t=qkv_tile(wq_t_full, half), wk_t=qkv_tile(wk_t_full, half),
            wv_t=qkv_tile(wv_t_full, half),
            wo_t=wo_t, wg_nat=wg_nat, wu_nat=wu_nat, wd_nat=wd_nat,
            ident_f=ident_f, ident_b=ident_b, tri_f=tri_f, mask_b=mask_b,
            ones_row_f=ones_row_f,
            is_odd=np.full((128, 1), float(half), np.float32),
            ind_hp=ind,
        )
        in_maps.append(m)

    res = run_bass_kernel_spmd(nc, in_maps, core_ids=list(range(8)))
    out = np.zeros((B, L, D), np.float32)
    for c in range(8):
        b, half = c // 2, c % 2
        out[b, half * HALF:(half + 1) * HALF, :] = res.results[c]["out_own"]
    return out

